# version 06
# baseline (speedup 1.0000x reference)
import numpy as np
import ml_dtypes
from contextlib import ExitStack

import concourse.bass as bass
import concourse.mybir as mybir
from concourse.bass_utils import run_bass_kernel_spmd

F32 = mybir.dt.float32
BF16 = mybir.dt.bfloat16
AF = mybir.ActivationFunctionType
ALU = mybir.AluOpType
AX = mybir.AxisListType

B, S, D = 16, 2048, 1024
NSEQ = 2
NCORE = 8
DCH = D // 128
NTB = S // 512
WIN = 512
HL, HG, DH = 8, 16, 64
HID, HIDP = 2730, 2816
NHC = HIDP // 128
QKV, INNER = 1536, 512
EPS = 1e-5
NEG = -1.0e30

ALL_STAGES = ("ffn1", "attn", "ffn2")
DIAG_SERIAL_WQS = False
DIAG_HCAST_DVE = False
DIAG_FENCE_QKV = False
DIAG_SKIP_T = ()
DEBUG_TAPS = ()
INNER_LAYERS = 2
SKIP_INNER_FFN = False
SKIP_INNER_ATTN = False


class Ctr:
    __slots__ = ("h", "v", "owner", "step")

    def __init__(self, h, owner, step=1):
        self.h, self.v, self.owner, self.step = h, 0, owner, step

    def bump(self):
        self.v += self.step
        return (self.h, self.step)

    def at(self):
        return (self, self.v)


class Rec:
    def __init__(self):
        self.items = []

    def op(self, eng, fn, wait=(), inc=(), hwait=()):
        self.items.append((eng, tuple(wait), fn, tuple(inc), tuple(hwait)))

    def replay(self, block):
        for name in ("tensor", "vector", "scalar", "gpsimd", "sync"):
            items = [(w, f, i, hw) for (e, w, f, i, hw) in self.items if e == name]

            def emit(eng, items=items, name=name):
                seen = {}
                for (waits, fn, incs, hwaits) in items:
                    for (ctr, val) in waits:
                        if ctr.owner == name or val <= 0:
                            continue
                        if seen.get(id(ctr), -1) >= val:
                            continue
                        seen[id(ctr)] = val
                        eng.wait_ge(ctr.h, val)
                    for (ctr, val) in hwaits:
                        if val <= 0 or seen.get(id(ctr), -1) >= val:
                            continue
                        seen[id(ctr)] = val
                        eng.wait_ge(ctr.h, val)
                    ins = fn(eng)
                    for (h, n) in incs:
                        ins = ins.then_inc(h, n)

            getattr(block, name)(emit)


class Bank:
    __slots__ = ("t", "ready", "marks", "held")

    def __init__(self, t):
        self.t = t
        self.ready = ()
        self.marks = []
        self.held = False

    def begin_fill(self):
        w = tuple(self.marks)
        self.marks = []
        self.held = True
        return w

    def release(self, ctrval):
        self.marks.append(ctrval)
        self.held = False


class Ctx:
    pass


def _ts(i, n=128):
    return slice(i * n, (i + 1) * n)


_MROW = {"mu": 0, "rstd": 1, "var": 1, "tmp": 2, "sc2": 2, "rr": 0}


def mrow(c, name):
    i = _MROW[name]
    return c.MISC[0:1, i * 512:(i + 1) * 512]


def _rotary_tables():
    f32 = np.float32
    half = np.arange(0, DH, 2, dtype=f32)
    inv_freq = (f32(1.0) / np.power(f32(10000.0), half / f32(DH))).astype(f32)
    t = np.arange(S, dtype=f32)
    freqs = (t[:, None] * inv_freq[None, :]).astype(f32)
    freqs = np.concatenate([freqs, freqs], axis=-1)
    base = ((half + f32(0.4) * f32(DH)) / (f32(1.4) * f32(DH))).astype(f32)
    power = ((t - f32(S // 2)) / f32(WIN // 2)).astype(f32)
    scale = np.power(base[None, :], power[:, None]).astype(f32)
    scale = np.concatenate([scale, scale], axis=-1)
    cos = np.cos(freqs).astype(f32)
    sin = np.sin(freqs).astype(f32)
    return np.stack([(cos * scale).astype(f32), (sin * scale).astype(f32),
                     (cos / scale).astype(f32), (sin / scale).astype(f32)], axis=0)


def host_prep(inputs):
    f32 = np.float32
    g = {}
    g["wq"] = np.ascontiguousarray(np.asarray(inputs["L0_attn_W_Q"], f32).T)
    g["wk"] = np.ascontiguousarray(np.asarray(inputs["L0_attn_W_K"], f32).T)
    g["wv"] = np.ascontiguousarray(np.asarray(inputs["L0_attn_W_V"], f32).T)
    g["wog"] = np.ascontiguousarray(np.asarray(inputs["L0_attn_W_O"], f32).T)
    g["wqkv"] = np.ascontiguousarray(np.asarray(inputs["L0_ffn_wqkv"], f32))
    g["wol"] = np.ascontiguousarray(np.asarray(inputs["L0_ffn_wo"], f32))
    w1 = np.asarray(inputs["L0_ffn_w1"], f32)
    w1p = np.zeros((2, D, 2 * HIDP), dtype=f32)
    w1p[:, :, 0:HID] = w1[:, :, 0:HID]
    w1p[:, :, HIDP:HIDP + HID] = w1[:, :, HID:2 * HID]
    g["w1"] = w1p
    w2 = np.asarray(inputs["L0_ffn_w2"], f32)
    w2p = np.zeros((2, HIDP, D), dtype=f32)
    w2p[:, 0:HID, :] = w2
    g["w2"] = w2p
    b1 = np.asarray(inputs["L0_ffn_b1"], f32)
    b1p = np.zeros((2, 2 * HIDP), dtype=f32)
    b1p[:, 0:HID] = b1[:, 0:HID]
    b1p[:, HIDP:HIDP + HID] = b1[:, HID:2 * HID]
    g["b1c"] = np.ascontiguousarray(b1p.reshape(2, 2 * NHC, 128).transpose(2, 0, 1))
    sm = np.zeros((128, 2, 3, DCH), dtype=f32)
    for l in range(2):
        sm[:, l, 0, :] = np.asarray(inputs["L0_ffn_ln_g"][l], f32).reshape(DCH, 128).T
        sm[:, l, 1, :] = np.asarray(inputs["L0_ffn_ln_b"][l], f32).reshape(DCH, 128).T
        sm[:, l, 2, :] = np.asarray(inputs["L0_ffn_b2"][l], f32).reshape(DCH, 128).T
    g["smalls"] = np.ascontiguousarray(sm.reshape(128, 2 * 3 * DCH))
    qsks = np.stack([np.asarray(inputs["L0_ffn_q_scale"], f32),
                     np.asarray(inputs["L0_ffn_k_scale"], f32)], axis=1)
    g["qsks"] = np.ascontiguousarray(qsks.reshape(1, 2 * 2 * DH))
    g["tabs"] = np.ascontiguousarray(_rotary_tables())

    x = np.asarray(inputs["L0_attn_x"], f32)
    xs = []
    for cid in range(NCORE):
        xc = x[2 * cid:2 * cid + 2]
        xs.append(np.ascontiguousarray(xc.transpose(2, 0, 1).reshape(D, NSEQ * S)))
    return g, xs


def bank(c, n=1):
    out = []
    for _ in range(n):
        for _try in range(len(c.banks) + 1):
            b = c.banks[c.rr % len(c.banks)]
            c.rr += 1
            if not b.held:
                b.held = True
                break
        else:
            raise RuntimeError("all psum banks held")
        out.append(b)
    return out if n > 1 else out[0]


def tbank(c):
    b = c.tbk[c.trr % len(c.tbk)]
    c.trr += 1
    return b


def tb_wait(c):
    return tuple(c.tb_marks.values())


def tb_mark(c, ctrval):
    c.tb_marks[id(ctrval[0])] = ctrval


def mm_fill(c, b, parts, out=None):
    R = c.R
    w0 = b.begin_fill()
    n = len(parts)
    for i, p in enumerate(parts):
        if len(p) == 3:
            l, r, wop = p
            o = b.t if out is None else out
        else:
            o, l, r, wop = p
        st, sp = (i == 0), (i == n - 1)
        inc = (c.pe.bump(),) if sp else ()
        R.op("tensor",
             lambda t, o=o, l=l, r=r, st=st, sp=sp: t.matmul(o, l, r, start=st, stop=sp),
             wait=tuple(wop) + (w0 if i == 0 else ()), inc=inc)
    b.ready = (c.pe.at(),)


def ev_op(c, eng, fn, waits=(), banks=(), hard=False):
    ctr = {"vector": c.dv, "scalar": c.ac, "gpsimd": c.po, "tensor": c.pe}[eng]
    allw = tuple(waits)
    for b in banks:
        allw = allw + tuple(b.ready)
    hw = ((ctr, ctr.v),) if hard else ()
    c.R.op(eng, fn, wait=allw, inc=(ctr.bump(),), hwait=hw)
    for b in banks:
        b.release(ctr.at())
    return (ctr.at(),)


def fence(c):
    marks = (c.pe.at(), c.dv.at(), c.ac.at(), c.po.at(), c.pld.at(), c.spd.at())
    c.R.op("tensor", lambda t: t.nop(), wait=marks)
    c.R.op("vector", lambda v: v.engine_nop(), wait=marks)
    c.R.op("scalar", lambda sc: sc.nop(), wait=marks)
    c.R.op("gpsimd", lambda g: g.engine_nop(), wait=marks)
    c.R.op("sync", lambda sy: sy.nop(), wait=marks)


def dsem(c, key):
    if key not in c.dsems:
        h = c.stack.enter_context(c.nc.semaphore(f"ds_{key}"))
        c.dsems[key] = Ctr(h, None, 16)
    return c.dsems[key]


def pool_dma(c, out_ap, in_ap, wait=(), sem=None):
    ctr = sem if sem is not None else c.pld
    c.R.op("gpsimd",
           lambda g, o=out_ap, i=in_ap: g.dma_start(out=o, in_=i),
           wait=tuple(wait), inc=(ctr.bump(),))
    return (ctr.at(),)


def _tap_psum(c, name, bb):
    if name not in DEBUG_TAPS or name in c.taps:
        return
    lnt = c.LNT
    w = ev_op(c, "vector", lambda v, lnt=lnt: v.tensor_copy(lnt[:, 0:512], bb.t[:]),
              waits=tuple(bb.ready))
    _tap(c, name, lnt[:, 0:512], w)


def _tap(c, name, ap, wait):
    if name not in DEBUG_TAPS or name in c.taps:
        return
    c.taps[name] = True
    shape = list(ap.shape)
    d = c.nc.dram_tensor(f"dbg_{name}", shape, ap.dtype, kind="ExternalOutput").ap()
    sync_dma(c, d, ap, wait=wait)
    fence(c)


def _tap_async(c, name, ap, wait):
    if name not in DEBUG_TAPS or name in c.taps:
        return
    c.taps[name] = True
    shape = list(ap.shape)
    d = c.nc.dram_tensor(f"dbg_{name}", shape, ap.dtype, kind="ExternalOutput").ap()
    sync_dma(c, d, ap, wait=wait)


def sync_dma(c, out_ap, in_ap, wait=(), sem=None):
    ctr = sem if sem is not None else c.spd
    c.R.op("sync",
           lambda sy, o=out_ap, i=in_ap: sy.dma_start(out=o, in_=i),
           wait=tuple(wait), inc=(ctr.bump(),))
    return (ctr.at(),)


def build_nc(stages):
    nc = bass.Bass("TRN2", target_bir_lowering=False, debug=False)
    c = Ctx()
    c.nc = nc
    c.R = Rec()

    c.d = {}
    for name, shape in [("xT", [D, NSEQ * S]), ("wq", [D, D]), ("wk", [D, D]),
                        ("wv", [D, D]), ("wog", [D, D]), ("wqkv", [2, D, QKV]),
                        ("wol", [2, INNER, D]), ("w1", [2, D, 2 * HIDP]),
                        ("w2", [2, HIDP, D]), ("b1c", [128, 2, 2 * NHC]),
                        ("smalls", [128, 2 * 3 * DCH]), ("qsks", [1, 2 * 2 * DH]),
                        ("tabs", [4, S, DH])]:
        c.d[name] = nc.dram_tensor(name, shape, F32, kind="ExternalInput").ap()
    c.d["outT"] = nc.dram_tensor("outT", [D, NSEQ * S], F32, kind="ExternalOutput").ap()

    with ExitStack() as stack:
        c.stack = stack
        e = stack.enter_context
        c.pe = Ctr(e(nc.semaphore("c_pe")), "tensor")
        c.dv = Ctr(e(nc.semaphore("c_dv")), "vector")
        c.ac = Ctr(e(nc.semaphore("c_ac")), "scalar")
        c.po = Ctr(e(nc.semaphore("c_po")), "gpsimd")
        c.pld = Ctr(e(nc.semaphore("c_pld")), None, 16)
        c.spd = Ctr(e(nc.semaphore("c_spd")), None, 16)
        c.dsems = {}

        c.banks = [Bank(e(nc.psum_tensor(f"ps{i}", [128, 512], F32))[:, :]) for i in range(7)]
        c.rr = 0
        tb = e(nc.psum_tensor("pstr", [128, 8, 128], BF16))
        c.tbk = [Bank(tb[:, i, :]) for i in range(8)]
        c.tbv = tb
        c.trr = 0
        c.tb_marks = {}

        c.XR = e(nc.sbuf_tensor("XR", [128, DCH, S], F32))
        c.MUP = e(nc.sbuf_tensor("MUP", [128, 128], BF16))
        c.MLO = e(nc.sbuf_tensor("MLO", [128, 128], BF16))
        c.ID = e(nc.sbuf_tensor("ID", [128, 128], BF16))
        c.ONES = e(nc.sbuf_tensor("ONESC", [128, 2], BF16))
        c.REB = e(nc.sbuf_tensor("REB", [128, 1, 128], BF16))
        c.RE64 = e(nc.sbuf_tensor("RE64", [128, 64], BF16))
        c.SM = e(nc.sbuf_tensor("SM", [128, 2 * 3 * DCH], F32))
        c.B1C = e(nc.sbuf_tensor("B1C", [128, 2, 2 * NHC], F32))
        c.QS = e(nc.sbuf_tensor("QS", [1, 2 * 2 * DH], BF16))
        c.QSW = e(nc.sbuf_tensor("QSW", [1, 2 * 2 * DH], BF16))
        c.MISC = e(nc.sbuf_tensor("MISC", [1, 1536], F32))
        c.RZ = e(nc.sbuf_tensor("RZ", [128, 2, 512], BF16))
        c.RZC = e(nc.sbuf_tensor("RZC", [128, 512], BF16))
        c.QSB = e(nc.sbuf_tensor("QSB", [128, 4, DH], F32))
        c.RO8 = e(nc.sbuf_tensor("RO8", [1, 128], BF16))
        c.RO1 = e(nc.sbuf_tensor("RO1", [1, 128], BF16))
        c.HLS = e(nc.sbuf_tensor("HLS", [128, 2], BF16))
        c.HLSF = e(nc.sbuf_tensor("HLSF", [128, 1], F32))
        c.MXS = e(nc.sbuf_tensor("MXS", [128, 24], F32))
        c.NMB = e(nc.sbuf_tensor("NMB", [128, 4], BF16))
        c.taps = {}
        c.tab_layer = -1
        c.rz_guard = ()
        c.xbb_guard = ()
        c.bt_guard = ()
        c.hls_guard = ()
        c.stg_guard = ()
        c.pt_guard = [(), (), (), ()]
        c.pt_rr = 0
        c.ol_guard = ()
        c.gl_layer = -1

        _init_consts(c)
        for s in range(NSEQ):
            _load_x(c, s)
            if "ffn1" in stages:
                _ffn_macaron(c, s, 0)
            if "attn" in stages:
                _global_attn(c, s)
            if "ffn2" in stages:
                _ffn_macaron(c, s, 1)
            _store_x(c, s)
        c.R.op("sync", lambda e_: e_.nop(),
               wait=(c.spd.at(), dsem(c, "sto").at()))

        with nc.Block() as block:
            c.R.replay(block)
    return nc


def _init_consts(c):
    R = c.R
    c.w_sm = sync_dma(c, c.SM[:], c.d["smalls"], sem=dsem(c, "ini"))
    c.w_b1c = sync_dma(c, c.B1C[:], c.d["b1c"], sem=dsem(c, "ini"))
    c.w_qs = pool_dma(c, c.QS[:], c.d["qsks"], sem=dsem(c, "inip"))

    def g_init(g):
        g.memset(c.MUP[:], 0.0)
        g.affine_select(out=c.MUP[:], in_=c.MUP[:], compare_op=ALU.is_ge,
                        fill=NEG, base=0, pattern=[[-1, 128]], channel_multiplier=1)
        g.memset(c.MLO[:], 0.0)
        g.affine_select(out=c.MLO[:], in_=c.MLO[:], compare_op=ALU.is_ge,
                        fill=NEG, base=0, pattern=[[1, 128]], channel_multiplier=-1)
        g.memset(c.ID[:], 0.0)
        g.affine_select(out=c.ID[:], in_=c.ID[:], compare_op=ALU.not_equal,
                        fill=1.0, base=0, pattern=[[-1, 128]], channel_multiplier=1)
        g.memset(c.ONES[:], 1.0)
        g.memset(c.REB[:], 0.0)
        g.memset(c.REB[0:1, :, :], 1.0)
        g.memset(c.REB[32:33, :, :], 1.0)
        g.memset(c.RE64[:], 0.0)
        g.memset(c.RE64[0:1, :], 1.0)
        g.memset(c.RE64[32:33, :], 1.0)
        g.memset(c.RZ[:], 0.0)
        g.memset(c.RZC[:], 0.0)
        g.memset(c.RZC[0:1, :], 1.0)
        g.memset(c.RZC[32:33, :], 1.0)
        g.memset(c.RO8[:], 8.0)
        return g.memset(c.RO1[:], 1.0)

    R.op("gpsimd", g_init, inc=(c.po.bump(),))
    c.w_init = (c.po.at(),)

    def qsw(v):
        ins = None
        for i in range(4):
            o = i * DH
            v.tensor_scalar_mul(c.QSW[0:1, o:o + 32], c.QS[0:1, o + 32:o + 64], -1.0)
            ins = v.tensor_copy(c.QSW[0:1, o + 32:o + 64], c.QS[0:1, o:o + 32])
        return ins

    R.op("vector", qsw, wait=c.w_qs, inc=(c.dv.bump(),))
    c.w_qsw = (c.dv.at(),)


def _load_x(c, s):
    w = (dsem(c, "sto").at(),) if s > 0 else ()
    c.w_x = sync_dma(
        c, c.XR[:],
        c.d["xT"].rearrange("(c p) n -> p c n", p=128)[:, :, s * S:(s + 1) * S],
        wait=w, sem=dsem(c, "xld"))


def _store_x(c, s):
    sync_dma(c,
             c.d["outT"].rearrange("(c p) n -> p c n", p=128)[:, :, s * S:(s + 1) * S],
             c.XR[:], wait=(c.dv.at(),), sem=dsem(c, "sto"))


def _stat_block(c, get_src, j, src_wait):
    R = c.R
    mu, rstd, var, tmp, sc2 = (mrow(c, k) for k in ("mu", "rstd", "var", "tmp", "sc2"))
    bs1, bs2 = bank(c, 2)
    w1_ = bs1.begin_fill()
    w2_ = bs2.begin_fill()
    slot_mm = [(), ()]
    for d in range(DCH):
        sl = d % 2
        src = get_src(d, j)
        xb = c.BSCR[:, sl * 512:(sl + 1) * 512]
        sq = c.BSCR[:, 1024 + sl * 512:1024 + (sl + 1) * 512]
        wc = ev_op(c, "gpsimd",
                   lambda g, o=xb, i=src: g.tensor_copy(o, i),
                   waits=tuple(src_wait) + c.bscr_guard + slot_mm[sl])
        ws = ev_op(c, "scalar",
                   lambda sc_, o=sq, i=src: sc_.activation(o, i, AF.Square),
                   waits=tuple(src_wait) + c.bscr_guard + slot_mm[sl])
        inc1 = (c.pe.bump(),)
        R.op("tensor",
             lambda t, o=bs1.t[0:1, :], xb=xb, st=(d == 0), sp=(d == DCH - 1):
                 t.matmul(o, c.ONES[:, 0:1], xb, start=st, stop=sp),
             wait=wc[0:1] + (w1_ if d == 0 else ()), inc=inc1)
        inc2 = (c.pe.bump(),)
        R.op("tensor",
             lambda t, o=bs2.t[0:1, :], sq=sq, st=(d == 0), sp=(d == DCH - 1):
                 t.matmul(o, c.ONES[:, 0:1], sq, start=st, stop=sp),
             wait=ws[0:1] + (w2_ if d == 0 else ()), inc=inc2)
        slot_mm[sl] = (c.pe.at(),)
    bs1.ready = (c.pe.at(),)
    bs2.ready = (c.pe.at(),)
    c.bscr_guard = (c.pe.at(),)
    wser = (c.ac.at(),) + c.rz_guard
    ev_op(c, "vector", lambda v: v.tensor_scalar_mul(mu, bs1.t[0:1, :], 1.0 / D),
          waits=wser, banks=(bs1,))
    ev_op(c, "vector", lambda v: v.tensor_scalar_mul(var, bs2.t[0:1, :], 1.0 / D),
          banks=(bs2,))
    ev_op(c, "vector", lambda v: v.tensor_mul(tmp, mu, mu), hard=True)
    ev_op(c, "vector", lambda v: v.tensor_sub(var, var, tmp), hard=True)
    wv = (c.dv.at(),)
    ev_op(c, "scalar", lambda sc: sc.activation(rstd, var, AF.Ln, bias=np.float32(EPS)),
          waits=wv)
    wact = ev_op(c, "scalar", lambda sc: sc.activation(rstd, rstd, AF.Exp,
                                                       scale=np.float32(-0.5)), hard=True)
    ev_op(c, "vector", lambda v: v.tensor_scalar_mul(mu, mu, -1.0), hard=True)
    ev_op(c, "vector", lambda v: v.tensor_mul(mu, mu, rstd), waits=wact, hard=True)

    rz = c.RZ
    ev_op(c, "vector", lambda v: v.tensor_copy(rz[0:1, 0, :], rstd), hard=True)
    ev_op(c, "vector", lambda v: v.tensor_sub(sc2, rstd, rz[0:1, 0, :]), hard=True)
    ev_op(c, "vector", lambda v: v.tensor_copy(rz[32:33, 0, :], sc2), hard=True)
    ev_op(c, "vector", lambda v: v.tensor_copy(rz[64:65, 0, :], rz[0:1, 0, :]), hard=True)
    ev_op(c, "vector", lambda v: v.tensor_copy(rz[0:1, 1, :], mu), hard=True)
    ev_op(c, "vector", lambda v: v.tensor_sub(sc2, mu, rz[0:1, 1, :]), hard=True)
    ev_op(c, "vector", lambda v: v.tensor_copy(rz[32:33, 1, :], sc2), hard=True)
    rzw = ev_op(c, "vector", lambda v: v.tensor_copy(rz[64:65, 1, :], rz[0:1, 1, :]),
                hard=True)
    if getattr(c, "tap_rz", False):
        c.tap_rz = False
        _tap(c, "rzin", c.RZ[:, :, :], rzw)
        _tap(c, "msc", c.MISC[:, :], rzw)
    stag = getattr(c, "stat_tag", "")
    _tap(c, f"st_misc_{stag}_{j}", c.MISC[:, :], rzw)
    _tap(c, f"st_rz_{stag}_{j}", c.RZ[:, :, :], rzw)
    return rzw


def _ln_apply_block(c, get_src, get_dst, j, rz_wait, glA_fn, glB_fn, src_wait=(),
                    mm_wait=()):
    for d in range(DCH):
        ba, bb = bank(c, 2)
        mm_fill(c, ba, [(glA_fn(d), c.RZ[:, 0, :], tuple(rz_wait) + tuple(mm_wait))])
        gb = glB_fn(d)
        partsB = [(glA_fn(d), c.RZ[:, 1, :], tuple(rz_wait) + tuple(mm_wait))]
        if gb is not None:
            partsB.append((gb, c.RZC[:], c.w_init))
        mm_fill(c, bb, partsB)
        if gb is not None and d == 0:
            _tap_psum(c, "brep", bb)
            _tap_psum(c, "arep", ba)
        src = get_src(d, j)
        lnt = c.LNT
        ev_op(c, "vector",
              lambda v, src=src, ba=ba, lnt=lnt: v.tensor_mul(lnt[:], src, ba.t[:]),
              waits=src_wait, banks=(ba,))
        dst = get_dst(d, j)
        ev_op(c, "vector",
              lambda v, dst=dst, bb=bb, lnt=lnt: v.tensor_add(dst, lnt[:], bb.t[:]),
              banks=(bb,))
    c.rz_guard = (c.pe.at(),)


def _ln(c, get_src, get_dst, src_wait, glA_fn=None, glB_fn=None, blocks=(0, NTB),
        mm_wait=()):
    if glA_fn is None:
        glA_fn = lambda d: c.REB[:, 0, :]
        glB_fn = lambda d: None
        mm_wait = tuple(mm_wait) + c.w_init
    for j in range(blocks[0], blocks[1]):
        rzw = _stat_block(c, get_src, j, src_wait)
        _ln_apply_block(c, get_src, get_dst, j, rzw, glA_fn, glB_fn, src_wait, mm_wait)


def _ffn_macaron(c, s, apply_idx):
    nc = c.nc
    with ExitStack() as ph:
        e = ph.enter_context
        tag = f"{s}_{apply_idx}"
        YR = e(nc.sbuf_tensor(f"YR_{tag}", [128, DCH, S + 1], F32))
        c.LNT = e(nc.sbuf_tensor(f"LNT_{tag}", [128, 512], F32))
        c.BSCR = e(nc.sbuf_tensor(f"BSCR_{tag}", [128, 2048], BF16))
        c.bscr_guard = ()
        c.stat_tag = f"f{tag}"
        wg = ev_op(c, "gpsimd", lambda g: g.memset(YR[:, :, 0:1], 0.0),
                   waits=c.w_init)
        _tap(c, f"xin_{tag}", c.XR[:, :, :], (c.dv.at(),))
        _ln(c,
            get_src=lambda d, j: c.XR[:, d, _ts(j, 512)],
            get_dst=lambda d, j: YR[:, d, 1 + j * 512:1 + (j + 1) * 512],
            src_wait=c.w_x)
        _tap(c, f"yr1_{tag}", YR[:, :, :], (c.dv.at(),))
        for l in range(INNER_LAYERS):
            _inner_layer(c, s, apply_idx, l, YR, wg)
        _tap(c, f"yr2_{tag}", YR[:, :, :], (c.dv.at(),))
        for d in range(DCH):
            for j in range(NTB):
                ev_op(c, "vector",
                      lambda v, d=d, j=j: v.scalar_tensor_tensor(
                          c.XR[:, d, _ts(j, 512)],
                          YR[:, d, 1 + j * 512:1 + (j + 1) * 512],
                          0.5, c.XR[:, d, _ts(j, 512)],
                          op0=ALU.mult, op1=ALU.add),
                      waits=c.w_x)
        fence(c)


def _col_to_rows(c, colap, wcol, dst_list):
    hi = c.HLS[:, 0:1]
    lo = c.HLS[:, 1:2]
    f32s = c.HLSF[:, 0:1]
    ev_op(c, "vector", lambda v: v.tensor_copy(hi, colap),
          waits=tuple(wcol) + c.hls_guard)
    ev_op(c, "vector", lambda v: v.tensor_sub(f32s, colap, hi), hard=True)
    ev_op(c, "vector", lambda v: v.tensor_copy(lo, f32s), hard=True)
    whl = (c.dv.at(),)
    outs = {}
    for name, src in (("hi", hi), ("lo", lo)):
        bb = bank(c)
        w0 = bb.begin_fill()
        c.R.op("tensor",
               lambda t, src=src, bb=bb: t.matmul(bb.t[0:1, 0:128], src, c.ID[:],
                                                  start=True, stop=True),
               wait=whl + w0 + c.w_init, inc=(c.pe.bump(),))
        bb.ready = (c.pe.at(),)
        outs[name] = bb
    for (dst, use_lo) in dst_list:
        bb = outs["lo"] if use_lo else outs["hi"]
        ev_op(c, "vector",
              lambda v, dst=dst, bb=bb: v.tensor_copy(dst, bb.t[0:1, 0:128]),
              waits=tuple(bb.ready))
    outs["hi"].release((c.dv, c.dv.v))
    outs["lo"].release((c.dv, c.dv.v))
    c.hls_guard = (c.dv.at(),)


def _build_tables(c, l):
    QSB = c.QSB
    srcs = [(c.QS, l * 2 + 0, c.RO8), (c.QSW, l * 2 + 0, c.RO8),
            (c.QS, l * 2 + 1, c.RO1), (c.QSW, l * 2 + 1, c.RO1)]
    for i, (rowt, idx, lco) in enumerate(srcs):
        bb = bank(c)
        mm_fill(c, bb, [(bb.t[0:128, 0:64], lco[0:1, :],
                         rowt[0:1, idx * DH:(idx + 1) * DH], c.w_qsw + c.w_init)])
        ev_op(c, "vector",
              lambda v, i=i, bb=bb: v.tensor_copy(QSB[:, i, :], bb.t[:, 0:64]),
              banks=(bb,))
    wq = (c.dv.at(),)
    for t4 in range(4):
        wld = sync_dma(c, c.BT[:],
                       c.d["tabs"][t4].rearrange("(tt p) d -> p tt d", p=128),
                       wait=c.bt_guard, sem=dsem(c, "bt"))
        tab_, bt_ = c.TAB, c.BT
        ev_op(c, "vector",
              lambda v, t4=t4, tab_=tab_, bt_=bt_: v.tensor_mul(
                  tab_[:, t4, :, :], bt_[:, :, :],
                  c.QSB[:, t4, :].unsqueeze(1).broadcast_to((128, S // 128, DH))),
              waits=wld + wq, hard=True)
        c.bt_guard = (c.dv.at(),)
    c.w_tab = (c.dv.at(),)


def _build_gl(c, l):
    glw = (c.pe.at(),)
    def z(g):
        g.memset(c.GLA[:], 0.0)
        return g.memset(c.GLB[:], 0.0)
    wz = ev_op(c, "gpsimd", z, waits=glw)
    glw = glw + wz
    for d in range(DCH):
        gcol = c.SM[:, l * 24 + d:l * 24 + d + 1]
        bcol = c.SM[:, l * 24 + 8 + d:l * 24 + 8 + d + 1]
        _col_to_rows(c, gcol, c.w_sm + glw,
                     [(c.GLA[0:1, d, :], False), (c.GLA[32:33, d, :], False),
                      (c.GLA[64:65, d, :], True)])
        _col_to_rows(c, bcol, c.w_sm + glw,
                     [(c.GLB[0:1, d, :], False), (c.GLB[32:33, d, :], True)])
    c.w_gl = (c.dv.at(),)
    _tap(c, "gla", c.GLA[:, :, :], c.w_gl)
    _tap(c, "glb", c.GLB[:, :, :], c.w_gl)


def _inner_layer(c, s, apply_idx, l, YR, guard_wait):
    nc = c.nc
    R = c.R
    tag = f"{s}_{apply_idx}_{l}"
    with ExitStack() as ph:
        e = ph.enter_context
        c.TAB = e(nc.sbuf_tensor(f"TAB_{tag}", [128, 4, S // 128, DH], BF16))
        c.PT = e(nc.sbuf_tensor(f"PT_{tag}", [128, 4, 512], BF16))
        c.pt_guard = [(), (), (), ()]
        c.STG = e(nc.sbuf_tensor(f"STG_{tag}", [128, 4, 2, 128], BF16))
        with ExitStack() as bts:
            c.BT = bts.enter_context(
                nc.sbuf_tensor(f"BT_{tag}", [128, S // 128, DH], F32))
            _build_tables(c, l)
            fence(c)
        vsb = e(nc.sbuf_tensor(f"vsb_{tag}", [128, 8, HL, 65], BF16))
        qw = e(nc.sbuf_tensor(f"qw_{tag}", [128, 4, 512], BF16))
        kw = e(nc.sbuf_tensor(f"kw_{tag}", [128, 2, 4, 512], BF16))
        qc2 = e(nc.sbuf_tensor(f"qc2_{tag}", [128, 2, 512], BF16))
        kc2 = e(nc.sbuf_tensor(f"kc2_{tag}", [128, 2, 1024], BF16))
        o_l = e(nc.sbuf_tensor(f"ol_{tag}", [128, 4, 512], BF16))
        HT = e(nc.sbuf_tensor(f"HT_{tag}", [128, 2, DCH, 128], BF16))
        SQF = c.BSCR[:, 0:1024]
        QRS = e(nc.sbuf_tensor(f"QRS_{tag}", [128, 16], F32))
        OSC = e(nc.sbuf_tensor(f"OSC_{tag}", [128, 512], BF16))
        wqs = e(nc.sbuf_tensor(f"wqs_{tag}", [128, 2, QKV], BF16))
        wos = e(nc.sbuf_tensor(f"wos_{tag}", [128, 1, 4, 256], BF16))
        def initcur(g):
            g.memset(qc2[:], 0.0)
            g.memset(kc2[:], 0.0)
            g.memset(kc2[64:65, :, :], 1.0)
            g.memset(vsb[:], 0.0)
            return g.memset(vsb[:, :, :, 64:65], 1.0)
        wcur = ev_op(c, "gpsimd", initcur, waits=c.w_init)
        c.vslot_mark = {}
        c.wos_mark = [(c.pe.at(),)]
        c.sqf_guard = ()
        wqs_mark = [(c.pe.at(),), (c.pe.at(),)]
        ht_mark = [(), ()]
        vse = {}
        qwm, kwm = {}, {}
        for w in range(4 if not SKIP_INNER_ATTN else 0):
            wlay = (c.dv.at(),)
            for pr in range(2):
                tiles = (2 * pr, 2 * pr + 1)
                hbuf = pr % 2
                for wt in tiles:
                    T = w * 4 + wt
                    for d in range(DCH):
                        off = 1 if d < 4 else 0
                        src = YR[:, d, off + T * 128: off + T * 128 + 128]
                        eng_h = "vector" if DIAG_HCAST_DVE else "gpsimd"
                        ev_op(c, eng_h,
                              lambda g, o=HT[:, wt % 2, d, :], i=src: g.tensor_copy(o, i),
                              waits=wlay + guard_wait + ht_mark[wt % 2])
                wh = (c.po.at(),) + ((c.dv.at(),) if DIAG_HCAST_DVE else ())
                if DIAG_FENCE_QKV:
                    fence(c)
                if w == 0 and pr == 0:
                    _tap(c, "ht0", HT[:, :, :, :], wh)
                pb = {}
                w0s = {}
                for wt in tiles:
                    pb[wt] = bank(c, 3)
                    for part in range(3):
                        w0s[(wt, part)] = pb[wt][part].begin_fill()
                for d in range(DCH):
                    buf = d % 2
                    wld = pool_dma(c, wqs[:, buf, :], c.d["wqkv"][l, _ts(d), :],
                                   wait=wqs_mark[buf], sem=dsem(c, f"wqs{buf}"))
                    for wi, wt in enumerate(tiles):
                        for part in range(3):
                            bk_ = pb[wt][part]
                            last = (d == DCH - 1)
                            round_last = (wi == 1 and part == 2)
                            inc = (c.pe.bump(),) if (last or round_last) else ()
                            R.op("tensor",
                                 lambda t, o=bk_.t, l_=HT[:, wt % 2, d, :],
                                        r=wqs[:, buf, _ts(part, 512)],
                                        st=(d == 0), sp=last:
                                     t.matmul(o, l_, r, start=st, stop=sp),
                                 wait=wld + wh +
                                      (w0s[(wt, part)] if d == 0 else ()),
                                 inc=inc)
                            if last:
                                bk_.ready = (c.pe.at(),)
                    wqs_mark[buf] = (c.pe.at(),)
                for wt in tiles:
                    ht_mark[wt % 2] = (c.pe.at(),)
                if DIAG_FENCE_QKV:
                    fence(c)
                for wt in tiles:
                    T = w * 4 + wt
                    bq, bkk, bv = pb[wt]
                    slot = T % 8
                    if w == 0 and wt == 1:
                        _tap_psum(c, "bvps", bv)
                    vse[slot] = ev_op(
                        c, "vector",
                        lambda v, bv=bv, slot=slot: v.tensor_copy(
                            vsb[:, slot, :, 0:64],
                            bv.t[:].rearrange("p (h n) -> p h n", h=HL)),
                        waits=tuple(wcur) + c.vslot_mark.get(slot, ()),
                        banks=(bv,))
                    ev_op(c, "scalar",
                          lambda sc, bq=bq: sc.activation(SQF[:, 0:512], bq.t[:], AF.Square),
                          waits=tuple(bq.ready) + c.sqf_guard + c.bscr_guard)
                    ev_op(c, "scalar",
                          lambda sc, bkk=bkk: sc.activation(SQF[:, 512:1024], bkk.t[:], AF.Square),
                          waits=tuple(bkk.ready))
                    wsq = (c.ac.at(),)
                    ev_op(c, "vector",
                          lambda v: v.reduce_sum(
                              QRS[:, :].rearrange("p (a b) -> p a b", a=16),
                              SQF[:, :].rearrange("p (a b) -> p a b", a=16),
                              axis=AX.X),
                          waits=wsq)
                    c.sqf_guard = (c.dv.at(),)
                    c.bscr_guard = c.bscr_guard + (c.dv.at(),)
                    ev_op(c, "vector",
                          lambda v: v.tensor_scalar_max(QRS[:, :], QRS[:, :], 1e-24),
                          hard=True)
                    wv_ = (c.dv.at(),)
                    ev_op(c, "scalar",
                          lambda sc: sc.activation(QRS[:, :], QRS[:, :], AF.Ln),
                          waits=wv_)
                    ev_op(c, "scalar",
                          lambda sc: sc.activation(QRS[:, :], QRS[:, :], AF.Exp,
                                                   scale=np.float32(-0.5)), hard=True)
                    wrs = (c.ac.at(),)
                    for hp in range(4):
                        for hh in range(2):
                            h = 2 * hp + hh
                            for (qk, bsrc, tA, tV, rcol) in (
                                    (0, bq, 0, 1, h), (1, bkk, 2, 3, 8 + h)):
                                ps = bsrc.t
                                base = h * DH
                                rstd = QRS[:, rcol:rcol + 1]
                                dst = c.STG[:, hp, qk, :]
                                dsl = dst[:, hh * DH:(hh + 1) * DH]
                                t1 = c.LNT[:, 0:64]
                                t2 = c.LNT[:, 64:128]
                                tab_ = c.TAB
                                ev_op(c, "vector",
                                      lambda v, ps=ps, base=base, rstd=rstd, tA=tA,
                                             T=T, t1=t1, tab_=tab_:
                                      v.scalar_tensor_tensor(
                                          t1, ps[:, base:base + DH], rstd,
                                          tab_[:, tA, T, :],
                                          op0=ALU.mult, op1=ALU.mult),
                                      waits=wrs + c.w_tab + c.stg_guard)
                                ev_op(c, "vector",
                                      lambda v, ps=ps, base=base, rstd=rstd, tV=tV,
                                             T=T, t2=t2, tab_=tab_:
                                      v.scalar_tensor_tensor(
                                          t2[:, 0:32], ps[:, base + 32:base + 64], rstd,
                                          tab_[:, tV, T, 0:32],
                                          op0=ALU.mult, op1=ALU.mult))
                                ev_op(c, "vector",
                                      lambda v, ps=ps, base=base, rstd=rstd, tV=tV,
                                             T=T, t2=t2, tab_=tab_:
                                      v.scalar_tensor_tensor(
                                          t2[:, 32:64], ps[:, base:base + 32], rstd,
                                          tab_[:, tV, T, 32:64],
                                          op0=ALU.mult, op1=ALU.mult))
                                ev_op(c, "vector",
                                      lambda v, dsl=dsl, t1=t1, t2=t2:
                                          v.tensor_add(dsl, t1, t2),
                                      hard=True)
                    wstg = (c.dv.at(),)
                    bq.release((c.dv, c.dv.v))
                    bkk.release((c.dv, c.dv.v))
                    wtb = tb_wait(c)
                    for hp in range(4):
                        for qk in range(2):
                            tb = c.tbk[2 * hp + qk]
                            stg_ = c.STG
                            R.op("tensor",
                                 lambda t, tb=tb, hp=hp, qk=qk, stg_=stg_:
                                     t.transpose(tb.t[:], stg_[:, hp, qk, :], c.ID[:]),
                                 wait=wstg + wtb + c.w_init,
                                 inc=(c.pe.bump(),) if (hp == 3 and qk == 1) else ())
                    wtr = (c.pe.at(),)
                    c.stg_guard = (c.pe.at(),)
                    tbq = c.tbv[:, :, :].rearrange("p (a b) n -> p a b n", b=2)
                    ev_op(c, "vector",
                          lambda v, wt=wt: v.tensor_copy(qw[:, :, wt * 128:(wt + 1) * 128],
                                                         tbq[:, :, 0, :]),
                          waits=wtr + qwm.get(0, ()))
                    ev_op(c, "vector",
                          lambda v, wt=wt, w=w: v.tensor_copy(kw[:, w % 2, :, wt * 128:(wt + 1) * 128],
                                                              tbq[:, :, 1, :]),
                          waits=kwm.get(w % 2, ()))
                    tb_mark(c, (c.dv, c.dv.v))
            wqk = (c.dv.at(), c.ac.at())
            if w == 0:
                _tap(c, "qw0", qw[:, :, :], wqk)
                _tap(c, "kw0", kw[:, :, :, :], wqk)
                _tap(c, "vsb0", vsb[:, :, :, :], wqk)
            cur_mark = [(), ()]
            for hp in range(4):
                for hh in range(2):
                    h = 2 * hp + hh
                    cb = h % 2
                    wcp1 = ev_op(c, "gpsimd",
                                 lambda g, cb=cb, hp=hp, hh=hh: g.tensor_copy(
                                     qc2[0:64, cb, :], qw[_ts(hh, 64), hp, :]),
                                 waits=wqk + cur_mark[cb] + wcur)
                    ev_op(c, "gpsimd",
                          lambda g, cb=cb: g.memset(qc2[64:65, cb, :], 0.0))
                    if w > 0:
                        ev_op(c, "gpsimd",
                              lambda g, cb=cb, hp=hp, hh=hh, w=w: g.tensor_copy(
                                  kc2[0:64, cb, 0:512],
                                  kw[_ts(hh, 64), (w - 1) % 2, hp, :]))
                    wcp = ev_op(c, "gpsimd",
                                lambda g, cb=cb, hp=hp, hh=hh, w=w: g.tensor_copy(
                                    kc2[0:64, cb, 512:1024], kw[_ts(hh, 64), w % 2, hp, :]))
                    qwm[0] = (c.po.at(),)
                    kwm[w % 2] = (c.po.at(),)
                    kwm[(w - 1) % 2] = (c.po.at(),)
                    MX = c.MXS
                    nstW = 2 if w > 0 else 1
                    for qc in range(4):
                        if w > 0:
                            strips = [(qc * 128, 512), (qc * 128 + 512, 128)]
                        else:
                            strips = [(512, (qc + 1) * 128)]
                        sb_ = []
                        for (k0, ln_) in strips:
                            bb = bank(c)
                            mm_fill(c, bb, [(bb.t[:, 0:ln_],
                                             qc2[:, cb, _ts(qc)],
                                             kc2[:, cb, k0:k0 + ln_], wcp)])
                            sb_.append((bb, ln_))
                        if w > 0:
                            b0 = sb_[0][0]
                            ev_op(c, "vector",
                                  lambda v, bb=b0: v.tensor_add(
                                      bb.t[:, 0:128], bb.t[:, 0:128], c.MLO[:]),
                                  waits=c.w_init + tuple(b0.ready))
                            b0.ready = b0.ready + (c.dv.at(),)
                            bcz, lcz = sb_[1]
                            ev_op(c, "vector",
                                  lambda v, bcz=bcz: v.tensor_add(
                                      bcz.t[:, 0:128], bcz.t[:, 0:128], c.MUP[:]),
                                  waits=c.w_init + tuple(bcz.ready))
                            bcz.ready = bcz.ready + (c.dv.at(),)
                        else:
                            bcz, lcz = sb_[0]
                            ev_op(c, "vector",
                                  lambda v, bcz=bcz, lcz=lcz: v.tensor_add(
                                      bcz.t[:, lcz - 128:lcz], bcz.t[:, lcz - 128:lcz],
                                      c.MUP[:]),
                                  waits=c.w_init + tuple(bcz.ready))
                            bcz.ready = bcz.ready + (c.dv.at(),)
                        for i, (bb, ln_) in enumerate(sb_):
                            ev_op(c, "vector",
                                  lambda v, bb=bb, ln_=ln_, i=i, qc=qc: v.reduce_max(
                                      MX[:, qc * nstW + i:qc * nstW + i + 1],
                                      bb.t[:, 0:ln_], axis=AX.X),
                                  banks=(bb,))
                    if w > 0:
                        mxv = MX[:, 0:8].rearrange("p (q s) -> p q s", s=2)
                        ev_op(c, "vector",
                              lambda v, mxv=mxv: v.tensor_max(
                                  MX[:, 16:20], mxv[:, :, 0], mxv[:, :, 1]),
                              hard=True)
                        src16 = MX[:, 16:20]
                    else:
                        src16 = MX[:, 0:4]
                    ev_op(c, "vector",
                          lambda v, src16=src16: v.tensor_scalar_mul(
                              c.NMB[:, 0:4], src16, -1.0),
                          hard=True)
                    wnm = (c.dv.at(),)
                    tbm = bank(c)
                    w0m = tbm.begin_fill()
                    for qc in range(4):
                        R.op("tensor",
                             lambda t, tbm=tbm, qc=qc: t.matmul(
                                 tbm.t[0:1, _ts(qc)], c.NMB[:, qc:qc + 1],
                                 c.ID[:], start=True, stop=True),
                             wait=wnm + (w0m if qc == 0 else ()) + c.w_init,
                             inc=(c.pe.bump(),) if qc == 3 else ())
                    tbm.ready = (c.pe.at(),)
                    ev_op(c, "vector",
                          lambda v, tbm=tbm, cb=cb: v.tensor_copy(
                              qc2[64:65, cb, 0:512], tbm.t[0:1, 0:512]),
                          banks=(tbm,))
                    wmax = (c.dv.at(),)
                    if w == 0 and h == 0:
                        _tap(c, "qc0", qc2[:, cb, :], wmax)
                        _tap(c, "kc0", kc2[:, cb, :], wmax)
                    if w == 1 and h == 0:
                        _tap(c, "qc1", qc2[:, cb, :], wmax)
                        _tap(c, "kc1", kc2[:, cb, :], wmax)
                    ob = bank(c)
                    first = True
                    tlist = list(range(8)) if w > 0 else list(range(4, 8))
                    tlist = [t_ for t_ in tlist if t_ not in DIAG_SKIP_T]
                    tlist.sort(key=lambda t_: (t_ != 4,))
                    for t_ in tlist:
                        qlo = max(0, (t_ - 4) * 128)
                        qhi = min(512, t_ * 128 + 128)
                        bst = bank(c)
                        mm_fill(c, bst, [(bst.t[:, 0:qhi - qlo],
                                          kc2[:, cb, _ts(t_)],
                                          qc2[:, cb, qlo:qhi], wmax)])
                        if t_ >= 4 and t_ - 4 <= 3 and qlo == (t_ - 4) * 128:
                            ev_op(c, "vector",
                                  lambda v, bst=bst: v.tensor_add(
                                      bst.t[:, 0:128], bst.t[:, 0:128], c.MLO[:]),
                                  waits=c.w_init + tuple(bst.ready))
                            bst.ready = bst.ready + (c.dv.at(),)
                        if w > 0 and t_ <= 3:
                            off = t_ * 128 - qlo
                            ev_op(c, "vector",
                                  lambda v, bst=bst, off=off: v.tensor_add(
                                      bst.t[:, off:off + 128], bst.t[:, off:off + 128],
                                      c.MUP[:]),
                                  waits=c.w_init + tuple(bst.ready))
                            bst.ready = bst.ready + (c.dv.at(),)
                        ptslot = c.pt_rr % 4
                        c.pt_rr += 1
                        pts = c.PT[:, ptslot, 0:qhi - qlo]
                        wex = ev_op(c, "scalar",
                                    lambda sc, pts=pts, bst=bst, n=qhi - qlo:
                                        sc.activation(pts, bst.t[:, 0:n], AF.Exp),
                                    waits=c.pt_guard[ptslot], banks=(bst,))
                        slot = ((w - 1) * 4 + t_) % 8
                        w0 = ob.begin_fill() if first else ()
                        R.op("tensor",
                             lambda t, ob=ob, qlo=qlo, qhi=qhi, slot=slot, pts=pts,
                                    st=first, h=h:
                                 t.matmul(ob.t[0:65, qlo:qhi], vsb[:, slot, h, :],
                                          pts, start=st, stop=True),
                             wait=wex + tuple(vse.get(slot, ())) + tuple(w0),
                             inc=(c.pe.bump(),))
                        ob.ready = (c.pe.at(),)
                        c.pt_guard[ptslot] = (c.pe.at(),)
                        c.vslot_mark[slot] = (c.pe.at(),)
                        if w == 0 and h == 0:
                            _tap(c, f"pt{t_}", pts, wex)
                        if w == 1 and h == 0:
                            _tap(c, f"pu{t_}", pts, wex)
                        first = False
                    if w == 0 and h == 0:
                        _tap_psum(c, "obps", ob)
                    if w == 1 and h == 0:
                        _tap_psum(c, "obp1", ob)
                    rr_row = mrow(c, "rr")
                    ev_op(c, "vector",
                          lambda v, ob=ob: v.reciprocal(rr_row, ob.t[64:65, 0:512]),
                          waits=tuple(ob.ready))
                    ev_op(c, "vector",
                          lambda v: v.tensor_copy(c.RZ[0:1, 0, :], rr_row),
                          waits=c.rz_guard, hard=True)
                    ev_op(c, "vector",
                          lambda v: v.tensor_sub(mrow(c, "sc2"), rr_row, c.RZ[0:1, 0, :]),
                          hard=True)
                    ev_op(c, "vector",
                          lambda v: v.tensor_copy(c.RZ[32:33, 0, :], mrow(c, "sc2")),
                          hard=True)
                    wrr = (c.dv.at(),)
                    br = bank(c)
                    mm_fill(c, br, [(br.t[0:64, :], c.RE64[:], c.RZ[:, 0, :],
                                     tuple(wrr) + c.w_init)])
                    c.rz_guard = (c.pe.at(),)
                    ev_op(c, "scalar",
                          lambda sc, ob=ob: sc.copy(OSC[0:64, :], ob.t[0:64, 0:512]),
                          waits=wrr, banks=(ob,))
                    wos_ = (c.ac.at(),)
                    dsto = o_l[_ts(hh, 64), hp, :]
                    ev_op(c, "vector",
                          lambda v, dsto=dsto, br=br: v.tensor_mul(
                              dsto, OSC[0:64, :], br.t[0:64, :]),
                          waits=wos_ + c.ol_guard, banks=(br,))
                    cur_mark[cb] = (c.pe.at(),) + (c.dv.at(),)
            wo_w = (c.dv.at(),)
            if w == 0:
                _tap(c, "ol0", o_l[:, :, :], wo_w)
            if w == 1:
                _tap(c, "ol1", o_l[:, :, :], wo_w)
            for dp in range(4):
                buf = 0
                wld = pool_dma(c, wos[:, buf, :, :],
                               c.d["wol"][l, :, dp * 256:(dp + 1) * 256]
                               .rearrange("(c p) n -> p c n", p=128),
                               wait=c.wos_mark[buf], sem=dsem(c, "wos"))
                for dd in range(2):
                    dout = 2 * dp + dd
                    bb = bank(c)
                    parts = []
                    for ic in range(4):
                        parts.append((wos[:, buf, ic, dd * 128:(dd + 1) * 128],
                                      o_l[:, ic, :], wld + wo_w))
                    mm_fill(c, bb, parts)
                    c.wos_mark[buf] = (c.pe.at(),)
                    ev_op(c, "vector",
                          lambda v, bb=bb, dout=dout, w=w: v.tensor_add(
                              YR[:, dout, 1 + w * 512:1 + (w + 1) * 512],
                              YR[:, dout, 1 + w * 512:1 + (w + 1) * 512],
                              bb.t[:]),
                          banks=(bb,))
            c.ol_guard = (c.pe.at(),)
        fence(c)
    if not SKIP_INNER_FFN:
        _inner_ffn(c, s, apply_idx, l, YR)


def _inner_ffn(c, s, apply_idx, l, YR):
    nc = c.nc
    R = c.R
    tag = f"f{s}_{apply_idx}_{l}"
    with ExitStack() as fs:
        e = fs.enter_context
        c.GLA = e(nc.sbuf_tensor(f"GLA_{tag}", [128, DCH, 128], BF16))
        c.GLB = e(nc.sbuf_tensor(f"GLB_{tag}", [128, DCH, 128], BF16))
        HN = e(nc.sbuf_tensor(f"HN_{tag}", [128, DCH, 2, 512], BF16))
        U = e(nc.sbuf_tensor(f"U_{tag}", [128, 11, 2, 512], BF16))
        GG = e(nc.sbuf_tensor(f"GG_{tag}", [128, 2, 512], F32))
        w1s = e(nc.sbuf_tensor(f"w1s_{tag}", [128, 2, 2, DCH, 128], BF16))
        w2s = e(nc.sbuf_tensor(f"w2s_{tag}", [128, 2, 11, 128], BF16))
        SVP = e(nc.sbuf_tensor(f"SVP_{tag}", [128, DCH, 2], F32))
        _build_gl(c, l)

        def srcf(d, j):
            off = 1 if d < 4 else 0
            return YR[:, d, off + j * 512: off + j * 512 + 512]

        BC = S // 2
        wpre = ev_op(c, "gpsimd",
                     lambda g: g.tensor_copy(SVP[:, :, 0:1], YR[:, :, BC:BC + 1]),
                     waits=(c.dv.at(),))
        w1_mark = [(c.pe.at(),), (c.pe.at(),)]
        w2_mark = [(c.pe.at(),), (c.pe.at(),)]
        c.gg_guard = [(), ()]
        c.u_guard = ()
        hn_guard = ()
        for half in range(2):
            wy = (c.dv.at(),)
            if half == 1:
                ev_op(c, "gpsimd",
                      lambda g: g.tensor_copy(SVP[:, :, 1:2], YR[:, :, BC:BC + 1]),
                      waits=wy)
                ev_op(c, "gpsimd",
                      lambda g: g.tensor_copy(YR[:, :, BC:BC + 1], SVP[:, :, 0:1]))
                wy = wy + (c.po.at(),)
            c.tap_rz = True
            _ln(c, get_src=srcf,
                get_dst=lambda d, j: HN[:, d, j % 2, :],
                src_wait=wy + hn_guard,
                glA_fn=lambda d: c.GLA[:, d, :],
                glB_fn=lambda d: c.GLB[:, d, :],
                blocks=(2 * half, 2 * half + 2),
                mm_wait=c.w_gl)
            whn = (c.dv.at(),)
            if half == 0:
                _tap(c, "hn", HN[:, :, :, :], whn)
            else:
                _tap(c, f"hn2_{tag}", HN[:, :, :, :], whn)
                ev_op(c, "gpsimd",
                      lambda g: g.tensor_copy(YR[:, :, BC:BC + 1], SVP[:, :, 1:2]),
                      waits=whn + (c.ac.at(),))
            for grp in range(2):
                for i in range(11):
                    hid = grp * 11 + i
                    buf = i % 2
                    wld = pool_dma(
                        c, w1s[:, buf, 0, :, :],
                        c.d["w1"][l, :, _ts(hid)].rearrange("(cc p) n -> p cc n", p=128),
                        wait=w1_mark[buf], sem=dsem(c, f"w1s{buf}"))
                    wld2 = pool_dma(
                        c, w1s[:, buf, 1, :, :],
                        c.d["w1"][l, :, _ts(NHC + hid)].rearrange("(cc p) n -> p cc n", p=128),
                        sem=dsem(c, f"w1s{buf}"))
                    bks = {}
                    for ag in range(2):
                        for ttl in range(2):
                            tt = 2 * half + ttl
                            bb = bank(c)
                            parts = []
                            for d in range(DCH):
                                parts.append((w1s[:, buf, ag, d, :],
                                              HN[:, d, ttl, :],
                                              (wld + wld2 + whn) if d == 0 else ()))
                            mm_fill(c, bb, parts)
                            bks[(ag, ttl)] = bb
                    w1_mark[buf] = (c.pe.at(),)
                    for ttl in range(2):
                        ba, bg = bks[(0, ttl)], bks[(1, ttl)]
                        gg = GG[:, ttl, :]
                        ev_op(c, "scalar",
                              lambda sc, bg=bg, gg=gg, hid=hid: sc.activation(
                                  gg, bg.t[:], AF.Gelu,
                                  bias=c.B1C[:, l, NHC + hid:NHC + hid + 1]),
                              waits=c.w_b1c + c.gg_guard[ttl], banks=(bg,))
                        wgg = (c.ac.at(),)
                        ev_op(c, "vector",
                              lambda v, ba=ba, gg=gg, hid=hid, i=i, ttl=ttl:
                              v.scalar_tensor_tensor(
                                  U[:, i, ttl, :], ba.t[:],
                                  c.B1C[:, l, hid:hid + 1], gg,
                                  op0=ALU.add, op1=ALU.mult),
                              waits=wgg + c.w_b1c + c.u_guard, banks=(ba,))
                        c.gg_guard[ttl] = (c.dv.at(),)
                wu = (c.dv.at(),)
                for dc in range(DCH):
                    buf = dc % 2
                    wld = pool_dma(
                        c, w2s[:, buf, :, :],
                        c.d["w2"][l, grp * 1408:(grp + 1) * 1408, _ts(dc)]
                        .rearrange("(hh p) n -> p hh n", p=128),
                        wait=w2_mark[buf], sem=dsem(c, f"w2s{buf}"))
                    for ttl in range(2):
                        tt = 2 * half + ttl
                        bb = bank(c)
                        parts = []
                        for i in range(11):
                            parts.append((w2s[:, buf, i, :], U[:, i, ttl, :],
                                          (wld + wu) if i == 0 else ()))
                        mm_fill(c, bb, parts)
                        w2_mark[buf] = (c.pe.at(),)
                        wbnd = wpre if half == 0 else ()
                        if grp == 0:
                            ev_op(c, "vector",
                                  lambda v, bb=bb, dc=dc, tt=tt: v.tensor_add(
                                      YR[:, dc, 1 + tt * 512:1 + (tt + 1) * 512],
                                      YR[:, dc, 1 + tt * 512:1 + (tt + 1) * 512],
                                      bb.t[:]),
                                  waits=wbnd, banks=(bb,))
                        else:
                            ev_op(c, "vector",
                                  lambda v, bb=bb, dc=dc, tt=tt: v.scalar_tensor_tensor(
                                      YR[:, dc, 1 + tt * 512:1 + (tt + 1) * 512],
                                      bb.t[:],
                                      c.SM[:, l * 24 + 16 + dc:l * 24 + 16 + dc + 1],
                                      YR[:, dc, 1 + tt * 512:1 + (tt + 1) * 512],
                                      op0=ALU.add, op1=ALU.add),
                                  waits=c.w_sm + wbnd, banks=(bb,))
                if half == 0 and grp == 0:
                    _tap(c, "u", U[:, :, :, :], (c.dv.at(),))
                c.u_guard = (c.pe.at(),)
            hn_guard = (c.pe.at(),)
        fence(c)


def _global_attn(c, s):
    nc = c.nc
    R = c.R
    tag = f"g{s}"
    with ExitStack() as ph:
        e = ph.enter_context
        kfm = e(nc.sbuf_tensor(f"kfm_{tag}", [128, DCH, S], BF16))
        vsg = e(nc.sbuf_tensor(f"vsg_{tag}", [128, 16, HG, 65], BF16))
        qfm = e(nc.sbuf_tensor(f"qfm_{tag}", [128, DCH, 512], BF16))
        ybf = e(nc.sbuf_tensor(f"ybf_{tag}", [128, DCH, 512], BF16))
        kcur = e(nc.sbuf_tensor(f"kcur_{tag}", [128, 2, S], BF16))
        qcur = e(nc.sbuf_tensor(f"qcur_{tag}", [128, 2, 512], BF16))
        og = e(nc.sbuf_tensor(f"og_{tag}", [128, DCH, 512], BF16))
        wkq = e(nc.sbuf_tensor(f"wkq_{tag}", [128, 2, DCH, 128], BF16))
        wvs = e(nc.sbuf_tensor(f"wvs_{tag}", [128, 2, 1024], BF16))
        wos = e(nc.sbuf_tensor(f"wos_{tag}", [128, 2, DCH, 128], BF16))
        OSC = e(nc.sbuf_tensor(f"gOSC_{tag}", [128, 512], BF16))
        c.LNT = e(nc.sbuf_tensor(f"gLNT_{tag}", [128, 512], F32))
        c.BSCR = e(nc.sbuf_tensor(f"gBSCR_{tag}", [128, 2048], BF16))
        c.stat_tag = f"g{tag}"
        c.PT = e(nc.sbuf_tensor(f"gPT_{tag}", [128, 4, 512], BF16))
        c.pt_guard = [(), (), (), ()]
        c.bscr_guard = ()
        c.pt_rr = 0
        MX = c.MXS

        def initg(g):
            g.memset(kcur[:], 0.0)
            g.memset(kcur[64:65, :, :], 1.0)
            g.memset(qcur[:], 0.0)
            g.memset(vsg[:], 0.0)
            return g.memset(vsg[:, :, :, 64:65], 1.0)
        wcur = ev_op(c, "gpsimd", initg, waits=c.w_init)
        _tap_async(c, f"avA_{tag}", vsg[:, 15, :, :], wcur)
        def canary(lbl):
            _tap(c, f"cn_{lbl}_{tag}", vsg[:, 15, 12:16, 63:65], (c.po.at(), c.dv.at(), c.pe.at()))
        canary("T0")
        c.vslot_mark = {}
        c.vsg_w = {}
        wkq_mark = [(c.pe.at(),), (c.pe.at(),)]
        wvs_mark = [(c.pe.at(),), (c.pe.at(),)]
        wos_mark = [(c.pe.at(),), (c.pe.at(),)]
        kfm_mark = ()
        qfm_mark = ()
        ybf_mark = ()
        ogm = ()
        cur_mark = [(), ()]

        for qb in range(NTB):
            tb = qb
            _ln(c, get_src=lambda d, j: c.XR[:, d, _ts(j, 512)],
                get_dst=lambda d, j: ybf[:, d, :],
                src_wait=c.w_x + ybf_mark, blocks=(tb, tb + 1))
            wyb = (c.dv.at(),)
            canary(f"ln{qb}")
            for wname, dstbuf, dmark, scl in (("wk", kfm, kfm_mark, None),
                                              ("wq", qfm, qfm_mark, 0.125)):
                for cc in range(DCH):
                    buf = cc % 2
                    wld = pool_dma(
                        c, wkq[:, buf, :, :],
                        c.d[wname][:, _ts(cc)].rearrange("(dd p) n -> p dd n", p=128),
                        wait=wkq_mark[buf], sem=dsem(c, f"gkq{buf}"))
                    bb = bank(c)
                    parts = []
                    for d in range(DCH):
                        parts.append((wkq[:, buf, d, :], ybf[:, d, :],
                                      (wld + wyb) if d == 0 else ()))
                    mm_fill(c, bb, parts)
                    wkq_mark[buf] = (c.pe.at(),)
                    if scl is None:
                        ev_op(c, "vector",
                              lambda v, bb=bb, cc=cc, tb=tb, dstbuf=dstbuf:
                                  v.tensor_copy(dstbuf[:, cc, _ts(tb, 512)], bb.t[:]),
                              banks=(bb,))
                    else:
                        ev_op(c, "vector",
                              lambda v, bb=bb, cc=cc, dstbuf=dstbuf:
                                  v.tensor_scalar_mul(dstbuf[:, cc, :], bb.t[:],
                                                      0.125),
                              banks=(bb,))
            wkq_done = (c.dv.at(),)
            canary(f"kq{qb}")
            if qb == 3:
                _tap(c, f"gqf3_{tag}", qfm[:, :, :], wkq_done)
                _tap_async(c, f"agqf3_{tag}", qfm[:, :, :], wkq_done)
                _tap_async(c, f"agkf3_{tag}", kfm[:, :, :], wkq_done)
            for pr4 in range(2):
                tts = (2 * pr4, 2 * pr4 + 1)
                pbv = {}
                w0v = {}
                for tt4 in tts:
                    pbv[tt4] = bank(c, 2)
                    for half in range(2):
                        w0v[(tt4, half)] = pbv[tt4][half].begin_fill()
                for d in range(DCH):
                    buf = d % 2
                    wld = pool_dma(c, wvs[:, buf, :], c.d["wv"][_ts(d), :],
                                   wait=wvs_mark[buf], sem=dsem(c, f"gwv{buf}"))
                    for tt4 in tts:
                        for half in range(2):
                            bk_ = pbv[tt4][half]
                            last = (d == DCH - 1)
                            rlast = (tt4 == tts[1] and half == 1)
                            R.op("tensor",
                                 lambda t, o=bk_.t, tt4=tt4, half=half, d=d, buf=buf:
                                     t.matmul(o, ybf[:, d, _ts(tt4)],
                                              wvs[:, buf, _ts(half, 512)],
                                              start=(d == 0), stop=(d == DCH - 1)),
                                 wait=wld + wyb + (w0v[(tt4, half)] if d == 0 else ()),
                                 inc=(c.pe.bump(),) if (last or rlast) else ())
                            if last:
                                bk_.ready = (c.pe.at(),)
                    wvs_mark[buf] = (c.pe.at(),)
                for tt4 in tts:
                    slot = tb * 4 + tt4
                    for half in range(2):
                        bk_ = pbv[tt4][half]
                        ev_op(c, "vector",
                              lambda v, bk_=bk_, slot=slot, half=half: v.tensor_copy(
                                  vsg[:, slot, half * 8:(half + 1) * 8, 0:64],
                                  bk_.t[:].rearrange("p (h n) -> p h n", h=8)),
                              waits=tuple(wcur) + c.vslot_mark.get((slot, half), ()),
                              banks=(bk_,))
                        c.vsg_w[(slot, half)] = (c.dv.at(),)
            ybf_mark = (c.pe.at(),)
            wv_done = (c.dv.at(),)
            canary(f"v{qb}")
            if qb == 3:
                _tap_async(c, f"avB_{tag}", vsg[:, 15, :, :], wv_done)
            nkt = 4 * qb + 4
            for h in range(HG):
                cb = h % 2
                wc1 = ev_op(c, "gpsimd",
                            lambda g, h=h, cb=cb: g.tensor_copy(
                                qcur[0:64, cb, :], qfm[_ts(h % 2, 64), h // 2, :]),
                            waits=wkq_done + cur_mark[cb])
                ev_op(c, "gpsimd",
                      lambda g, cb=cb: g.memset(qcur[64:65, cb, :], 0.0))
                wcp = ev_op(c, "gpsimd",
                            lambda g, h=h, qb=qb, cb=cb: g.tensor_copy(
                                kcur[0:64, cb, 0:(qb + 1) * 512],
                                kfm[_ts(h % 2, 64), h // 2, 0:(qb + 1) * 512]),
                            waits=wkq_done + cur_mark[cb])
                nst = qb + 1
                for qc in range(4):
                    kspan = qb * 512 + qc * 128 + 128
                    sb_ = []
                    for si in range(nst):
                        k0 = si * 512
                        ln_ = min(512, kspan - k0)
                        bb = bank(c)
                        mm_fill(c, bb, [(bb.t[:, 0:ln_], qcur[:, cb, _ts(qc)],
                                         kcur[:, cb, k0:k0 + ln_], tuple(wcp))])
                        sb_.append((bb, ln_))
                    bcz, lcz = sb_[-1]
                    ev_op(c, "vector",
                          lambda v, bcz=bcz, lcz=lcz: v.tensor_add(
                              bcz.t[:, lcz - 128:lcz], bcz.t[:, lcz - 128:lcz],
                              c.MUP[:]),
                          waits=c.w_init + tuple(bcz.ready))
                    bcz.ready = bcz.ready + (c.dv.at(),)
                    for i, (bb, ln_) in enumerate(sb_):
                        ev_op(c, "vector",
                              lambda v, bb=bb, ln_=ln_, i=i, qc=qc: v.reduce_max(
                                  MX[:, qc * nst + i:qc * nst + i + 1],
                                  bb.t[:, 0:ln_], axis=AX.X),
                              banks=(bb,))
                if nst == 1:
                    src16 = MX[:, 0:4]
                else:
                    mxv = MX[:, 0:4 * nst].rearrange("p (q s) -> p q s", s=nst)
                    ev_op(c, "vector",
                          lambda v, mxv=mxv: v.tensor_max(
                              MX[:, 16:20], mxv[:, :, 0], mxv[:, :, 1]),
                          hard=True)
                    for i in range(2, nst):
                        ev_op(c, "vector",
                              lambda v, mxv=mxv, i=i: v.tensor_max(
                                  MX[:, 16:20], MX[:, 16:20], mxv[:, :, i]),
                              hard=True)
                    src16 = MX[:, 16:20]
                ev_op(c, "vector",
                      lambda v, src16=src16: v.tensor_scalar_mul(
                          c.NMB[:, 0:4], src16, -1.0),
                      hard=True)
                wnm = (c.dv.at(),)
                tbm = bank(c)
                w0m = tbm.begin_fill()
                for qc in range(4):
                    R.op("tensor",
                         lambda t, tbm=tbm, qc=qc: t.matmul(
                             tbm.t[0:1, _ts(qc)], c.NMB[:, qc:qc + 1],
                             c.ID[:], start=True, stop=True),
                         wait=wnm + (w0m if qc == 0 else ()) + c.w_init,
                         inc=(c.pe.bump(),) if qc == 3 else ())
                tbm.ready = (c.pe.at(),)
                ev_op(c, "vector",
                      lambda v, tbm=tbm, cb=cb: v.tensor_copy(
                          qcur[64:65, cb, 0:512], tbm.t[0:1, 0:512]),
                      banks=(tbm,))
                wmax = (c.dv.at(),)
                if qb == 3 and h in (12, 14):
                    _tap_async(c, f"aqc_{h}_{tag}", qcur[:, cb, :], wmax)
                    _tap_async(c, f"amx_{h}_{tag}", c.MXS[:, :], wmax)
                if qb == 1 and h == 0:
                    _tap(c, "gqc1", qcur[:, cb, :], wmax)
                    _tap(c, "gkc1", kcur[:, cb, :], wmax)
                if qb == 3 and h == 12:
                    _tap(c, f"gqc3_{tag}", qcur[:, cb, :], wmax)
                ob = bank(c)
                first = True
                for t_ in range(nkt):
                    qlo = max(0, t_ * 128 - qb * 512)
                    qhi = 512
                    bst = bank(c)
                    mm_fill(c, bst, [(bst.t[:, 0:qhi - qlo], kcur[:, cb, _ts(t_)],
                                      qcur[:, cb, qlo:qhi], wmax)])
                    if t_ >= 4 * qb:
                        ev_op(c, "vector",
                              lambda v, bst=bst: v.tensor_add(
                                  bst.t[:, 0:128], bst.t[:, 0:128], c.MLO[:]),
                              waits=c.w_init + tuple(bst.ready))
                        bst.ready = bst.ready + (c.dv.at(),)
                    ptslot = c.pt_rr % 4
                    c.pt_rr += 1
                    pts = c.PT[:, ptslot, 0:qhi - qlo]
                    wex = ev_op(c, "scalar",
                                lambda sc, pts=pts, bst=bst, n=qhi - qlo:
                                    sc.activation(pts, bst.t[:, 0:n], AF.Exp),
                                waits=c.pt_guard[ptslot], banks=(bst,))
                    w0 = ob.begin_fill() if first else ()
                    R.op("tensor",
                         lambda t, ob=ob, qlo=qlo, pts=pts, slot=t_, st=first, h=h:
                             t.matmul(ob.t[0:65, qlo:512], vsg[:, slot, h, :],
                                      pts, start=st, stop=True),
                         wait=wex + tuple(c.vsg_w.get((t_, h // 8), ())) + tuple(w0),
                         inc=(c.pe.bump(),))
                    ob.ready = (c.pe.at(),)
                    c.pt_guard[ptslot] = (c.pe.at(),)
                    c.vslot_mark[(t_, h // 8)] = (c.pe.at(),)
                    first = False
                if qb == 1 and h == 0:
                    _tap_psum(c, "gob1", ob)
                if qb == 3 and h == 12:
                    _tap_psum(c, f"gob3_{tag}", ob)
                if qb == 3 and h in (12, 14) and f"aob_{h}_{tag}" in DEBUG_TAPS:
                    stg_ = c.BSCR[:, (h - 12) // 2 * 512:(h - 12) // 2 * 512 + 512]
                    wst_ = ev_op(c, "vector",
                                 lambda v, stg_=stg_, ob=ob: v.tensor_copy(stg_, ob.t[:]),
                                 waits=tuple(ob.ready))
                    _tap_async(c, f"aob_{h}_{tag}", stg_, wst_)
                rr_row = mrow(c, "rr")
                ev_op(c, "vector",
                      lambda v, ob=ob: v.reciprocal(rr_row, ob.t[64:65, 0:512]),
                      waits=tuple(ob.ready))
                ev_op(c, "vector",
                      lambda v: v.tensor_copy(c.RZ[0:1, 0, :], rr_row),
                      waits=c.rz_guard, hard=True)
                ev_op(c, "vector",
                      lambda v: v.tensor_sub(mrow(c, "sc2"), rr_row, c.RZ[0:1, 0, :]),
                      hard=True)
                ev_op(c, "vector",
                      lambda v: v.tensor_copy(c.RZ[32:33, 0, :], mrow(c, "sc2")),
                      hard=True)
                wrr = (c.dv.at(),)
                br = bank(c)
                mm_fill(c, br, [(br.t[0:64, :], c.RE64[:], c.RZ[:, 0, :],
                                 tuple(wrr) + c.w_init)])
                c.rz_guard = (c.pe.at(),)
                ev_op(c, "scalar",
                      lambda sc, ob=ob: sc.copy(OSC[0:64, :], ob.t[0:64, 0:512]),
                      waits=wrr, banks=(ob,))
                wos_ = (c.ac.at(),)
                dsto = og[_ts(h % 2, 64), h // 2, :]
                ev_op(c, "vector",
                      lambda v, dsto=dsto, br=br: v.tensor_mul(
                          dsto, OSC[0:64, :], br.t[0:64, :]),
                      waits=wos_ + ogm, banks=(br,))
                cur_mark[cb] = (c.pe.at(), c.dv.at())
                if h % 4 == 3:
                    canary(f"h{qb}_{h}")
            if qb == 3:
                _tap(c, f"gog3_{tag}", og[:, :, :], (c.dv.at(),))
                _tap_async(c, f"agog3_{tag}", og[:, :, :], (c.dv.at(),))
            wog_w = (c.dv.at(),)
            for dc in range(DCH):
                buf = dc % 2
                wld = pool_dma(c, wos[:, buf, :, :],
                               c.d["wog"][:, _ts(dc)].rearrange("(ic p) n -> p ic n", p=128),
                               wait=wos_mark[buf], sem=dsem(c, f"gwo{buf}"))
                bb = bank(c)
                parts = []
                for ic in range(DCH):
                    parts.append((wos[:, buf, ic, :], og[:, ic, :],
                                  (wld + wog_w) if ic == 0 else ()))
                mm_fill(c, bb, parts)
                wos_mark[buf] = (c.pe.at(),)
                ev_op(c, "vector",
                      lambda v, bb=bb, dc=dc, qb=qb: v.tensor_add(
                          c.XR[:, dc, _ts(qb, 512)],
                          c.XR[:, dc, _ts(qb, 512)], bb.t[:]),
                      waits=c.w_x, banks=(bb,))
            ogm = (c.pe.at(),)
            canary(f"wo{qb}")
        _tap_async(c, f"avD_{tag}", vsg[:, 15, :, :], (c.dv.at(),))
        fence(c)


_CACHE = {}


def _get_nc(stages):
    key = (tuple(stages), INNER_LAYERS, SKIP_INNER_FFN, SKIP_INNER_ATTN)
    if key not in _CACHE:
        _CACHE[key] = build_nc(stages)
    return _CACHE[key]


def launch(inputs, stages=ALL_STAGES):
    g, xs = host_prep(inputs)
    nc = _get_nc(stages)
    in_maps = []
    for cid in range(NCORE):
        m = dict(g)
        m["xT"] = xs[cid]
        in_maps.append(m)
    res = run_bass_kernel_spmd(nc, in_maps, core_ids=list(range(NCORE)))
    outs = []
    for cid in range(NCORE):
        o = np.asarray(res.results[cid]["outT"], np.float32)
        outs.append(o.reshape(D, NSEQ, S).transpose(1, 2, 0))
    if DEBUG_TAPS:
        global LAST_TAPS
        LAST_TAPS = res.results
    return np.concatenate(outs, axis=0)


def kernel(**inputs):
    return launch(inputs, ALL_STAGES)


# version 07
# speedup vs baseline: 1.1673x; 1.1673x over previous
import numpy as np
import ml_dtypes
from contextlib import ExitStack

import concourse.bass as bass
import concourse.mybir as mybir
from concourse.bass_utils import run_bass_kernel_spmd

F32 = mybir.dt.float32
BF16 = mybir.dt.bfloat16
AF = mybir.ActivationFunctionType
ALU = mybir.AluOpType
AX = mybir.AxisListType

B, S, D = 16, 2048, 1024
NSEQ = 2
NCORE = 8
DCH = D // 128
NTB = S // 512
WIN = 512
HL, HG, DH = 8, 16, 64
HID, HIDP = 2730, 2816
NHC = HIDP // 128
QKV, INNER = 1536, 512
EPS = 1e-5
NEG = -1.0e30

ALL_STAGES = ("ffn1", "attn", "ffn2")
DIAG_SERIAL_WQS = False
DIAG_HCAST_DVE = False
DIAG_FENCE_QKV = False
DIAG_SKIP_T = ()
DEBUG_TAPS = ()
INNER_LAYERS = 2
SKIP_INNER_FFN = False
SKIP_INNER_ATTN = False


class Ctr:
    __slots__ = ("h", "v", "owner", "step")

    def __init__(self, h, owner, step=1):
        self.h, self.v, self.owner, self.step = h, 0, owner, step

    def bump(self):
        self.v += self.step
        return (self.h, self.step)

    def at(self):
        return (self, self.v)


class Rec:
    def __init__(self):
        self.items = []

    def op(self, eng, fn, wait=(), inc=(), hwait=()):
        self.items.append((eng, tuple(wait), fn, tuple(inc), tuple(hwait)))

    def replay(self, block):
        for name in ("tensor", "vector", "scalar", "gpsimd", "sync"):
            items = [(w, f, i, hw) for (e, w, f, i, hw) in self.items if e == name]

            def emit(eng, items=items, name=name):
                seen = {}
                for (waits, fn, incs, hwaits) in items:
                    for (ctr, val) in waits:
                        if ctr.owner == name or val <= 0:
                            continue
                        if seen.get(id(ctr), -1) >= val:
                            continue
                        seen[id(ctr)] = val
                        eng.wait_ge(ctr.h, val)
                    for (ctr, val) in hwaits:
                        if val <= 0 or seen.get(id(ctr), -1) >= val:
                            continue
                        seen[id(ctr)] = val
                        eng.wait_ge(ctr.h, val)
                    ins = fn(eng)
                    for (h, n) in incs:
                        ins = ins.then_inc(h, n)

            getattr(block, name)(emit)


class Bank:
    __slots__ = ("t", "ready", "marks", "held")

    def __init__(self, t):
        self.t = t
        self.ready = ()
        self.marks = []
        self.held = False

    def begin_fill(self):
        w = tuple(self.marks)
        self.marks = []
        self.held = True
        return w

    def release(self, ctrval):
        self.marks.append(ctrval)
        self.held = False


class Ctx:
    pass


def _ts(i, n=128):
    return slice(i * n, (i + 1) * n)


_MROW = {"mu": 0, "rstd": 1, "var": 1, "tmp": 2, "sc2": 2, "rr": 0}


def mrow(c, name):
    i = _MROW[name]
    return c.MISC[0:1, i * 512:(i + 1) * 512]


def _rotary_tables():
    f32 = np.float32
    half = np.arange(0, DH, 2, dtype=f32)
    inv_freq = (f32(1.0) / np.power(f32(10000.0), half / f32(DH))).astype(f32)
    t = np.arange(S, dtype=f32)
    freqs = (t[:, None] * inv_freq[None, :]).astype(f32)
    freqs = np.concatenate([freqs, freqs], axis=-1)
    base = ((half + f32(0.4) * f32(DH)) / (f32(1.4) * f32(DH))).astype(f32)
    power = ((t - f32(S // 2)) / f32(WIN // 2)).astype(f32)
    scale = np.power(base[None, :], power[:, None]).astype(f32)
    scale = np.concatenate([scale, scale], axis=-1)
    cos = np.cos(freqs).astype(f32)
    sin = np.sin(freqs).astype(f32)
    return np.stack([(cos * scale).astype(f32), (sin * scale).astype(f32),
                     (cos / scale).astype(f32), (sin / scale).astype(f32)], axis=0)


def host_prep(inputs):
    f32 = np.float32
    g = {}
    g["wq"] = np.ascontiguousarray(np.asarray(inputs["L0_attn_W_Q"], f32).T)
    g["wk"] = np.ascontiguousarray(np.asarray(inputs["L0_attn_W_K"], f32).T)
    g["wv"] = np.ascontiguousarray(np.asarray(inputs["L0_attn_W_V"], f32).T)
    g["wog"] = np.ascontiguousarray(np.asarray(inputs["L0_attn_W_O"], f32).T)
    g["wqkv"] = np.ascontiguousarray(np.asarray(inputs["L0_ffn_wqkv"], f32))
    g["wol"] = np.ascontiguousarray(np.asarray(inputs["L0_ffn_wo"], f32))
    w1 = np.asarray(inputs["L0_ffn_w1"], f32)
    w1p = np.zeros((2, D, 2 * HIDP), dtype=f32)
    w1p[:, :, 0:HID] = w1[:, :, 0:HID]
    w1p[:, :, HIDP:HIDP + HID] = w1[:, :, HID:2 * HID]
    g["w1"] = w1p
    w2 = np.asarray(inputs["L0_ffn_w2"], f32)
    w2p = np.zeros((2, HIDP, D), dtype=f32)
    w2p[:, 0:HID, :] = w2
    g["w2"] = w2p
    b1 = np.asarray(inputs["L0_ffn_b1"], f32)
    b1p = np.zeros((2, 2 * HIDP), dtype=f32)
    b1p[:, 0:HID] = b1[:, 0:HID]
    b1p[:, HIDP:HIDP + HID] = b1[:, HID:2 * HID]
    g["b1c"] = np.ascontiguousarray(b1p.reshape(2, 2 * NHC, 128).transpose(2, 0, 1))
    sm = np.zeros((128, 2, 3, DCH), dtype=f32)
    for l in range(2):
        sm[:, l, 0, :] = np.asarray(inputs["L0_ffn_ln_g"][l], f32).reshape(DCH, 128).T
        sm[:, l, 1, :] = np.asarray(inputs["L0_ffn_ln_b"][l], f32).reshape(DCH, 128).T
        sm[:, l, 2, :] = np.asarray(inputs["L0_ffn_b2"][l], f32).reshape(DCH, 128).T
    g["smalls"] = np.ascontiguousarray(sm.reshape(128, 2 * 3 * DCH))
    qsks = np.stack([np.asarray(inputs["L0_ffn_q_scale"], f32),
                     np.asarray(inputs["L0_ffn_k_scale"], f32)], axis=1)
    g["qsks"] = np.ascontiguousarray(qsks.reshape(1, 2 * 2 * DH))
    g["tabs"] = np.ascontiguousarray(_rotary_tables())

    x = np.asarray(inputs["L0_attn_x"], f32)
    xs = []
    for cid in range(NCORE):
        xc = x[2 * cid:2 * cid + 2]
        xs.append(np.ascontiguousarray(xc.transpose(2, 0, 1).reshape(D, NSEQ * S)))
    return g, xs


def bank(c, n=1):
    out = []
    for _ in range(n):
        for _try in range(len(c.banks) + 1):
            b = c.banks[c.rr % len(c.banks)]
            c.rr += 1
            if not b.held:
                b.held = True
                break
        else:
            raise RuntimeError("all psum banks held")
        out.append(b)
    return out if n > 1 else out[0]


def tbank(c):
    b = c.tbk[c.trr % len(c.tbk)]
    c.trr += 1
    return b


def tb_wait(c):
    return tuple(c.tb_marks.values())


def tb_mark(c, ctrval):
    c.tb_marks[id(ctrval[0])] = ctrval


def mm_fill(c, b, parts, out=None):
    R = c.R
    w0 = b.begin_fill()
    n = len(parts)
    for i, p in enumerate(parts):
        if len(p) == 3:
            l, r, wop = p
            o = b.t if out is None else out
        else:
            o, l, r, wop = p
        st, sp = (i == 0), (i == n - 1)
        inc = (c.pe.bump(),) if sp else ()
        R.op("tensor",
             lambda t, o=o, l=l, r=r, st=st, sp=sp: t.matmul(o, l, r, start=st, stop=sp),
             wait=tuple(wop) + (w0 if i == 0 else ()), inc=inc)
    b.ready = (c.pe.at(),)


def ev_op(c, eng, fn, waits=(), banks=(), hard=False):
    ctr = {"vector": c.dv, "scalar": c.ac, "gpsimd": c.po, "tensor": c.pe}[eng]
    allw = tuple(waits)
    for b in banks:
        allw = allw + tuple(b.ready)
    hw = ((ctr, ctr.v),) if hard else ()
    c.R.op(eng, fn, wait=allw, inc=(ctr.bump(),), hwait=hw)
    for b in banks:
        b.release(ctr.at())
    return (ctr.at(),)


def fence(c):
    marks = (c.pe.at(), c.dv.at(), c.ac.at(), c.po.at(), c.pld.at(), c.spd.at())
    c.R.op("tensor", lambda t: t.nop(), wait=marks)
    c.R.op("vector", lambda v: v.engine_nop(), wait=marks)
    c.R.op("scalar", lambda sc: sc.nop(), wait=marks)
    c.R.op("gpsimd", lambda g: g.engine_nop(), wait=marks)
    c.R.op("sync", lambda sy: sy.nop(), wait=marks)


def dsem(c, key):
    if key not in c.dsems:
        h = c.stack.enter_context(c.nc.semaphore(f"ds_{key}"))
        c.dsems[key] = Ctr(h, None, 16)
    return c.dsems[key]


def pool_dma(c, out_ap, in_ap, wait=(), sem=None):
    ctr = sem if sem is not None else c.pld
    c.R.op("gpsimd",
           lambda g, o=out_ap, i=in_ap: g.dma_start(out=o, in_=i),
           wait=tuple(wait), inc=(ctr.bump(),))
    return (ctr.at(),)


def _tap_psum(c, name, bb):
    if name not in DEBUG_TAPS or name in c.taps:
        return
    lnt = c.LNT
    w = ev_op(c, "vector", lambda v, lnt=lnt: v.tensor_copy(lnt[:, 0:512], bb.t[:]),
              waits=tuple(bb.ready))
    _tap(c, name, lnt[:, 0:512], w)


def _tap(c, name, ap, wait):
    if name not in DEBUG_TAPS or name in c.taps:
        return
    c.taps[name] = True
    shape = list(ap.shape)
    d = c.nc.dram_tensor(f"dbg_{name}", shape, ap.dtype, kind="ExternalOutput").ap()
    sync_dma(c, d, ap, wait=wait)
    fence(c)


def _tap_async(c, name, ap, wait):
    if name not in DEBUG_TAPS or name in c.taps:
        return
    c.taps[name] = True
    shape = list(ap.shape)
    d = c.nc.dram_tensor(f"dbg_{name}", shape, ap.dtype, kind="ExternalOutput").ap()
    sync_dma(c, d, ap, wait=wait)


def sync_dma(c, out_ap, in_ap, wait=(), sem=None):
    ctr = sem if sem is not None else c.spd
    c.R.op("sync",
           lambda sy, o=out_ap, i=in_ap: sy.dma_start(out=o, in_=i),
           wait=tuple(wait), inc=(ctr.bump(),))
    return (ctr.at(),)


def build_nc(stages):
    nc = bass.Bass("TRN2", target_bir_lowering=False, debug=False)
    c = Ctx()
    c.nc = nc
    c.R = Rec()

    c.d = {}
    for name, shape in [("xT", [D, NSEQ * S]), ("wq", [D, D]), ("wk", [D, D]),
                        ("wv", [D, D]), ("wog", [D, D]), ("wqkv", [2, D, QKV]),
                        ("wol", [2, INNER, D]), ("w1", [2, D, 2 * HIDP]),
                        ("w2", [2, HIDP, D]), ("b1c", [128, 2, 2 * NHC]),
                        ("smalls", [128, 2 * 3 * DCH]), ("qsks", [1, 2 * 2 * DH]),
                        ("tabs", [4, S, DH])]:
        c.d[name] = nc.dram_tensor(name, shape, F32, kind="ExternalInput").ap()
    c.d["outT"] = nc.dram_tensor("outT", [D, NSEQ * S], F32, kind="ExternalOutput").ap()

    with ExitStack() as stack:
        c.stack = stack
        e = stack.enter_context
        c.pe = Ctr(e(nc.semaphore("c_pe")), "tensor")
        c.dv = Ctr(e(nc.semaphore("c_dv")), "vector")
        c.ac = Ctr(e(nc.semaphore("c_ac")), "scalar")
        c.po = Ctr(e(nc.semaphore("c_po")), "gpsimd")
        c.pld = Ctr(e(nc.semaphore("c_pld")), None, 16)
        c.spd = Ctr(e(nc.semaphore("c_spd")), None, 16)
        c.dsems = {}

        c.banks = [Bank(e(nc.psum_tensor(f"ps{i}", [128, 512], F32))[:, :]) for i in range(7)]
        c.rr = 0
        tb = e(nc.psum_tensor("pstr", [128, 8, 128], BF16))
        c.tbk = [Bank(tb[:, i, :]) for i in range(8)]
        c.tbv = tb
        c.trr = 0
        c.tb_marks = {}

        c.XR = e(nc.sbuf_tensor("XR", [128, DCH, S], F32))
        c.MUP = e(nc.sbuf_tensor("MUP", [128, 128], BF16))
        c.MLO = e(nc.sbuf_tensor("MLO", [128, 128], BF16))
        c.ID = e(nc.sbuf_tensor("ID", [128, 128], BF16))
        c.ONES = e(nc.sbuf_tensor("ONESC", [128, 2], BF16))
        c.REB = e(nc.sbuf_tensor("REB", [128, 1, 128], BF16))
        c.RE64 = e(nc.sbuf_tensor("RE64", [128, 64], BF16))
        c.SM = e(nc.sbuf_tensor("SM", [128, 2 * 3 * DCH], F32))
        c.B1C = e(nc.sbuf_tensor("B1C", [128, 2, 2 * NHC], F32))
        c.QS = e(nc.sbuf_tensor("QS", [1, 2 * 2 * DH], BF16))
        c.QSW = e(nc.sbuf_tensor("QSW", [1, 2 * 2 * DH], BF16))
        c.MISC = e(nc.sbuf_tensor("MISC", [1, 1536], F32))
        c.RZ = e(nc.sbuf_tensor("RZ", [128, 2, 512], BF16))
        c.RZC = e(nc.sbuf_tensor("RZC", [128, 512], BF16))
        c.QSB = e(nc.sbuf_tensor("QSB", [128, 4, DH], F32))
        c.RO8 = e(nc.sbuf_tensor("RO8", [1, 128], BF16))
        c.RO1 = e(nc.sbuf_tensor("RO1", [1, 128], BF16))
        c.HLS = e(nc.sbuf_tensor("HLS", [128, 2], BF16))
        c.HLSF = e(nc.sbuf_tensor("HLSF", [128, 1], F32))
        c.MXS = e(nc.sbuf_tensor("MXS", [128, 24], F32))
        c.NMB = e(nc.sbuf_tensor("NMB", [128, 4], BF16))
        c.taps = {}
        c.tab_layer = -1
        c.rz_guard = ()
        c.xbb_guard = ()
        c.bt_guard = ()
        c.hls_guard = ()
        c.stg_guard = ()
        c.pt_guard = [(), (), (), ()]
        c.pt_rr = 0
        c.ol_guard = ()
        c.gl_layer = -1

        _init_consts(c)
        for s in range(NSEQ):
            _load_x(c, s)
            if "ffn1" in stages:
                _ffn_macaron(c, s, 0)
            if "attn" in stages:
                _global_attn(c, s)
            if "ffn2" in stages:
                _ffn_macaron(c, s, 1)
            _store_x(c, s)
        c.R.op("sync", lambda e_: e_.nop(),
               wait=(c.spd.at(), dsem(c, "sto").at()))

        with nc.Block() as block:
            c.R.replay(block)
    return nc


def _init_consts(c):
    R = c.R
    c.w_sm = sync_dma(c, c.SM[:], c.d["smalls"], sem=dsem(c, "ini"))
    c.w_b1c = sync_dma(c, c.B1C[:], c.d["b1c"], sem=dsem(c, "ini"))
    c.w_qs = pool_dma(c, c.QS[:], c.d["qsks"], sem=dsem(c, "inip"))

    init_fns = [
        lambda g: g.memset(c.MUP[:], 0.0),
        lambda g: g.affine_select(out=c.MUP[:], in_=c.MUP[:], compare_op=ALU.is_ge,
                                  fill=NEG, base=0, pattern=[[-1, 128]],
                                  channel_multiplier=1),
        lambda g: g.memset(c.MLO[:], 0.0),
        lambda g: g.affine_select(out=c.MLO[:], in_=c.MLO[:], compare_op=ALU.is_ge,
                                  fill=NEG, base=0, pattern=[[1, 128]],
                                  channel_multiplier=-1),
        lambda g: g.memset(c.ID[:], 0.0),
        lambda g: g.affine_select(out=c.ID[:], in_=c.ID[:], compare_op=ALU.not_equal,
                                  fill=1.0, base=0, pattern=[[-1, 128]],
                                  channel_multiplier=1),
        lambda g: g.memset(c.ONES[:], 1.0),
        lambda g: g.memset(c.REB[:], 0.0),
        lambda g: g.memset(c.REB[0:1, :, :], 1.0),
        lambda g: g.memset(c.REB[32:33, :, :], 1.0),
        lambda g: g.memset(c.RE64[:], 0.0),
        lambda g: g.memset(c.RE64[0:1, :], 1.0),
        lambda g: g.memset(c.RE64[32:33, :], 1.0),
        lambda g: g.memset(c.RZ[:], 0.0),
        lambda g: g.memset(c.RZC[:], 0.0),
        lambda g: g.memset(c.RZC[0:1, :], 1.0),
        lambda g: g.memset(c.RZC[32:33, :], 1.0),
        lambda g: g.memset(c.RO8[:], 8.0),
        lambda g: g.memset(c.RO1[:], 1.0),
    ]
    for fn in init_fns:
        R.op("gpsimd", fn, inc=(c.po.bump(),))
    c.w_init = (c.po.at(),)

    for i in range(4):
        o = i * DH
        R.op("vector",
             lambda v, o=o: v.tensor_scalar_mul(c.QSW[0:1, o:o + 32],
                                                c.QS[0:1, o + 32:o + 64], -1.0),
             wait=c.w_qs, inc=(c.dv.bump(),))
        R.op("vector",
             lambda v, o=o: v.tensor_copy(c.QSW[0:1, o + 32:o + 64],
                                          c.QS[0:1, o:o + 32]),
             wait=c.w_qs, inc=(c.dv.bump(),))
    c.w_qsw = (c.dv.at(),)


def _load_x(c, s):
    w = (dsem(c, "sto").at(),) if s > 0 else ()
    c.w_x = sync_dma(
        c, c.XR[:],
        c.d["xT"].rearrange("(c p) n -> p c n", p=128)[:, :, s * S:(s + 1) * S],
        wait=w, sem=dsem(c, "xld"))


def _store_x(c, s):
    sync_dma(c,
             c.d["outT"].rearrange("(c p) n -> p c n", p=128)[:, :, s * S:(s + 1) * S],
             c.XR[:], wait=(c.dv.at(),), sem=dsem(c, "sto"))


def _stat_block(c, get_src, j, src_wait):
    R = c.R
    mu, rstd, var, tmp, sc2 = (mrow(c, k) for k in ("mu", "rstd", "var", "tmp", "sc2"))
    bs1, bs2 = bank(c, 2)
    w1_ = bs1.begin_fill()
    w2_ = bs2.begin_fill()
    slot_mm = [(), ()]
    for d in range(DCH):
        sl = d % 2
        src = get_src(d, j)
        xb = c.BSCR[:, sl * 512:(sl + 1) * 512]
        sq = c.BSCR[:, 1024 + sl * 512:1024 + (sl + 1) * 512]
        wc = ev_op(c, "gpsimd",
                   lambda g, o=xb, i=src: g.tensor_copy(o, i),
                   waits=tuple(src_wait) + c.bscr_guard + slot_mm[sl])
        ws = ev_op(c, "scalar",
                   lambda sc_, o=sq, i=src: sc_.activation(o, i, AF.Square),
                   waits=tuple(src_wait) + c.bscr_guard + slot_mm[sl])
        inc1 = (c.pe.bump(),)
        R.op("tensor",
             lambda t, o=bs1.t[0:1, :], xb=xb, st=(d == 0), sp=(d == DCH - 1):
                 t.matmul(o, c.ONES[:, 0:1], xb, start=st, stop=sp),
             wait=wc[0:1] + (w1_ if d == 0 else ()), inc=inc1)
        inc2 = (c.pe.bump(),)
        R.op("tensor",
             lambda t, o=bs2.t[0:1, :], sq=sq, st=(d == 0), sp=(d == DCH - 1):
                 t.matmul(o, c.ONES[:, 0:1], sq, start=st, stop=sp),
             wait=ws[0:1] + (w2_ if d == 0 else ()), inc=inc2)
        slot_mm[sl] = (c.pe.at(),)
    bs1.ready = (c.pe.at(),)
    bs2.ready = (c.pe.at(),)
    c.bscr_guard = (c.pe.at(),)
    wser = (c.ac.at(),) + c.rz_guard
    ev_op(c, "vector", lambda v: v.tensor_scalar_mul(mu, bs1.t[0:1, :], 1.0 / D),
          waits=wser, banks=(bs1,))
    ev_op(c, "vector", lambda v: v.tensor_scalar_mul(var, bs2.t[0:1, :], 1.0 / D),
          banks=(bs2,))
    ev_op(c, "vector", lambda v: v.tensor_mul(tmp, mu, mu), hard=True)
    ev_op(c, "vector", lambda v: v.tensor_sub(var, var, tmp), hard=True)
    wv = (c.dv.at(),)
    ev_op(c, "scalar", lambda sc: sc.activation(rstd, var, AF.Ln, bias=np.float32(EPS)),
          waits=wv)
    wact = ev_op(c, "scalar", lambda sc: sc.activation(rstd, rstd, AF.Exp,
                                                       scale=np.float32(-0.5)), hard=True)
    ev_op(c, "vector", lambda v: v.tensor_scalar_mul(mu, mu, -1.0), hard=True)
    ev_op(c, "vector", lambda v: v.tensor_mul(mu, mu, rstd), waits=wact, hard=True)

    rz = c.RZ
    ev_op(c, "vector", lambda v: v.tensor_copy(rz[0:1, 0, :], rstd), hard=True)
    ev_op(c, "vector", lambda v: v.tensor_sub(sc2, rstd, rz[0:1, 0, :]), hard=True)
    ev_op(c, "vector", lambda v: v.tensor_copy(rz[32:33, 0, :], sc2), hard=True)
    ev_op(c, "vector", lambda v: v.tensor_copy(rz[64:65, 0, :], rz[0:1, 0, :]), hard=True)
    ev_op(c, "vector", lambda v: v.tensor_copy(rz[0:1, 1, :], mu), hard=True)
    ev_op(c, "vector", lambda v: v.tensor_sub(sc2, mu, rz[0:1, 1, :]), hard=True)
    ev_op(c, "vector", lambda v: v.tensor_copy(rz[32:33, 1, :], sc2), hard=True)
    rzw = ev_op(c, "vector", lambda v: v.tensor_copy(rz[64:65, 1, :], rz[0:1, 1, :]),
                hard=True)
    if getattr(c, "tap_rz", False):
        c.tap_rz = False
        _tap(c, "rzin", c.RZ[:, :, :], rzw)
        _tap(c, "msc", c.MISC[:, :], rzw)
    stag = getattr(c, "stat_tag", "")
    _tap(c, f"st_misc_{stag}_{j}", c.MISC[:, :], rzw)
    _tap(c, f"st_rz_{stag}_{j}", c.RZ[:, :, :], rzw)
    return rzw


def _ln_apply_block(c, get_src, get_dst, j, rz_wait, glA_fn, glB_fn, src_wait=(),
                    mm_wait=()):
    for d in range(DCH):
        ba, bb = bank(c, 2)
        mm_fill(c, ba, [(glA_fn(d), c.RZ[:, 0, :], tuple(rz_wait) + tuple(mm_wait))])
        gb = glB_fn(d)
        partsB = [(glA_fn(d), c.RZ[:, 1, :], tuple(rz_wait) + tuple(mm_wait))]
        if gb is not None:
            partsB.append((gb, c.RZC[:], c.w_init))
        mm_fill(c, bb, partsB)
        if gb is not None and d == 0:
            _tap_psum(c, "brep", bb)
            _tap_psum(c, "arep", ba)
        src = get_src(d, j)
        lnt = c.LNT
        ev_op(c, "vector",
              lambda v, src=src, ba=ba, lnt=lnt: v.tensor_mul(lnt[:], src, ba.t[:]),
              waits=src_wait, banks=(ba,))
        dst = get_dst(d, j)
        ev_op(c, "vector",
              lambda v, dst=dst, bb=bb, lnt=lnt: v.tensor_add(dst, lnt[:], bb.t[:]),
              banks=(bb,))
    c.rz_guard = (c.pe.at(),)


def _ln(c, get_src, get_dst, src_wait, glA_fn=None, glB_fn=None, blocks=(0, NTB),
        mm_wait=()):
    if glA_fn is None:
        glA_fn = lambda d: c.REB[:, 0, :]
        glB_fn = lambda d: None
        mm_wait = tuple(mm_wait) + c.w_init
    for j in range(blocks[0], blocks[1]):
        rzw = _stat_block(c, get_src, j, src_wait)
        _ln_apply_block(c, get_src, get_dst, j, rzw, glA_fn, glB_fn, src_wait, mm_wait)


def _ffn_macaron(c, s, apply_idx):
    nc = c.nc
    with ExitStack() as ph:
        e = ph.enter_context
        tag = f"{s}_{apply_idx}"
        YR = e(nc.sbuf_tensor(f"YR_{tag}", [128, DCH, S + 1], F32))
        c.LNT = e(nc.sbuf_tensor(f"LNT_{tag}", [128, 512], F32))
        c.BSCR = e(nc.sbuf_tensor(f"BSCR_{tag}", [128, 2048], BF16))
        c.bscr_guard = ()
        c.stat_tag = f"f{tag}"
        wg = ev_op(c, "gpsimd", lambda g: g.memset(YR[:, :, 0:1], 0.0),
                   waits=c.w_init)
        _tap(c, f"xin_{tag}", c.XR[:, :, :], (c.dv.at(),))
        _ln(c,
            get_src=lambda d, j: c.XR[:, d, _ts(j, 512)],
            get_dst=lambda d, j: YR[:, d, 1 + j * 512:1 + (j + 1) * 512],
            src_wait=c.w_x)
        _tap(c, f"yr1_{tag}", YR[:, :, :], (c.dv.at(),))
        for l in range(INNER_LAYERS):
            _inner_layer(c, s, apply_idx, l, YR, wg)
        _tap(c, f"yr2_{tag}", YR[:, :, :], (c.dv.at(),))
        for d in range(DCH):
            for j in range(NTB):
                ev_op(c, "vector",
                      lambda v, d=d, j=j: v.scalar_tensor_tensor(
                          c.XR[:, d, _ts(j, 512)],
                          YR[:, d, 1 + j * 512:1 + (j + 1) * 512],
                          0.5, c.XR[:, d, _ts(j, 512)],
                          op0=ALU.mult, op1=ALU.add),
                      waits=c.w_x)
        fence(c)


def _col_to_rows(c, colap, wcol, dst_list):
    hi = c.HLS[:, 0:1]
    lo = c.HLS[:, 1:2]
    f32s = c.HLSF[:, 0:1]
    ev_op(c, "vector", lambda v: v.tensor_copy(hi, colap),
          waits=tuple(wcol) + c.hls_guard)
    ev_op(c, "vector", lambda v: v.tensor_sub(f32s, colap, hi), hard=True)
    ev_op(c, "vector", lambda v: v.tensor_copy(lo, f32s), hard=True)
    whl = (c.dv.at(),)
    outs = {}
    for name, src in (("hi", hi), ("lo", lo)):
        bb = bank(c)
        w0 = bb.begin_fill()
        c.R.op("tensor",
               lambda t, src=src, bb=bb: t.matmul(bb.t[0:1, 0:128], src, c.ID[:],
                                                  start=True, stop=True),
               wait=whl + w0 + c.w_init, inc=(c.pe.bump(),))
        bb.ready = (c.pe.at(),)
        outs[name] = bb
    for (dst, use_lo) in dst_list:
        bb = outs["lo"] if use_lo else outs["hi"]
        ev_op(c, "vector",
              lambda v, dst=dst, bb=bb: v.tensor_copy(dst, bb.t[0:1, 0:128]),
              waits=tuple(bb.ready))
    outs["hi"].release((c.dv, c.dv.v))
    outs["lo"].release((c.dv, c.dv.v))
    c.hls_guard = (c.dv.at(),)


def _build_tables(c, l):
    QSB = c.QSB
    srcs = [(c.QS, l * 2 + 0, c.RO8), (c.QSW, l * 2 + 0, c.RO8),
            (c.QS, l * 2 + 1, c.RO1), (c.QSW, l * 2 + 1, c.RO1)]
    for i, (rowt, idx, lco) in enumerate(srcs):
        bb = bank(c)
        mm_fill(c, bb, [(bb.t[0:128, 0:64], lco[0:1, :],
                         rowt[0:1, idx * DH:(idx + 1) * DH], c.w_qsw + c.w_init)])
        ev_op(c, "vector",
              lambda v, i=i, bb=bb: v.tensor_copy(QSB[:, i, :], bb.t[:, 0:64]),
              banks=(bb,))
    wq = (c.dv.at(),)
    for t4 in range(4):
        wld = sync_dma(c, c.BT[:],
                       c.d["tabs"][t4].rearrange("(tt p) d -> p tt d", p=128),
                       wait=c.bt_guard, sem=dsem(c, "bt"))
        tab_, bt_ = c.TAB, c.BT
        ev_op(c, "vector",
              lambda v, t4=t4, tab_=tab_, bt_=bt_: v.tensor_mul(
                  tab_[:, t4, :, :], bt_[:, :, :],
                  c.QSB[:, t4, :].unsqueeze(1).broadcast_to((128, S // 128, DH))),
              waits=wld + wq, hard=True)
        c.bt_guard = (c.dv.at(),)
    c.w_tab = (c.dv.at(),)


def _build_gl(c, l):
    glw = (c.pe.at(),)
    gla_, glb_ = c.GLA, c.GLB
    ev_op(c, "gpsimd", lambda g, gla_=gla_: g.memset(gla_[:], 0.0), waits=glw)
    wz = ev_op(c, "gpsimd", lambda g, glb_=glb_: g.memset(glb_[:], 0.0), waits=glw)
    glw = glw + wz
    for d in range(DCH):
        gcol = c.SM[:, l * 24 + d:l * 24 + d + 1]
        bcol = c.SM[:, l * 24 + 8 + d:l * 24 + 8 + d + 1]
        _col_to_rows(c, gcol, c.w_sm + glw,
                     [(c.GLA[0:1, d, :], False), (c.GLA[32:33, d, :], False),
                      (c.GLA[64:65, d, :], True)])
        _col_to_rows(c, bcol, c.w_sm + glw,
                     [(c.GLB[0:1, d, :], False), (c.GLB[32:33, d, :], True)])
    c.w_gl = (c.dv.at(),)
    _tap(c, "gla", c.GLA[:, :, :], c.w_gl)
    _tap(c, "glb", c.GLB[:, :, :], c.w_gl)


def _inner_layer(c, s, apply_idx, l, YR, guard_wait):
    nc = c.nc
    R = c.R
    tag = f"{s}_{apply_idx}_{l}"
    with ExitStack() as ph:
        e = ph.enter_context
        c.TAB = e(nc.sbuf_tensor(f"TAB_{tag}", [128, 4, S // 128, DH], BF16))
        c.PT = e(nc.sbuf_tensor(f"PT_{tag}", [128, 4, 512], BF16))
        c.pt_guard = [(), (), (), ()]
        c.STG = e(nc.sbuf_tensor(f"STG_{tag}", [128, 4, 2, 128], BF16))
        with ExitStack() as bts:
            c.BT = bts.enter_context(
                nc.sbuf_tensor(f"BT_{tag}", [128, S // 128, DH], F32))
            _build_tables(c, l)
            fence(c)
        vsb = e(nc.sbuf_tensor(f"vsb_{tag}", [128, 8, HL, 65], BF16))
        qw = e(nc.sbuf_tensor(f"qw_{tag}", [128, 4, 512], BF16))
        kw = e(nc.sbuf_tensor(f"kw_{tag}", [128, 2, 4, 512], BF16))
        qc2 = e(nc.sbuf_tensor(f"qc2_{tag}", [128, 2, 512], BF16))
        kc2 = e(nc.sbuf_tensor(f"kc2_{tag}", [128, 2, 1024], BF16))
        o_l = e(nc.sbuf_tensor(f"ol_{tag}", [128, 4, 512], BF16))
        HT = e(nc.sbuf_tensor(f"HT_{tag}", [128, 2, DCH, 128], BF16))
        SQF = c.BSCR[:, 0:1024]
        QRS = e(nc.sbuf_tensor(f"QRS_{tag}", [128, 16], F32))
        OSC = e(nc.sbuf_tensor(f"OSC_{tag}", [128, 512], BF16))
        wqs = e(nc.sbuf_tensor(f"wqs_{tag}", [128, 2, QKV], BF16))
        wos = e(nc.sbuf_tensor(f"wos_{tag}", [128, 1, 4, 256], BF16))
        def initcur(g):
            g.memset(qc2[:], 0.0)
            g.memset(kc2[:], 0.0)
            g.memset(kc2[64:65, :, :], 1.0)
            g.memset(vsb[:], 0.0)
            return g.memset(vsb[:, :, :, 64:65], 1.0)
        wcur = ev_op(c, "gpsimd", initcur, waits=c.w_init)
        c.vslot_mark = {}
        c.wos_mark = [(c.pe.at(),)]
        c.sqf_guard = ()
        wqs_mark = [(c.pe.at(),), (c.pe.at(),)]
        ht_mark = [(), ()]
        vse = {}
        qwm, kwm = {}, {}
        for w in range(4 if not SKIP_INNER_ATTN else 0):
            wlay = (c.dv.at(),)
            for pr in range(2):
                tiles = (2 * pr, 2 * pr + 1)
                hbuf = pr % 2
                for wt in tiles:
                    T = w * 4 + wt
                    for d in range(DCH):
                        off = 1 if d < 4 else 0
                        src = YR[:, d, off + T * 128: off + T * 128 + 128]
                        eng_h = "vector" if DIAG_HCAST_DVE else "gpsimd"
                        ev_op(c, eng_h,
                              lambda g, o=HT[:, wt % 2, d, :], i=src: g.tensor_copy(o, i),
                              waits=wlay + guard_wait + ht_mark[wt % 2])
                wh = (c.po.at(),) + ((c.dv.at(),) if DIAG_HCAST_DVE else ())
                if DIAG_FENCE_QKV:
                    fence(c)
                if w == 0 and pr == 0:
                    _tap(c, "ht0", HT[:, :, :, :], wh)
                pb = {}
                w0s = {}
                for wt in tiles:
                    pb[wt] = bank(c, 3)
                    for part in range(3):
                        w0s[(wt, part)] = pb[wt][part].begin_fill()
                for d in range(DCH):
                    buf = d % 2
                    wld = pool_dma(c, wqs[:, buf, :], c.d["wqkv"][l, _ts(d), :],
                                   wait=wqs_mark[buf], sem=dsem(c, f"wqs{buf}"))
                    for wi, wt in enumerate(tiles):
                        for part in range(3):
                            bk_ = pb[wt][part]
                            last = (d == DCH - 1)
                            round_last = (wi == 1 and part == 2)
                            inc = (c.pe.bump(),) if (last or round_last) else ()
                            R.op("tensor",
                                 lambda t, o=bk_.t, l_=HT[:, wt % 2, d, :],
                                        r=wqs[:, buf, _ts(part, 512)],
                                        st=(d == 0), sp=last:
                                     t.matmul(o, l_, r, start=st, stop=sp),
                                 wait=wld + wh +
                                      (w0s[(wt, part)] if d == 0 else ()),
                                 inc=inc)
                            if last:
                                bk_.ready = (c.pe.at(),)
                    wqs_mark[buf] = (c.pe.at(),)
                for wt in tiles:
                    ht_mark[wt % 2] = (c.pe.at(),)
                if DIAG_FENCE_QKV:
                    fence(c)
                for wt in tiles:
                    T = w * 4 + wt
                    bq, bkk, bv = pb[wt]
                    slot = T % 8
                    if w == 0 and wt == 1:
                        _tap_psum(c, "bvps", bv)
                    vse[slot] = ev_op(
                        c, "vector",
                        lambda v, bv=bv, slot=slot: v.tensor_copy(
                            vsb[:, slot, :, 0:64],
                            bv.t[:].rearrange("p (h n) -> p h n", h=HL)),
                        waits=tuple(wcur) + c.vslot_mark.get(slot, ()),
                        banks=(bv,))
                    ev_op(c, "scalar",
                          lambda sc, bq=bq: sc.activation(SQF[:, 0:512], bq.t[:], AF.Square),
                          waits=tuple(bq.ready) + c.sqf_guard + c.bscr_guard)
                    ev_op(c, "scalar",
                          lambda sc, bkk=bkk: sc.activation(SQF[:, 512:1024], bkk.t[:], AF.Square),
                          waits=tuple(bkk.ready))
                    wsq = (c.ac.at(),)
                    ev_op(c, "vector",
                          lambda v: v.reduce_sum(
                              QRS[:, :].rearrange("p (a b) -> p a b", a=16),
                              SQF[:, :].rearrange("p (a b) -> p a b", a=16),
                              axis=AX.X),
                          waits=wsq)
                    c.sqf_guard = (c.dv.at(),)
                    c.bscr_guard = c.bscr_guard + (c.dv.at(),)
                    ev_op(c, "vector",
                          lambda v: v.tensor_scalar_max(QRS[:, :], QRS[:, :], 1e-24),
                          hard=True)
                    wv_ = (c.dv.at(),)
                    ev_op(c, "scalar",
                          lambda sc: sc.activation(QRS[:, :], QRS[:, :], AF.Ln),
                          waits=wv_)
                    ev_op(c, "scalar",
                          lambda sc: sc.activation(QRS[:, :], QRS[:, :], AF.Exp,
                                                   scale=np.float32(-0.5)), hard=True)
                    wrs = (c.ac.at(),)
                    for hp in range(4):
                        for hh in range(2):
                            h = 2 * hp + hh
                            for (qk, bsrc, tA, tV, rcol) in (
                                    (0, bq, 0, 1, h), (1, bkk, 2, 3, 8 + h)):
                                ps = bsrc.t
                                base = h * DH
                                rstd = QRS[:, rcol:rcol + 1]
                                dst = c.STG[:, hp, qk, :]
                                dsl = dst[:, hh * DH:(hh + 1) * DH]
                                t1 = c.LNT[:, 0:64]
                                t2 = c.LNT[:, 64:128]
                                tab_ = c.TAB
                                ev_op(c, "vector",
                                      lambda v, ps=ps, base=base, rstd=rstd, tA=tA,
                                             T=T, t1=t1, tab_=tab_:
                                      v.scalar_tensor_tensor(
                                          t1, ps[:, base:base + DH], rstd,
                                          tab_[:, tA, T, :],
                                          op0=ALU.mult, op1=ALU.mult),
                                      waits=wrs + c.w_tab + c.stg_guard)
                                ev_op(c, "vector",
                                      lambda v, ps=ps, base=base, rstd=rstd, tV=tV,
                                             T=T, t2=t2, tab_=tab_:
                                      v.scalar_tensor_tensor(
                                          t2[:, 0:32], ps[:, base + 32:base + 64], rstd,
                                          tab_[:, tV, T, 0:32],
                                          op0=ALU.mult, op1=ALU.mult))
                                ev_op(c, "vector",
                                      lambda v, ps=ps, base=base, rstd=rstd, tV=tV,
                                             T=T, t2=t2, tab_=tab_:
                                      v.scalar_tensor_tensor(
                                          t2[:, 32:64], ps[:, base:base + 32], rstd,
                                          tab_[:, tV, T, 32:64],
                                          op0=ALU.mult, op1=ALU.mult))
                                ev_op(c, "vector",
                                      lambda v, dsl=dsl, t1=t1, t2=t2:
                                          v.tensor_add(dsl, t1, t2),
                                      hard=True)
                    wstg = (c.dv.at(),)
                    bq.release((c.dv, c.dv.v))
                    bkk.release((c.dv, c.dv.v))
                    wtb = tb_wait(c)
                    for hp in range(4):
                        for qk in range(2):
                            tb = c.tbk[2 * hp + qk]
                            stg_ = c.STG
                            R.op("tensor",
                                 lambda t, tb=tb, hp=hp, qk=qk, stg_=stg_:
                                     t.transpose(tb.t[:], stg_[:, hp, qk, :], c.ID[:]),
                                 wait=wstg + wtb + c.w_init,
                                 inc=(c.pe.bump(),) if (hp == 3 and qk == 1) else ())
                    wtr = (c.pe.at(),)
                    c.stg_guard = (c.pe.at(),)
                    tbq = c.tbv[:, :, :].rearrange("p (a b) n -> p a b n", b=2)
                    ev_op(c, "vector",
                          lambda v, wt=wt: v.tensor_copy(qw[:, :, wt * 128:(wt + 1) * 128],
                                                         tbq[:, :, 0, :]),
                          waits=wtr + qwm.get(0, ()))
                    ev_op(c, "vector",
                          lambda v, wt=wt, w=w: v.tensor_copy(kw[:, w % 2, :, wt * 128:(wt + 1) * 128],
                                                              tbq[:, :, 1, :]),
                          waits=kwm.get(w % 2, ()))
                    tb_mark(c, (c.dv, c.dv.v))
            wqk = (c.dv.at(), c.ac.at())
            if w == 0:
                _tap(c, "qw0", qw[:, :, :], wqk)
                _tap(c, "kw0", kw[:, :, :, :], wqk)
                _tap(c, "vsb0", vsb[:, :, :, :], wqk)
            cur_mark = [(), ()]
            for hp in range(4):
                for hh in range(2):
                    h = 2 * hp + hh
                    cb = h % 2
                    wcp1 = ev_op(c, "gpsimd",
                                 lambda g, cb=cb, hp=hp, hh=hh: g.tensor_copy(
                                     qc2[0:64, cb, :], qw[_ts(hh, 64), hp, :]),
                                 waits=wqk + cur_mark[cb] + wcur)
                    ev_op(c, "gpsimd",
                          lambda g, cb=cb: g.memset(qc2[64:65, cb, :], 0.0))
                    if w > 0:
                        ev_op(c, "gpsimd",
                              lambda g, cb=cb, hp=hp, hh=hh, w=w: g.tensor_copy(
                                  kc2[0:64, cb, 0:512],
                                  kw[_ts(hh, 64), (w - 1) % 2, hp, :]))
                    wcp = ev_op(c, "gpsimd",
                                lambda g, cb=cb, hp=hp, hh=hh, w=w: g.tensor_copy(
                                    kc2[0:64, cb, 512:1024], kw[_ts(hh, 64), w % 2, hp, :]))
                    qwm[0] = (c.po.at(),)
                    kwm[w % 2] = (c.po.at(),)
                    kwm[(w - 1) % 2] = (c.po.at(),)
                    MX = c.MXS
                    nstW = 2 if w > 0 else 1
                    for qc in range(4):
                        if w > 0:
                            strips = [(qc * 128, 512), (qc * 128 + 512, 128)]
                        else:
                            strips = [(512, (qc + 1) * 128)]
                        sb_ = []
                        for (k0, ln_) in strips:
                            bb = bank(c)
                            mm_fill(c, bb, [(bb.t[:, 0:ln_],
                                             qc2[:, cb, _ts(qc)],
                                             kc2[:, cb, k0:k0 + ln_], wcp)])
                            sb_.append((bb, ln_))
                        if w > 0:
                            b0 = sb_[0][0]
                            ev_op(c, "vector",
                                  lambda v, bb=b0: v.tensor_add(
                                      bb.t[:, 0:128], bb.t[:, 0:128], c.MLO[:]),
                                  waits=c.w_init + tuple(b0.ready))
                            b0.ready = b0.ready + (c.dv.at(),)
                            bcz, lcz = sb_[1]
                            ev_op(c, "vector",
                                  lambda v, bcz=bcz: v.tensor_add(
                                      bcz.t[:, 0:128], bcz.t[:, 0:128], c.MUP[:]),
                                  waits=c.w_init + tuple(bcz.ready))
                            bcz.ready = bcz.ready + (c.dv.at(),)
                        else:
                            bcz, lcz = sb_[0]
                            ev_op(c, "vector",
                                  lambda v, bcz=bcz, lcz=lcz: v.tensor_add(
                                      bcz.t[:, lcz - 128:lcz], bcz.t[:, lcz - 128:lcz],
                                      c.MUP[:]),
                                  waits=c.w_init + tuple(bcz.ready))
                            bcz.ready = bcz.ready + (c.dv.at(),)
                        for i, (bb, ln_) in enumerate(sb_):
                            ev_op(c, "vector",
                                  lambda v, bb=bb, ln_=ln_, i=i, qc=qc: v.reduce_max(
                                      MX[:, qc * nstW + i:qc * nstW + i + 1],
                                      bb.t[:, 0:ln_], axis=AX.X),
                                  banks=(bb,))
                    if w > 0:
                        mxv = MX[:, 0:8].rearrange("p (q s) -> p q s", s=2)
                        ev_op(c, "vector",
                              lambda v, mxv=mxv: v.tensor_max(
                                  MX[:, 16:20], mxv[:, :, 0], mxv[:, :, 1]),
                              hard=True)
                        src16 = MX[:, 16:20]
                    else:
                        src16 = MX[:, 0:4]
                    ev_op(c, "vector",
                          lambda v, src16=src16: v.tensor_scalar_mul(
                              c.NMB[:, 0:4], src16, -1.0),
                          hard=True)
                    wnm = (c.dv.at(),)
                    tbm = bank(c)
                    w0m = tbm.begin_fill()
                    for qc in range(4):
                        R.op("tensor",
                             lambda t, tbm=tbm, qc=qc: t.matmul(
                                 tbm.t[0:1, _ts(qc)], c.NMB[:, qc:qc + 1],
                                 c.ID[:], start=True, stop=True),
                             wait=wnm + (w0m if qc == 0 else ()) + c.w_init,
                             inc=(c.pe.bump(),) if qc == 3 else ())
                    tbm.ready = (c.pe.at(),)
                    ev_op(c, "vector",
                          lambda v, tbm=tbm, cb=cb: v.tensor_copy(
                              qc2[64:65, cb, 0:512], tbm.t[0:1, 0:512]),
                          banks=(tbm,))
                    wmax = (c.dv.at(),)
                    if w == 0 and h == 0:
                        _tap(c, "qc0", qc2[:, cb, :], wmax)
                        _tap(c, "kc0", kc2[:, cb, :], wmax)
                    if w == 1 and h == 0:
                        _tap(c, "qc1", qc2[:, cb, :], wmax)
                        _tap(c, "kc1", kc2[:, cb, :], wmax)
                    ob = bank(c)
                    first = True
                    tlist = list(range(8)) if w > 0 else list(range(4, 8))
                    tlist = [t_ for t_ in tlist if t_ not in DIAG_SKIP_T]
                    tlist.sort(key=lambda t_: (t_ != 4,))
                    for t_ in tlist:
                        qlo = max(0, (t_ - 4) * 128)
                        qhi = min(512, t_ * 128 + 128)
                        bst = bank(c)
                        mm_fill(c, bst, [(bst.t[:, 0:qhi - qlo],
                                          kc2[:, cb, _ts(t_)],
                                          qc2[:, cb, qlo:qhi], wmax)])
                        if t_ >= 4 and t_ - 4 <= 3 and qlo == (t_ - 4) * 128:
                            ev_op(c, "vector",
                                  lambda v, bst=bst: v.tensor_add(
                                      bst.t[:, 0:128], bst.t[:, 0:128], c.MLO[:]),
                                  waits=c.w_init + tuple(bst.ready))
                            bst.ready = bst.ready + (c.dv.at(),)
                        if w > 0 and t_ <= 3:
                            off = t_ * 128 - qlo
                            ev_op(c, "vector",
                                  lambda v, bst=bst, off=off: v.tensor_add(
                                      bst.t[:, off:off + 128], bst.t[:, off:off + 128],
                                      c.MUP[:]),
                                  waits=c.w_init + tuple(bst.ready))
                            bst.ready = bst.ready + (c.dv.at(),)
                        ptslot = c.pt_rr % 4
                        c.pt_rr += 1
                        pts = c.PT[:, ptslot, 0:qhi - qlo]
                        wex = ev_op(c, "scalar",
                                    lambda sc, pts=pts, bst=bst, n=qhi - qlo:
                                        sc.activation(pts, bst.t[:, 0:n], AF.Exp),
                                    waits=c.pt_guard[ptslot], banks=(bst,))
                        slot = ((w - 1) * 4 + t_) % 8
                        w0 = ob.begin_fill() if first else ()
                        R.op("tensor",
                             lambda t, ob=ob, qlo=qlo, qhi=qhi, slot=slot, pts=pts,
                                    st=first, h=h:
                                 t.matmul(ob.t[0:65, qlo:qhi], vsb[:, slot, h, :],
                                          pts, start=st, stop=True),
                             wait=wex + tuple(vse.get(slot, ())) + tuple(w0),
                             inc=(c.pe.bump(),))
                        ob.ready = (c.pe.at(),)
                        c.pt_guard[ptslot] = (c.pe.at(),)
                        c.vslot_mark[slot] = (c.pe.at(),)
                        if w == 0 and h == 0:
                            _tap(c, f"pt{t_}", pts, wex)
                        if w == 1 and h == 0:
                            _tap(c, f"pu{t_}", pts, wex)
                        first = False
                    if w == 0 and h == 0:
                        _tap_psum(c, "obps", ob)
                    if w == 1 and h == 0:
                        _tap_psum(c, "obp1", ob)
                    rr_row = mrow(c, "rr")
                    ev_op(c, "vector",
                          lambda v, ob=ob: v.reciprocal(rr_row, ob.t[64:65, 0:512]),
                          waits=tuple(ob.ready))
                    ev_op(c, "vector",
                          lambda v: v.tensor_copy(c.RZ[0:1, 0, :], rr_row),
                          waits=c.rz_guard, hard=True)
                    ev_op(c, "vector",
                          lambda v: v.tensor_sub(mrow(c, "sc2"), rr_row, c.RZ[0:1, 0, :]),
                          hard=True)
                    ev_op(c, "vector",
                          lambda v: v.tensor_copy(c.RZ[32:33, 0, :], mrow(c, "sc2")),
                          hard=True)
                    wrr = (c.dv.at(),)
                    br = bank(c)
                    mm_fill(c, br, [(br.t[0:64, :], c.RE64[:], c.RZ[:, 0, :],
                                     tuple(wrr) + c.w_init)])
                    c.rz_guard = (c.pe.at(),)
                    ev_op(c, "scalar",
                          lambda sc, ob=ob: sc.copy(OSC[0:64, :], ob.t[0:64, 0:512]),
                          waits=wrr, banks=(ob,))
                    wos_ = (c.ac.at(),)
                    dsto = o_l[_ts(hh, 64), hp, :]
                    ev_op(c, "vector",
                          lambda v, dsto=dsto, br=br: v.tensor_mul(
                              dsto, OSC[0:64, :], br.t[0:64, :]),
                          waits=wos_ + c.ol_guard, banks=(br,))
                    cur_mark[cb] = (c.pe.at(),) + (c.dv.at(),)
            wo_w = (c.dv.at(),)
            if w == 0:
                _tap(c, "ol0", o_l[:, :, :], wo_w)
            if w == 1:
                _tap(c, "ol1", o_l[:, :, :], wo_w)
            for dp in range(4):
                buf = 0
                wld = pool_dma(c, wos[:, buf, :, :],
                               c.d["wol"][l, :, dp * 256:(dp + 1) * 256]
                               .rearrange("(c p) n -> p c n", p=128),
                               wait=c.wos_mark[buf], sem=dsem(c, "wos"))
                for dd in range(2):
                    dout = 2 * dp + dd
                    bb = bank(c)
                    parts = []
                    for ic in range(4):
                        parts.append((wos[:, buf, ic, dd * 128:(dd + 1) * 128],
                                      o_l[:, ic, :], wld + wo_w))
                    mm_fill(c, bb, parts)
                    c.wos_mark[buf] = (c.pe.at(),)
                    ev_op(c, "vector",
                          lambda v, bb=bb, dout=dout, w=w: v.tensor_add(
                              YR[:, dout, 1 + w * 512:1 + (w + 1) * 512],
                              YR[:, dout, 1 + w * 512:1 + (w + 1) * 512],
                              bb.t[:]),
                          banks=(bb,))
            c.ol_guard = (c.pe.at(),)
        fence(c)
    if not SKIP_INNER_FFN:
        _inner_ffn(c, s, apply_idx, l, YR)


def _inner_ffn(c, s, apply_idx, l, YR):
    nc = c.nc
    R = c.R
    tag = f"f{s}_{apply_idx}_{l}"
    with ExitStack() as fs:
        e = fs.enter_context
        c.GLA = e(nc.sbuf_tensor(f"GLA_{tag}", [128, DCH, 128], BF16))
        c.GLB = e(nc.sbuf_tensor(f"GLB_{tag}", [128, DCH, 128], BF16))
        HN = e(nc.sbuf_tensor(f"HN_{tag}", [128, DCH, 2, 512], BF16))
        U = e(nc.sbuf_tensor(f"U_{tag}", [128, 11, 2, 512], BF16))
        GG = e(nc.sbuf_tensor(f"GG_{tag}", [128, 2, 512], F32))
        w1s = e(nc.sbuf_tensor(f"w1s_{tag}", [128, 2, 2, DCH, 128], BF16))
        w2s = e(nc.sbuf_tensor(f"w2s_{tag}", [128, 2, 11, 128], BF16))
        SVP = e(nc.sbuf_tensor(f"SVP_{tag}", [128, DCH, 2], F32))
        _build_gl(c, l)

        def srcf(d, j):
            off = 1 if d < 4 else 0
            return YR[:, d, off + j * 512: off + j * 512 + 512]

        BC = S // 2
        wpre = ev_op(c, "gpsimd",
                     lambda g: g.tensor_copy(SVP[:, :, 0:1], YR[:, :, BC:BC + 1]),
                     waits=(c.dv.at(),))
        w1_mark = [(c.pe.at(),), (c.pe.at(),)]
        w2_mark = [(c.pe.at(),), (c.pe.at(),)]
        c.gg_guard = [(), ()]
        c.u_guard = ()
        hn_guard = ()
        for half in range(2):
            wy = (c.dv.at(),)
            if half == 1:
                ev_op(c, "gpsimd",
                      lambda g: g.tensor_copy(SVP[:, :, 1:2], YR[:, :, BC:BC + 1]),
                      waits=wy)
                ev_op(c, "gpsimd",
                      lambda g: g.tensor_copy(YR[:, :, BC:BC + 1], SVP[:, :, 0:1]))
                wy = wy + (c.po.at(),)
            c.tap_rz = True
            _ln(c, get_src=srcf,
                get_dst=lambda d, j: HN[:, d, j % 2, :],
                src_wait=wy + hn_guard,
                glA_fn=lambda d: c.GLA[:, d, :],
                glB_fn=lambda d: c.GLB[:, d, :],
                blocks=(2 * half, 2 * half + 2),
                mm_wait=c.w_gl)
            whn = (c.dv.at(),)
            if half == 0:
                _tap(c, "hn", HN[:, :, :, :], whn)
            else:
                _tap(c, f"hn2_{tag}", HN[:, :, :, :], whn)
                ev_op(c, "gpsimd",
                      lambda g: g.tensor_copy(YR[:, :, BC:BC + 1], SVP[:, :, 1:2]),
                      waits=whn + (c.ac.at(),))
            for grp in range(2):
                for i in range(11):
                    hid = grp * 11 + i
                    buf = i % 2
                    wld = pool_dma(
                        c, w1s[:, buf, 0, :, :],
                        c.d["w1"][l, :, _ts(hid)].rearrange("(cc p) n -> p cc n", p=128),
                        wait=w1_mark[buf], sem=dsem(c, f"w1s{buf}"))
                    wld2 = pool_dma(
                        c, w1s[:, buf, 1, :, :],
                        c.d["w1"][l, :, _ts(NHC + hid)].rearrange("(cc p) n -> p cc n", p=128),
                        sem=dsem(c, f"w1s{buf}"))
                    bks = {}
                    for ag in range(2):
                        for ttl in range(2):
                            tt = 2 * half + ttl
                            bb = bank(c)
                            parts = []
                            for d in range(DCH):
                                parts.append((w1s[:, buf, ag, d, :],
                                              HN[:, d, ttl, :],
                                              (wld + wld2 + whn) if d == 0 else ()))
                            mm_fill(c, bb, parts)
                            bks[(ag, ttl)] = bb
                    w1_mark[buf] = (c.pe.at(),)
                    for ttl in range(2):
                        ba, bg = bks[(0, ttl)], bks[(1, ttl)]
                        gg = GG[:, ttl, :]
                        ev_op(c, "scalar",
                              lambda sc, bg=bg, gg=gg, hid=hid: sc.activation(
                                  gg, bg.t[:], AF.Gelu,
                                  bias=c.B1C[:, l, NHC + hid:NHC + hid + 1]),
                              waits=c.w_b1c + c.gg_guard[ttl], banks=(bg,))
                        wgg = (c.ac.at(),)
                        ev_op(c, "vector",
                              lambda v, ba=ba, gg=gg, hid=hid, i=i, ttl=ttl:
                              v.scalar_tensor_tensor(
                                  U[:, i, ttl, :], ba.t[:],
                                  c.B1C[:, l, hid:hid + 1], gg,
                                  op0=ALU.add, op1=ALU.mult),
                              waits=wgg + c.w_b1c + c.u_guard, banks=(ba,))
                        c.gg_guard[ttl] = (c.dv.at(),)
                wu = (c.dv.at(),)
                for dc in range(DCH):
                    buf = dc % 2
                    wld = pool_dma(
                        c, w2s[:, buf, :, :],
                        c.d["w2"][l, grp * 1408:(grp + 1) * 1408, _ts(dc)]
                        .rearrange("(hh p) n -> p hh n", p=128),
                        wait=w2_mark[buf], sem=dsem(c, f"w2s{buf}"))
                    for ttl in range(2):
                        tt = 2 * half + ttl
                        bb = bank(c)
                        parts = []
                        for i in range(11):
                            parts.append((w2s[:, buf, i, :], U[:, i, ttl, :],
                                          (wld + wu) if i == 0 else ()))
                        mm_fill(c, bb, parts)
                        w2_mark[buf] = (c.pe.at(),)
                        wbnd = wpre if half == 0 else ()
                        if grp == 0:
                            ev_op(c, "vector",
                                  lambda v, bb=bb, dc=dc, tt=tt: v.tensor_add(
                                      YR[:, dc, 1 + tt * 512:1 + (tt + 1) * 512],
                                      YR[:, dc, 1 + tt * 512:1 + (tt + 1) * 512],
                                      bb.t[:]),
                                  waits=wbnd, banks=(bb,))
                        else:
                            ev_op(c, "vector",
                                  lambda v, bb=bb, dc=dc, tt=tt: v.scalar_tensor_tensor(
                                      YR[:, dc, 1 + tt * 512:1 + (tt + 1) * 512],
                                      bb.t[:],
                                      c.SM[:, l * 24 + 16 + dc:l * 24 + 16 + dc + 1],
                                      YR[:, dc, 1 + tt * 512:1 + (tt + 1) * 512],
                                      op0=ALU.add, op1=ALU.add),
                                  waits=c.w_sm + wbnd, banks=(bb,))
                if half == 0 and grp == 0:
                    _tap(c, "u", U[:, :, :, :], (c.dv.at(),))
                c.u_guard = (c.pe.at(),)
            hn_guard = (c.pe.at(),)
        fence(c)


def _global_attn(c, s):
    nc = c.nc
    R = c.R
    tag = f"g{s}"
    with ExitStack() as ph:
        e = ph.enter_context
        kfm = e(nc.sbuf_tensor(f"kfm_{tag}", [128, DCH, S], BF16))
        vsg = e(nc.sbuf_tensor(f"vsg_{tag}", [128, 16, HG, 65], BF16))
        qfm = e(nc.sbuf_tensor(f"qfm_{tag}", [128, DCH, 512], BF16))
        ybf = e(nc.sbuf_tensor(f"ybf_{tag}", [128, DCH, 512], BF16))
        kcur = e(nc.sbuf_tensor(f"kcur_{tag}", [128, 2, S], BF16))
        qcur = e(nc.sbuf_tensor(f"qcur_{tag}", [128, 2, 512], BF16))
        og = e(nc.sbuf_tensor(f"og_{tag}", [128, DCH, 512], BF16))
        wkq = e(nc.sbuf_tensor(f"wkq_{tag}", [128, 2, DCH, 128], BF16))
        wvs = e(nc.sbuf_tensor(f"wvs_{tag}", [128, 2, 1024], BF16))
        wos = e(nc.sbuf_tensor(f"wos_{tag}", [128, 2, DCH, 128], BF16))
        OSC = e(nc.sbuf_tensor(f"gOSC_{tag}", [128, 512], BF16))
        c.LNT = e(nc.sbuf_tensor(f"gLNT_{tag}", [128, 512], F32))
        c.BSCR = e(nc.sbuf_tensor(f"gBSCR_{tag}", [128, 2048], BF16))
        c.stat_tag = f"g{tag}"
        c.PT = e(nc.sbuf_tensor(f"gPT_{tag}", [128, 4, 512], BF16))
        c.pt_guard = [(), (), (), ()]
        c.bscr_guard = ()
        c.pt_rr = 0
        MX = c.MXS

        for fn in (lambda g: g.memset(kcur[:], 0.0),
                   lambda g: g.memset(kcur[64:65, :, :], 1.0),
                   lambda g: g.memset(qcur[:], 0.0),
                   lambda g: g.memset(vsg[:], 0.0),
                   lambda g: g.memset(vsg[:, :, :, 64:65], 1.0)):
            wcur = ev_op(c, "gpsimd", fn, waits=c.w_init)
        _tap_async(c, f"avA_{tag}", vsg[:, 15, :, :], wcur)
        def canary(lbl):
            _tap(c, f"cn_{lbl}_{tag}", vsg[:, 15, 12:16, 63:65], (c.po.at(), c.dv.at(), c.pe.at()))
        canary("T0")
        c.vslot_mark = {}
        c.vsg_w = {}
        wkq_mark = [(c.pe.at(),), (c.pe.at(),)]
        wvs_mark = [(c.pe.at(),), (c.pe.at(),)]
        wos_mark = [(c.pe.at(),), (c.pe.at(),)]
        kfm_mark = ()
        qfm_mark = ()
        ybf_mark = ()
        ogm = ()
        cur_mark = [(), ()]

        for qb in range(NTB):
            tb = qb
            _ln(c, get_src=lambda d, j: c.XR[:, d, _ts(j, 512)],
                get_dst=lambda d, j: ybf[:, d, :],
                src_wait=c.w_x + ybf_mark, blocks=(tb, tb + 1))
            wyb = (c.dv.at(),)
            canary(f"ln{qb}")
            for wname, dstbuf, dmark, scl in (("wk", kfm, kfm_mark, None),
                                              ("wq", qfm, qfm_mark, 0.125)):
                for cc in range(DCH):
                    buf = cc % 2
                    wld = pool_dma(
                        c, wkq[:, buf, :, :],
                        c.d[wname][:, _ts(cc)].rearrange("(dd p) n -> p dd n", p=128),
                        wait=wkq_mark[buf], sem=dsem(c, f"gkq{buf}"))
                    bb = bank(c)
                    parts = []
                    for d in range(DCH):
                        parts.append((wkq[:, buf, d, :], ybf[:, d, :],
                                      (wld + wyb) if d == 0 else ()))
                    mm_fill(c, bb, parts)
                    wkq_mark[buf] = (c.pe.at(),)
                    if scl is None:
                        ev_op(c, "vector",
                              lambda v, bb=bb, cc=cc, tb=tb, dstbuf=dstbuf:
                                  v.tensor_copy(dstbuf[:, cc, _ts(tb, 512)], bb.t[:]),
                              banks=(bb,))
                    else:
                        ev_op(c, "vector",
                              lambda v, bb=bb, cc=cc, dstbuf=dstbuf:
                                  v.tensor_scalar_mul(dstbuf[:, cc, :], bb.t[:],
                                                      0.125),
                              banks=(bb,))
            wkq_done = (c.dv.at(),)
            canary(f"kq{qb}")
            if qb == 3:
                _tap(c, f"gqf3_{tag}", qfm[:, :, :], wkq_done)
                _tap_async(c, f"agqf3_{tag}", qfm[:, :, :], wkq_done)
                _tap_async(c, f"agkf3_{tag}", kfm[:, :, :], wkq_done)
            for pr4 in range(2):
                tts = (2 * pr4, 2 * pr4 + 1)
                pbv = {}
                w0v = {}
                for tt4 in tts:
                    pbv[tt4] = bank(c, 2)
                    for half in range(2):
                        w0v[(tt4, half)] = pbv[tt4][half].begin_fill()
                for d in range(DCH):
                    buf = d % 2
                    wld = pool_dma(c, wvs[:, buf, :], c.d["wv"][_ts(d), :],
                                   wait=wvs_mark[buf], sem=dsem(c, f"gwv{buf}"))
                    for tt4 in tts:
                        for half in range(2):
                            bk_ = pbv[tt4][half]
                            last = (d == DCH - 1)
                            rlast = (tt4 == tts[1] and half == 1)
                            R.op("tensor",
                                 lambda t, o=bk_.t, tt4=tt4, half=half, d=d, buf=buf:
                                     t.matmul(o, ybf[:, d, _ts(tt4)],
                                              wvs[:, buf, _ts(half, 512)],
                                              start=(d == 0), stop=(d == DCH - 1)),
                                 wait=wld + wyb + (w0v[(tt4, half)] if d == 0 else ()),
                                 inc=(c.pe.bump(),) if (last or rlast) else ())
                            if last:
                                bk_.ready = (c.pe.at(),)
                    wvs_mark[buf] = (c.pe.at(),)
                for tt4 in tts:
                    slot = tb * 4 + tt4
                    for half in range(2):
                        bk_ = pbv[tt4][half]
                        ev_op(c, "vector",
                              lambda v, bk_=bk_, slot=slot, half=half: v.tensor_copy(
                                  vsg[:, slot, half * 8:(half + 1) * 8, 0:64],
                                  bk_.t[:].rearrange("p (h n) -> p h n", h=8)),
                              waits=tuple(wcur) + c.vslot_mark.get((slot, half), ()),
                              banks=(bk_,))
                        c.vsg_w[(slot, half)] = (c.dv.at(),)
            ybf_mark = (c.pe.at(),)
            wv_done = (c.dv.at(),)
            canary(f"v{qb}")
            if qb == 3:
                _tap_async(c, f"avB_{tag}", vsg[:, 15, :, :], wv_done)
            nkt = 4 * qb + 4
            for h in range(HG):
                cb = h % 2
                wc1 = ev_op(c, "gpsimd",
                            lambda g, h=h, cb=cb: g.tensor_copy(
                                qcur[0:64, cb, :], qfm[_ts(h % 2, 64), h // 2, :]),
                            waits=wkq_done + cur_mark[cb])
                ev_op(c, "gpsimd",
                      lambda g, cb=cb: g.memset(qcur[64:65, cb, :], 0.0))
                wcp = ev_op(c, "gpsimd",
                            lambda g, h=h, qb=qb, cb=cb: g.tensor_copy(
                                kcur[0:64, cb, 0:(qb + 1) * 512],
                                kfm[_ts(h % 2, 64), h // 2, 0:(qb + 1) * 512]),
                            waits=wkq_done + cur_mark[cb])
                nst = qb + 1
                for qc in range(4):
                    kspan = qb * 512 + qc * 128 + 128
                    sb_ = []
                    for si in range(nst):
                        k0 = si * 512
                        ln_ = min(512, kspan - k0)
                        bb = bank(c)
                        mm_fill(c, bb, [(bb.t[:, 0:ln_], qcur[:, cb, _ts(qc)],
                                         kcur[:, cb, k0:k0 + ln_], tuple(wcp))])
                        sb_.append((bb, ln_))
                    bcz, lcz = sb_[-1]
                    ev_op(c, "vector",
                          lambda v, bcz=bcz, lcz=lcz: v.tensor_add(
                              bcz.t[:, lcz - 128:lcz], bcz.t[:, lcz - 128:lcz],
                              c.MUP[:]),
                          waits=c.w_init + tuple(bcz.ready))
                    bcz.ready = bcz.ready + (c.dv.at(),)
                    for i, (bb, ln_) in enumerate(sb_):
                        ev_op(c, "vector",
                              lambda v, bb=bb, ln_=ln_, i=i, qc=qc: v.reduce_max(
                                  MX[:, qc * nst + i:qc * nst + i + 1],
                                  bb.t[:, 0:ln_], axis=AX.X),
                              banks=(bb,))
                if nst == 1:
                    src16 = MX[:, 0:4]
                else:
                    mxv = MX[:, 0:4 * nst].rearrange("p (q s) -> p q s", s=nst)
                    ev_op(c, "vector",
                          lambda v, mxv=mxv: v.tensor_max(
                              MX[:, 16:20], mxv[:, :, 0], mxv[:, :, 1]),
                          hard=True)
                    for i in range(2, nst):
                        ev_op(c, "vector",
                              lambda v, mxv=mxv, i=i: v.tensor_max(
                                  MX[:, 16:20], MX[:, 16:20], mxv[:, :, i]),
                              hard=True)
                    src16 = MX[:, 16:20]
                ev_op(c, "vector",
                      lambda v, src16=src16: v.tensor_scalar_mul(
                          c.NMB[:, 0:4], src16, -1.0),
                      hard=True)
                wnm = (c.dv.at(),)
                tbm = bank(c)
                w0m = tbm.begin_fill()
                for qc in range(4):
                    R.op("tensor",
                         lambda t, tbm=tbm, qc=qc: t.matmul(
                             tbm.t[0:1, _ts(qc)], c.NMB[:, qc:qc + 1],
                             c.ID[:], start=True, stop=True),
                         wait=wnm + (w0m if qc == 0 else ()) + c.w_init,
                         inc=(c.pe.bump(),) if qc == 3 else ())
                tbm.ready = (c.pe.at(),)
                ev_op(c, "vector",
                      lambda v, tbm=tbm, cb=cb: v.tensor_copy(
                          qcur[64:65, cb, 0:512], tbm.t[0:1, 0:512]),
                      banks=(tbm,))
                wmax = (c.dv.at(),)
                if qb == 3 and h in (12, 14):
                    _tap_async(c, f"aqc_{h}_{tag}", qcur[:, cb, :], wmax)
                    _tap_async(c, f"amx_{h}_{tag}", c.MXS[:, :], wmax)
                if qb == 1 and h == 0:
                    _tap(c, "gqc1", qcur[:, cb, :], wmax)
                    _tap(c, "gkc1", kcur[:, cb, :], wmax)
                if qb == 3 and h == 12:
                    _tap(c, f"gqc3_{tag}", qcur[:, cb, :], wmax)
                ob = bank(c)
                first = True
                for t_ in range(nkt):
                    qlo = max(0, t_ * 128 - qb * 512)
                    qhi = 512
                    bst = bank(c)
                    mm_fill(c, bst, [(bst.t[:, 0:qhi - qlo], kcur[:, cb, _ts(t_)],
                                      qcur[:, cb, qlo:qhi], wmax)])
                    if t_ >= 4 * qb:
                        ev_op(c, "vector",
                              lambda v, bst=bst: v.tensor_add(
                                  bst.t[:, 0:128], bst.t[:, 0:128], c.MLO[:]),
                              waits=c.w_init + tuple(bst.ready))
                        bst.ready = bst.ready + (c.dv.at(),)
                    ptslot = c.pt_rr % 4
                    c.pt_rr += 1
                    pts = c.PT[:, ptslot, 0:qhi - qlo]
                    wex = ev_op(c, "scalar",
                                lambda sc, pts=pts, bst=bst, n=qhi - qlo:
                                    sc.activation(pts, bst.t[:, 0:n], AF.Exp),
                                waits=c.pt_guard[ptslot], banks=(bst,))
                    w0 = ob.begin_fill() if first else ()
                    R.op("tensor",
                         lambda t, ob=ob, qlo=qlo, pts=pts, slot=t_, st=first, h=h:
                             t.matmul(ob.t[0:65, qlo:512], vsg[:, slot, h, :],
                                      pts, start=st, stop=True),
                         wait=wex + tuple(c.vsg_w.get((t_, h // 8), ())) + tuple(w0),
                         inc=(c.pe.bump(),))
                    ob.ready = (c.pe.at(),)
                    c.pt_guard[ptslot] = (c.pe.at(),)
                    c.vslot_mark[(t_, h // 8)] = (c.pe.at(),)
                    first = False
                if qb == 1 and h == 0:
                    _tap_psum(c, "gob1", ob)
                if qb == 3 and h == 12:
                    _tap_psum(c, f"gob3_{tag}", ob)
                if qb == 3 and h in (12, 14) and f"aob_{h}_{tag}" in DEBUG_TAPS:
                    stg_ = c.BSCR[:, (h - 12) // 2 * 512:(h - 12) // 2 * 512 + 512]
                    wst_ = ev_op(c, "vector",
                                 lambda v, stg_=stg_, ob=ob: v.tensor_copy(stg_, ob.t[:]),
                                 waits=tuple(ob.ready))
                    _tap_async(c, f"aob_{h}_{tag}", stg_, wst_)
                rr_row = mrow(c, "rr")
                ev_op(c, "vector",
                      lambda v, ob=ob: v.reciprocal(rr_row, ob.t[64:65, 0:512]),
                      waits=tuple(ob.ready))
                ev_op(c, "vector",
                      lambda v: v.tensor_copy(c.RZ[0:1, 0, :], rr_row),
                      waits=c.rz_guard, hard=True)
                ev_op(c, "vector",
                      lambda v: v.tensor_sub(mrow(c, "sc2"), rr_row, c.RZ[0:1, 0, :]),
                      hard=True)
                ev_op(c, "vector",
                      lambda v: v.tensor_copy(c.RZ[32:33, 0, :], mrow(c, "sc2")),
                      hard=True)
                wrr = (c.dv.at(),)
                br = bank(c)
                mm_fill(c, br, [(br.t[0:64, :], c.RE64[:], c.RZ[:, 0, :],
                                 tuple(wrr) + c.w_init)])
                c.rz_guard = (c.pe.at(),)
                ev_op(c, "scalar",
                      lambda sc, ob=ob: sc.copy(OSC[0:64, :], ob.t[0:64, 0:512]),
                      waits=wrr, banks=(ob,))
                wos_ = (c.ac.at(),)
                dsto = og[_ts(h % 2, 64), h // 2, :]
                ev_op(c, "vector",
                      lambda v, dsto=dsto, br=br: v.tensor_mul(
                          dsto, OSC[0:64, :], br.t[0:64, :]),
                      waits=wos_ + ogm, banks=(br,))
                cur_mark[cb] = (c.pe.at(), c.dv.at())
                if h % 4 == 3:
                    canary(f"h{qb}_{h}")
            if qb == 3:
                _tap(c, f"gog3_{tag}", og[:, :, :], (c.dv.at(),))
                _tap_async(c, f"agog3_{tag}", og[:, :, :], (c.dv.at(),))
            wog_w = (c.dv.at(),)
            for dc in range(DCH):
                buf = dc % 2
                wld = pool_dma(c, wos[:, buf, :, :],
                               c.d["wog"][:, _ts(dc)].rearrange("(ic p) n -> p ic n", p=128),
                               wait=wos_mark[buf], sem=dsem(c, f"gwo{buf}"))
                bb = bank(c)
                parts = []
                for ic in range(DCH):
                    parts.append((wos[:, buf, ic, :], og[:, ic, :],
                                  (wld + wog_w) if ic == 0 else ()))
                mm_fill(c, bb, parts)
                wos_mark[buf] = (c.pe.at(),)
                ev_op(c, "vector",
                      lambda v, bb=bb, dc=dc, qb=qb: v.tensor_add(
                          c.XR[:, dc, _ts(qb, 512)],
                          c.XR[:, dc, _ts(qb, 512)], bb.t[:]),
                      waits=c.w_x, banks=(bb,))
            ogm = (c.pe.at(),)
            canary(f"wo{qb}")
        _tap_async(c, f"avD_{tag}", vsg[:, 15, :, :], (c.dv.at(),))
        fence(c)


_CACHE = {}


def _get_nc(stages):
    key = (tuple(stages), INNER_LAYERS, SKIP_INNER_FFN, SKIP_INNER_ATTN)
    if key not in _CACHE:
        _CACHE[key] = build_nc(stages)
    return _CACHE[key]


def launch(inputs, stages=ALL_STAGES):
    g, xs = host_prep(inputs)
    nc = _get_nc(stages)
    in_maps = []
    for cid in range(NCORE):
        m = dict(g)
        m["xT"] = xs[cid]
        in_maps.append(m)
    res = run_bass_kernel_spmd(nc, in_maps, core_ids=list(range(NCORE)))
    outs = []
    for cid in range(NCORE):
        o = np.asarray(res.results[cid]["outT"], np.float32)
        outs.append(o.reshape(D, NSEQ, S).transpose(1, 2, 0))
    if DEBUG_TAPS:
        global LAST_TAPS
        LAST_TAPS = res.results
    return np.concatenate(outs, axis=0)


def kernel(**inputs):
    return launch(inputs, ALL_STAGES)


# version 09
# speedup vs baseline: 1.1706x; 1.0028x over previous
import numpy as np
import ml_dtypes
from contextlib import ExitStack

import concourse.bass as bass
import concourse.mybir as mybir
from concourse.bass_utils import run_bass_kernel_spmd

F32 = mybir.dt.float32
BF16 = mybir.dt.bfloat16
AF = mybir.ActivationFunctionType
ALU = mybir.AluOpType
AX = mybir.AxisListType

B, S, D = 16, 2048, 1024
NSEQ = 2
NCORE = 8
DCH = D // 128
NTB = S // 512
WIN = 512
HL, HG, DH = 8, 16, 64
HID, HIDP = 2730, 2816
NHC = HIDP // 128
QKV, INNER = 1536, 512
EPS = 1e-5
NEG = -1.0e30

ALL_STAGES = ("ffn1", "attn", "ffn2")
DIAG_SERIAL_WQS = False
DIAG_HCAST_DVE = False
DIAG_FENCE_QKV = False
DIAG_SKIP_T = ()
DEBUG_TAPS = ()
INNER_LAYERS = 2
SKIP_INNER_FFN = False
SKIP_INNER_ATTN = False


class Ctr:
    __slots__ = ("h", "v", "owner", "step")

    def __init__(self, h, owner, step=1):
        self.h, self.v, self.owner, self.step = h, 0, owner, step

    def bump(self):
        self.v += self.step
        return (self.h, self.step)

    def at(self):
        return (self, self.v)


class Rec:
    def __init__(self):
        self.items = []

    def op(self, eng, fn, wait=(), inc=(), hwait=()):
        self.items.append((eng, tuple(wait), fn, tuple(inc), tuple(hwait)))

    def replay(self, block):
        for name in ("tensor", "vector", "scalar", "gpsimd", "sync"):
            items = [(w, f, i, hw) for (e, w, f, i, hw) in self.items if e == name]

            def emit(eng, items=items, name=name):
                seen = {}
                for (waits, fn, incs, hwaits) in items:
                    for (ctr, val) in waits:
                        if ctr.owner == name or val <= 0:
                            continue
                        if seen.get(id(ctr), -1) >= val:
                            continue
                        seen[id(ctr)] = val
                        eng.wait_ge(ctr.h, val)
                    for (ctr, val) in hwaits:
                        if val <= 0 or seen.get(id(ctr), -1) >= val:
                            continue
                        seen[id(ctr)] = val
                        eng.wait_ge(ctr.h, val)
                    ins = fn(eng)
                    for (h, n) in incs:
                        ins = ins.then_inc(h, n)

            getattr(block, name)(emit)


class Bank:
    __slots__ = ("t", "ready", "marks", "held")

    def __init__(self, t):
        self.t = t
        self.ready = ()
        self.marks = []
        self.held = False

    def begin_fill(self):
        w = tuple(self.marks)
        self.marks = []
        self.held = True
        return w

    def release(self, ctrval):
        self.marks.append(ctrval)
        self.held = False


class Ctx:
    pass


def _ts(i, n=128):
    return slice(i * n, (i + 1) * n)


_MROW = {"mu": 0, "rstd": 1, "var": 1, "tmp": 2, "sc2": 2, "rr": 0}


def mrow(c, name):
    i = _MROW[name]
    return c.MISC[0:1, i * 512:(i + 1) * 512]


def _rotary_tables():
    f32 = np.float32
    half = np.arange(0, DH, 2, dtype=f32)
    inv_freq = (f32(1.0) / np.power(f32(10000.0), half / f32(DH))).astype(f32)
    t = np.arange(S, dtype=f32)
    freqs = (t[:, None] * inv_freq[None, :]).astype(f32)
    freqs = np.concatenate([freqs, freqs], axis=-1)
    base = ((half + f32(0.4) * f32(DH)) / (f32(1.4) * f32(DH))).astype(f32)
    power = ((t - f32(S // 2)) / f32(WIN // 2)).astype(f32)
    scale = np.power(base[None, :], power[:, None]).astype(f32)
    scale = np.concatenate([scale, scale], axis=-1)
    cos = np.cos(freqs).astype(f32)
    sin = np.sin(freqs).astype(f32)
    return np.stack([(cos * scale).astype(f32), (sin * scale).astype(f32),
                     (cos / scale).astype(f32), (sin / scale).astype(f32)], axis=0)


def host_prep(inputs):
    f32 = np.float32
    g = {}
    g["wq"] = np.ascontiguousarray(np.asarray(inputs["L0_attn_W_Q"], f32).T)
    g["wk"] = np.ascontiguousarray(np.asarray(inputs["L0_attn_W_K"], f32).T)
    g["wv"] = np.ascontiguousarray(np.asarray(inputs["L0_attn_W_V"], f32).T)
    g["wog"] = np.ascontiguousarray(np.asarray(inputs["L0_attn_W_O"], f32).T)
    g["wqkv"] = np.ascontiguousarray(np.asarray(inputs["L0_ffn_wqkv"], f32))
    g["wol"] = np.ascontiguousarray(np.asarray(inputs["L0_ffn_wo"], f32))
    w1 = np.asarray(inputs["L0_ffn_w1"], f32)
    w1p = np.zeros((2, D, 2 * HIDP), dtype=f32)
    w1p[:, :, 0:HID] = w1[:, :, 0:HID]
    w1p[:, :, HIDP:HIDP + HID] = w1[:, :, HID:2 * HID]
    g["w1"] = w1p
    w2 = np.asarray(inputs["L0_ffn_w2"], f32)
    w2p = np.zeros((2, HIDP, D), dtype=f32)
    w2p[:, 0:HID, :] = w2
    g["w2"] = w2p
    b1 = np.asarray(inputs["L0_ffn_b1"], f32)
    b1p = np.zeros((2, 2 * HIDP), dtype=f32)
    b1p[:, 0:HID] = b1[:, 0:HID]
    b1p[:, HIDP:HIDP + HID] = b1[:, HID:2 * HID]
    g["b1c"] = np.ascontiguousarray(b1p.reshape(2, 2 * NHC, 128).transpose(2, 0, 1))
    sm = np.zeros((128, 2, 3, DCH), dtype=f32)
    for l in range(2):
        sm[:, l, 0, :] = np.asarray(inputs["L0_ffn_ln_g"][l], f32).reshape(DCH, 128).T
        sm[:, l, 1, :] = np.asarray(inputs["L0_ffn_ln_b"][l], f32).reshape(DCH, 128).T
        sm[:, l, 2, :] = np.asarray(inputs["L0_ffn_b2"][l], f32).reshape(DCH, 128).T
    g["smalls"] = np.ascontiguousarray(sm.reshape(128, 2 * 3 * DCH))
    qsks = np.stack([np.asarray(inputs["L0_ffn_q_scale"], f32),
                     np.asarray(inputs["L0_ffn_k_scale"], f32)], axis=1)
    g["qsks"] = np.ascontiguousarray(qsks.reshape(1, 2 * 2 * DH))
    g["tabs"] = np.ascontiguousarray(_rotary_tables())

    x = np.asarray(inputs["L0_attn_x"], f32)
    xs = []
    for cid in range(NCORE):
        xc = x[2 * cid:2 * cid + 2]
        xs.append(np.ascontiguousarray(xc.transpose(2, 0, 1).reshape(D, NSEQ * S)))
    return g, xs


def bank(c, n=1):
    out = []
    for _ in range(n):
        for _try in range(len(c.banks) + 1):
            b = c.banks[c.rr % len(c.banks)]
            c.rr += 1
            if not b.held:
                b.held = True
                break
        else:
            raise RuntimeError("all psum banks held")
        out.append(b)
    return out if n > 1 else out[0]


def tbank(c):
    b = c.tbk[c.trr % len(c.tbk)]
    c.trr += 1
    return b


def tb_wait(c):
    return tuple(c.tb_marks.values())


def tb_mark(c, ctrval):
    c.tb_marks[id(ctrval[0])] = ctrval


def mm_fill(c, b, parts, out=None):
    R = c.R
    w0 = b.begin_fill()
    n = len(parts)
    for i, p in enumerate(parts):
        if len(p) == 3:
            l, r, wop = p
            o = b.t if out is None else out
        else:
            o, l, r, wop = p
        st, sp = (i == 0), (i == n - 1)
        inc = (c.pe.bump(),) if sp else ()
        R.op("tensor",
             lambda t, o=o, l=l, r=r, st=st, sp=sp: t.matmul(o, l, r, start=st, stop=sp),
             wait=tuple(wop) + (w0 if i == 0 else ()), inc=inc)
    b.ready = (c.pe.at(),)


def ev_op(c, eng, fn, waits=(), banks=(), hard=False):
    ctr = {"vector": c.dv, "scalar": c.ac, "gpsimd": c.po, "tensor": c.pe}[eng]
    allw = tuple(waits)
    for b in banks:
        allw = allw + tuple(b.ready)
    hw = ((ctr, ctr.v),) if hard else ()
    c.R.op(eng, fn, wait=allw, inc=(ctr.bump(),), hwait=hw)
    for b in banks:
        b.release(ctr.at())
    return (ctr.at(),)


def fence(c):
    marks = (c.pe.at(), c.dv.at(), c.ac.at(), c.po.at(), c.pld.at(), c.spd.at())
    c.R.op("tensor", lambda t: t.nop(), wait=marks)
    c.R.op("vector", lambda v: v.engine_nop(), wait=marks)
    c.R.op("scalar", lambda sc: sc.nop(), wait=marks)
    c.R.op("gpsimd", lambda g: g.engine_nop(), wait=marks)
    c.R.op("sync", lambda sy: sy.nop(), wait=marks)


def dsem(c, key):
    if key not in c.dsems:
        h = c.stack.enter_context(c.nc.semaphore(f"ds_{key}"))
        c.dsems[key] = Ctr(h, None, 16)
    return c.dsems[key]


def pool_dma(c, out_ap, in_ap, wait=(), sem=None):
    ctr = sem if sem is not None else c.pld
    c.R.op("gpsimd",
           lambda g, o=out_ap, i=in_ap: g.dma_start(out=o, in_=i),
           wait=tuple(wait), inc=(ctr.bump(),))
    return (ctr.at(),)


def _tap_psum(c, name, bb):
    if name not in DEBUG_TAPS or name in c.taps:
        return
    lnt = c.LNT
    w = ev_op(c, "vector", lambda v, lnt=lnt: v.tensor_copy(lnt[:, 0:512], bb.t[:]),
              waits=tuple(bb.ready))
    _tap(c, name, lnt[:, 0:512], w)


def _tap(c, name, ap, wait):
    if name not in DEBUG_TAPS or name in c.taps:
        return
    c.taps[name] = True
    shape = list(ap.shape)
    d = c.nc.dram_tensor(f"dbg_{name}", shape, ap.dtype, kind="ExternalOutput").ap()
    sync_dma(c, d, ap, wait=wait)
    fence(c)


def _tap_async(c, name, ap, wait):
    if name not in DEBUG_TAPS or name in c.taps:
        return
    c.taps[name] = True
    shape = list(ap.shape)
    d = c.nc.dram_tensor(f"dbg_{name}", shape, ap.dtype, kind="ExternalOutput").ap()
    sync_dma(c, d, ap, wait=wait)


def sync_dma(c, out_ap, in_ap, wait=(), sem=None):
    ctr = sem if sem is not None else c.spd
    c.R.op("sync",
           lambda sy, o=out_ap, i=in_ap: sy.dma_start(out=o, in_=i),
           wait=tuple(wait), inc=(ctr.bump(),))
    return (ctr.at(),)


def build_nc(stages):
    nc = bass.Bass("TRN2", target_bir_lowering=False, debug=False)
    c = Ctx()
    c.nc = nc
    c.R = Rec()

    c.d = {}
    for name, shape in [("xT", [D, NSEQ * S]), ("wq", [D, D]), ("wk", [D, D]),
                        ("wv", [D, D]), ("wog", [D, D]), ("wqkv", [2, D, QKV]),
                        ("wol", [2, INNER, D]), ("w1", [2, D, 2 * HIDP]),
                        ("w2", [2, HIDP, D]), ("b1c", [128, 2, 2 * NHC]),
                        ("smalls", [128, 2 * 3 * DCH]), ("qsks", [1, 2 * 2 * DH]),
                        ("tabs", [4, S, DH])]:
        c.d[name] = nc.dram_tensor(name, shape, F32, kind="ExternalInput").ap()
    c.d["outT"] = nc.dram_tensor("outT", [D, NSEQ * S], F32, kind="ExternalOutput").ap()

    with ExitStack() as stack:
        c.stack = stack
        e = stack.enter_context
        c.pe = Ctr(e(nc.semaphore("c_pe")), "tensor")
        c.dv = Ctr(e(nc.semaphore("c_dv")), "vector")
        c.ac = Ctr(e(nc.semaphore("c_ac")), "scalar")
        c.po = Ctr(e(nc.semaphore("c_po")), "gpsimd")
        c.pld = Ctr(e(nc.semaphore("c_pld")), None, 16)
        c.spd = Ctr(e(nc.semaphore("c_spd")), None, 16)
        c.dsems = {}

        c.banks = [Bank(e(nc.psum_tensor(f"ps{i}", [128, 512], F32))[:, :]) for i in range(7)]
        c.rr = 0
        tb = e(nc.psum_tensor("pstr", [128, 8, 128], BF16))
        c.tbk = [Bank(tb[:, i, :]) for i in range(8)]
        c.tbv = tb
        c.trr = 0
        c.tb_marks = {}

        c.XR = e(nc.sbuf_tensor("XR", [128, DCH, S], F32))
        c.MUP = e(nc.sbuf_tensor("MUP", [128, 128], BF16))
        c.MLO = e(nc.sbuf_tensor("MLO", [128, 128], BF16))
        c.ID = e(nc.sbuf_tensor("ID", [128, 128], BF16))
        c.ONES = e(nc.sbuf_tensor("ONESC", [128, 2], BF16))
        c.REB = e(nc.sbuf_tensor("REB", [128, 1, 128], BF16))
        c.RE64 = e(nc.sbuf_tensor("RE64", [128, 64], BF16))
        c.SM = e(nc.sbuf_tensor("SM", [128, 2 * 3 * DCH], F32))
        c.B1C = e(nc.sbuf_tensor("B1C", [128, 2, 2 * NHC], F32))
        c.QS = e(nc.sbuf_tensor("QS", [1, 2 * 2 * DH], BF16))
        c.QSW = e(nc.sbuf_tensor("QSW", [1, 2 * 2 * DH], BF16))
        c.MISC = e(nc.sbuf_tensor("MISC", [1, 1536], F32))
        c.RZ = e(nc.sbuf_tensor("RZ", [128, 2, 512], BF16))
        c.RZC = e(nc.sbuf_tensor("RZC", [128, 512], BF16))
        c.QSB = e(nc.sbuf_tensor("QSB", [128, 4, DH], F32))
        c.RO8 = e(nc.sbuf_tensor("RO8", [1, 128], BF16))
        c.RO1 = e(nc.sbuf_tensor("RO1", [1, 128], BF16))
        c.HLS = e(nc.sbuf_tensor("HLS", [128, 2], BF16))
        c.HLSF = e(nc.sbuf_tensor("HLSF", [128, 1], F32))
        c.MXS = e(nc.sbuf_tensor("MXS", [128, 24], F32))
        c.NMB = e(nc.sbuf_tensor("NMB", [128, 4], BF16))
        c.taps = {}
        c.tab_layer = -1
        c.rz_guard = ()
        c.xbb_guard = ()
        c.bt_guard = ()
        c.hls_guard = ()
        c.stg_guard = ()
        c.pt_guard = [(), (), (), ()]
        c.pt_rr = 0
        c.ol_guard = ()
        c.gl_layer = -1

        _init_consts(c)
        for s in range(NSEQ):
            _load_x(c, s)
            if "ffn1" in stages:
                _ffn_macaron(c, s, 0)
            if "attn" in stages:
                _global_attn(c, s)
            if "ffn2" in stages:
                _ffn_macaron(c, s, 1)
            _store_x(c, s)
        c.R.op("sync", lambda e_: e_.nop(),
               wait=(c.spd.at(), dsem(c, "sto").at()))

        with nc.Block() as block:
            c.R.replay(block)
    return nc


def _init_consts(c):
    R = c.R
    c.w_sm = sync_dma(c, c.SM[:], c.d["smalls"], sem=dsem(c, "ini"))
    c.w_b1c = sync_dma(c, c.B1C[:], c.d["b1c"], sem=dsem(c, "ini"))
    c.w_qs = pool_dma(c, c.QS[:], c.d["qsks"], sem=dsem(c, "inip"))

    init_fns = [
        lambda g: g.memset(c.MUP[:], 0.0),
        lambda g: g.affine_select(out=c.MUP[:], in_=c.MUP[:], compare_op=ALU.is_ge,
                                  fill=NEG, base=0, pattern=[[-1, 128]],
                                  channel_multiplier=1),
        lambda g: g.memset(c.MLO[:], 0.0),
        lambda g: g.affine_select(out=c.MLO[:], in_=c.MLO[:], compare_op=ALU.is_ge,
                                  fill=NEG, base=0, pattern=[[1, 128]],
                                  channel_multiplier=-1),
        lambda g: g.memset(c.ID[:], 0.0),
        lambda g: g.affine_select(out=c.ID[:], in_=c.ID[:], compare_op=ALU.not_equal,
                                  fill=1.0, base=0, pattern=[[-1, 128]],
                                  channel_multiplier=1),
        lambda g: g.memset(c.ONES[:], 1.0),
        lambda g: g.memset(c.REB[:], 0.0),
        lambda g: g.memset(c.REB[0:1, :, :], 1.0),
        lambda g: g.memset(c.REB[32:33, :, :], 1.0),
        lambda g: g.memset(c.RE64[:], 0.0),
        lambda g: g.memset(c.RE64[0:1, :], 1.0),
        lambda g: g.memset(c.RE64[32:33, :], 1.0),
        lambda g: g.memset(c.RZ[:], 0.0),
        lambda g: g.memset(c.RZC[:], 0.0),
        lambda g: g.memset(c.RZC[0:1, :], 1.0),
        lambda g: g.memset(c.RZC[32:33, :], 1.0),
        lambda g: g.memset(c.RO8[:], 8.0),
        lambda g: g.memset(c.RO1[:], 1.0),
    ]
    for fn in init_fns:
        hw = ((c.po, c.po.v),)
        R.op("gpsimd", fn, inc=(c.po.bump(),), hwait=hw)
    c.w_init = (c.po.at(),)

    for i in range(4):
        o = i * DH
        R.op("vector",
             lambda v, o=o: v.tensor_scalar_mul(c.QSW[0:1, o:o + 32],
                                                c.QS[0:1, o + 32:o + 64], -1.0),
             wait=c.w_qs, inc=(c.dv.bump(),))
        R.op("vector",
             lambda v, o=o: v.tensor_copy(c.QSW[0:1, o + 32:o + 64],
                                          c.QS[0:1, o:o + 32]),
             wait=c.w_qs, inc=(c.dv.bump(),))
    c.w_qsw = (c.dv.at(),)


def _load_x(c, s):
    w = (dsem(c, "sto").at(),) if s > 0 else ()
    c.w_x = sync_dma(
        c, c.XR[:],
        c.d["xT"].rearrange("(c p) n -> p c n", p=128)[:, :, s * S:(s + 1) * S],
        wait=w, sem=dsem(c, "xld"))


def _store_x(c, s):
    sync_dma(c,
             c.d["outT"].rearrange("(c p) n -> p c n", p=128)[:, :, s * S:(s + 1) * S],
             c.XR[:], wait=(c.dv.at(),), sem=dsem(c, "sto"))


def _stat_block(c, get_src, j, src_wait):
    R = c.R
    mu, rstd, var, tmp, sc2 = (mrow(c, k) for k in ("mu", "rstd", "var", "tmp", "sc2"))
    bs1, bs2 = bank(c, 2)
    w1_ = bs1.begin_fill()
    w2_ = bs2.begin_fill()
    slot_mm = [(), ()]
    for d in range(DCH):
        sl = d % 2
        src = get_src(d, j)
        xb = c.BSCR[:, sl * 512:(sl + 1) * 512]
        sq = c.BSCR[:, 1024 + sl * 512:1024 + (sl + 1) * 512]
        wc = ev_op(c, "gpsimd",
                   lambda g, o=xb, i=src: g.tensor_copy(o, i),
                   waits=tuple(src_wait) + c.bscr_guard + slot_mm[sl])
        ws = ev_op(c, "scalar",
                   lambda sc_, o=sq, i=src: sc_.activation(o, i, AF.Square),
                   waits=tuple(src_wait) + c.bscr_guard + slot_mm[sl])
        inc1 = (c.pe.bump(),)
        R.op("tensor",
             lambda t, o=bs1.t[0:1, :], xb=xb, st=(d == 0), sp=(d == DCH - 1):
                 t.matmul(o, c.ONES[:, 0:1], xb, start=st, stop=sp),
             wait=wc[0:1] + (w1_ if d == 0 else ()), inc=inc1)
        inc2 = (c.pe.bump(),)
        R.op("tensor",
             lambda t, o=bs2.t[0:1, :], sq=sq, st=(d == 0), sp=(d == DCH - 1):
                 t.matmul(o, c.ONES[:, 0:1], sq, start=st, stop=sp),
             wait=ws[0:1] + (w2_ if d == 0 else ()), inc=inc2)
        slot_mm[sl] = (c.pe.at(),)
    bs1.ready = (c.pe.at(),)
    bs2.ready = (c.pe.at(),)
    c.bscr_guard = (c.pe.at(),)
    wser = (c.ac.at(),) + c.rz_guard
    ev_op(c, "vector", lambda v: v.tensor_scalar_mul(mu, bs1.t[0:1, :], 1.0 / D),
          waits=wser, banks=(bs1,))
    ev_op(c, "vector", lambda v: v.tensor_scalar_mul(var, bs2.t[0:1, :], 1.0 / D),
          banks=(bs2,))
    ev_op(c, "vector", lambda v: v.tensor_mul(tmp, mu, mu), hard=True)
    ev_op(c, "vector", lambda v: v.tensor_sub(var, var, tmp), hard=True)
    wv = (c.dv.at(),)
    ev_op(c, "scalar", lambda sc: sc.activation(rstd, var, AF.Ln, bias=np.float32(EPS)),
          waits=wv)
    wact = ev_op(c, "scalar", lambda sc: sc.activation(rstd, rstd, AF.Exp,
                                                       scale=np.float32(-0.5)), hard=True)
    ev_op(c, "vector", lambda v: v.tensor_scalar_mul(mu, mu, -1.0), hard=True)
    ev_op(c, "vector", lambda v: v.tensor_mul(mu, mu, rstd), waits=wact, hard=True)

    rz = c.RZ
    ev_op(c, "vector", lambda v: v.tensor_copy(rz[0:1, 0, :], rstd), hard=True)
    ev_op(c, "vector", lambda v: v.tensor_sub(sc2, rstd, rz[0:1, 0, :]), hard=True)
    ev_op(c, "vector", lambda v: v.tensor_copy(rz[32:33, 0, :], sc2), hard=True)
    ev_op(c, "vector", lambda v: v.tensor_copy(rz[64:65, 0, :], rz[0:1, 0, :]), hard=True)
    ev_op(c, "vector", lambda v: v.tensor_copy(rz[0:1, 1, :], mu), hard=True)
    ev_op(c, "vector", lambda v: v.tensor_sub(sc2, mu, rz[0:1, 1, :]), hard=True)
    ev_op(c, "vector", lambda v: v.tensor_copy(rz[32:33, 1, :], sc2), hard=True)
    rzw = ev_op(c, "vector", lambda v: v.tensor_copy(rz[64:65, 1, :], rz[0:1, 1, :]),
                hard=True)
    if getattr(c, "tap_rz", False):
        c.tap_rz = False
        _tap(c, "rzin", c.RZ[:, :, :], rzw)
        _tap(c, "msc", c.MISC[:, :], rzw)
    stag = getattr(c, "stat_tag", "")
    _tap(c, f"st_misc_{stag}_{j}", c.MISC[:, :], rzw)
    _tap(c, f"st_rz_{stag}_{j}", c.RZ[:, :, :], rzw)
    return rzw


def _ln_apply_block(c, get_src, get_dst, j, rz_wait, glA_fn, glB_fn, src_wait=(),
                    mm_wait=()):
    for d in range(DCH):
        ba, bb = bank(c, 2)
        mm_fill(c, ba, [(glA_fn(d), c.RZ[:, 0, :], tuple(rz_wait) + tuple(mm_wait))])
        gb = glB_fn(d)
        partsB = [(glA_fn(d), c.RZ[:, 1, :], tuple(rz_wait) + tuple(mm_wait))]
        if gb is not None:
            partsB.append((gb, c.RZC[:], c.w_init))
        mm_fill(c, bb, partsB)
        if gb is not None and d == 0:
            _tap_psum(c, "brep", bb)
            _tap_psum(c, "arep", ba)
        src = get_src(d, j)
        lnt = c.LNT
        ev_op(c, "vector",
              lambda v, src=src, ba=ba, lnt=lnt: v.tensor_mul(lnt[:], src, ba.t[:]),
              waits=src_wait, banks=(ba,))
        dst = get_dst(d, j)
        ev_op(c, "vector",
              lambda v, dst=dst, bb=bb, lnt=lnt: v.tensor_add(dst, lnt[:], bb.t[:]),
              banks=(bb,))
    c.rz_guard = (c.pe.at(),)


def _ln(c, get_src, get_dst, src_wait, glA_fn=None, glB_fn=None, blocks=(0, NTB),
        mm_wait=()):
    if glA_fn is None:
        glA_fn = lambda d: c.REB[:, 0, :]
        glB_fn = lambda d: None
        mm_wait = tuple(mm_wait) + c.w_init
    for j in range(blocks[0], blocks[1]):
        rzw = _stat_block(c, get_src, j, src_wait)
        _ln_apply_block(c, get_src, get_dst, j, rzw, glA_fn, glB_fn, src_wait, mm_wait)


def _ffn_macaron(c, s, apply_idx):
    nc = c.nc
    with ExitStack() as ph:
        e = ph.enter_context
        tag = f"{s}_{apply_idx}"
        YR = e(nc.sbuf_tensor(f"YR_{tag}", [128, DCH, S + 1], F32))
        c.LNT = e(nc.sbuf_tensor(f"LNT_{tag}", [128, 512], F32))
        c.BSCR = e(nc.sbuf_tensor(f"BSCR_{tag}", [128, 2048], BF16))
        c.bscr_guard = ()
        c.stat_tag = f"f{tag}"
        wg = ev_op(c, "gpsimd", lambda g: g.memset(YR[:, :, 0:1], 0.0),
                   waits=c.w_init)
        _tap(c, f"xin_{tag}", c.XR[:, :, :], (c.dv.at(),))
        _ln(c,
            get_src=lambda d, j: c.XR[:, d, _ts(j, 512)],
            get_dst=lambda d, j: YR[:, d, 1 + j * 512:1 + (j + 1) * 512],
            src_wait=c.w_x)
        _tap(c, f"yr1_{tag}", YR[:, :, :], (c.dv.at(),))
        for l in range(INNER_LAYERS):
            _inner_layer(c, s, apply_idx, l, YR, wg)
        _tap(c, f"yr2_{tag}", YR[:, :, :], (c.dv.at(),))
        for d in range(DCH):
            for j in range(NTB):
                ev_op(c, "vector",
                      lambda v, d=d, j=j: v.scalar_tensor_tensor(
                          c.XR[:, d, _ts(j, 512)],
                          YR[:, d, 1 + j * 512:1 + (j + 1) * 512],
                          0.5, c.XR[:, d, _ts(j, 512)],
                          op0=ALU.mult, op1=ALU.add),
                      waits=c.w_x)
        fence(c)


def _col_to_rows(c, colap, wcol, dst_list):
    hi = c.HLS[:, 0:1]
    lo = c.HLS[:, 1:2]
    f32s = c.HLSF[:, 0:1]
    ev_op(c, "vector", lambda v: v.tensor_copy(hi, colap),
          waits=tuple(wcol) + c.hls_guard)
    ev_op(c, "vector", lambda v: v.tensor_sub(f32s, colap, hi), hard=True)
    ev_op(c, "vector", lambda v: v.tensor_copy(lo, f32s), hard=True)
    whl = (c.dv.at(),)
    outs = {}
    for name, src in (("hi", hi), ("lo", lo)):
        bb = bank(c)
        w0 = bb.begin_fill()
        c.R.op("tensor",
               lambda t, src=src, bb=bb: t.matmul(bb.t[0:1, 0:128], src, c.ID[:],
                                                  start=True, stop=True),
               wait=whl + w0 + c.w_init, inc=(c.pe.bump(),))
        bb.ready = (c.pe.at(),)
        outs[name] = bb
    for (dst, use_lo) in dst_list:
        bb = outs["lo"] if use_lo else outs["hi"]
        ev_op(c, "vector",
              lambda v, dst=dst, bb=bb: v.tensor_copy(dst, bb.t[0:1, 0:128]),
              waits=tuple(bb.ready))
    outs["hi"].release((c.dv, c.dv.v))
    outs["lo"].release((c.dv, c.dv.v))
    c.hls_guard = (c.dv.at(),)


def _build_tables(c, l):
    QSB = c.QSB
    srcs = [(c.QS, l * 2 + 0, c.RO8), (c.QSW, l * 2 + 0, c.RO8),
            (c.QS, l * 2 + 1, c.RO1), (c.QSW, l * 2 + 1, c.RO1)]
    for i, (rowt, idx, lco) in enumerate(srcs):
        bb = bank(c)
        mm_fill(c, bb, [(bb.t[0:128, 0:64], lco[0:1, :],
                         rowt[0:1, idx * DH:(idx + 1) * DH], c.w_qsw + c.w_init)])
        ev_op(c, "vector",
              lambda v, i=i, bb=bb: v.tensor_copy(QSB[:, i, :], bb.t[:, 0:64]),
              banks=(bb,))
    wq = (c.dv.at(),)
    for t4 in range(4):
        wld = sync_dma(c, c.BT[:],
                       c.d["tabs"][t4].rearrange("(tt p) d -> p tt d", p=128),
                       wait=c.bt_guard, sem=dsem(c, "bt"))
        tab_, bt_ = c.TAB, c.BT
        ev_op(c, "vector",
              lambda v, t4=t4, tab_=tab_, bt_=bt_: v.tensor_mul(
                  tab_[:, t4, :, :], bt_[:, :, :],
                  c.QSB[:, t4, :].unsqueeze(1).broadcast_to((128, S // 128, DH))),
              waits=wld + wq, hard=True)
        c.bt_guard = (c.dv.at(),)
    c.w_tab = (c.dv.at(),)


def _build_gl(c, l):
    glw = (c.pe.at(),)
    gla_, glb_ = c.GLA, c.GLB
    ev_op(c, "gpsimd", lambda g, gla_=gla_: g.memset(gla_[:], 0.0), waits=glw)
    wz = ev_op(c, "gpsimd", lambda g, glb_=glb_: g.memset(glb_[:], 0.0), waits=glw)
    glw = glw + wz
    for d in range(DCH):
        gcol = c.SM[:, l * 24 + d:l * 24 + d + 1]
        bcol = c.SM[:, l * 24 + 8 + d:l * 24 + 8 + d + 1]
        _col_to_rows(c, gcol, c.w_sm + glw,
                     [(c.GLA[0:1, d, :], False), (c.GLA[32:33, d, :], False),
                      (c.GLA[64:65, d, :], True)])
        _col_to_rows(c, bcol, c.w_sm + glw,
                     [(c.GLB[0:1, d, :], False), (c.GLB[32:33, d, :], True)])
    c.w_gl = (c.dv.at(),)
    _tap(c, "gla", c.GLA[:, :, :], c.w_gl)
    _tap(c, "glb", c.GLB[:, :, :], c.w_gl)


def _inner_layer(c, s, apply_idx, l, YR, guard_wait):
    nc = c.nc
    R = c.R
    tag = f"{s}_{apply_idx}_{l}"
    with ExitStack() as ph:
        e = ph.enter_context
        c.TAB = e(nc.sbuf_tensor(f"TAB_{tag}", [128, 4, S // 128, DH], BF16))
        c.PT = e(nc.sbuf_tensor(f"PT_{tag}", [128, 4, 512], BF16))
        c.pt_guard = [(), (), (), ()]
        c.STG = e(nc.sbuf_tensor(f"STG_{tag}", [128, 4, 2, 128], BF16))
        with ExitStack() as bts:
            c.BT = bts.enter_context(
                nc.sbuf_tensor(f"BT_{tag}", [128, S // 128, DH], F32))
            _build_tables(c, l)
            fence(c)
        vsb = e(nc.sbuf_tensor(f"vsb_{tag}", [128, 8, HL, 65], BF16))
        qw = e(nc.sbuf_tensor(f"qw_{tag}", [128, 4, 512], BF16))
        kw = e(nc.sbuf_tensor(f"kw_{tag}", [128, 2, 4, 512], BF16))
        qc2 = e(nc.sbuf_tensor(f"qc2_{tag}", [128, 2, 512], BF16))
        kc2 = e(nc.sbuf_tensor(f"kc2_{tag}", [128, 2, 1024], BF16))
        o_l = e(nc.sbuf_tensor(f"ol_{tag}", [128, 4, 512], BF16))
        HT = e(nc.sbuf_tensor(f"HT_{tag}", [128, 2, DCH, 128], BF16))
        SQF = c.BSCR[:, 0:1024]
        QRS = e(nc.sbuf_tensor(f"QRS_{tag}", [128, 16], F32))
        OSC = e(nc.sbuf_tensor(f"OSC_{tag}", [128, 512], BF16))
        wqs = e(nc.sbuf_tensor(f"wqs_{tag}", [128, 2, QKV], BF16))
        wos = e(nc.sbuf_tensor(f"wos_{tag}", [128, 1, 4, 256], BF16))
        def initcur(g):
            g.memset(qc2[:], 0.0)
            g.memset(kc2[:], 0.0)
            g.memset(kc2[64:65, :, :], 1.0)
            g.memset(vsb[:], 0.0)
            return g.memset(vsb[:, :, :, 64:65], 1.0)
        wcur = ev_op(c, "gpsimd", initcur, waits=c.w_init)
        c.vslot_mark = {}
        c.wos_mark = [(c.pe.at(),)]
        c.sqf_guard = ()
        wqs_mark = [(c.pe.at(),), (c.pe.at(),)]
        ht_mark = [(), ()]
        vse = {}
        qwm, kwm = {}, {}
        for w in range(4 if not SKIP_INNER_ATTN else 0):
            wlay = (c.dv.at(),)
            for pr in range(2):
                tiles = (2 * pr, 2 * pr + 1)
                hbuf = pr % 2
                for wt in tiles:
                    T = w * 4 + wt
                    for d in range(DCH):
                        off = 1 if d < 4 else 0
                        src = YR[:, d, off + T * 128: off + T * 128 + 128]
                        eng_h = "vector" if DIAG_HCAST_DVE else "gpsimd"
                        ev_op(c, eng_h,
                              lambda g, o=HT[:, wt % 2, d, :], i=src: g.tensor_copy(o, i),
                              waits=wlay + guard_wait + ht_mark[wt % 2])
                wh = (c.po.at(),) + ((c.dv.at(),) if DIAG_HCAST_DVE else ())
                if DIAG_FENCE_QKV:
                    fence(c)
                if w == 0 and pr == 0:
                    _tap(c, "ht0", HT[:, :, :, :], wh)
                pb = {}
                w0s = {}
                for wt in tiles:
                    pb[wt] = bank(c, 3)
                    for part in range(3):
                        w0s[(wt, part)] = pb[wt][part].begin_fill()
                for d in range(DCH):
                    buf = d % 2
                    wld = pool_dma(c, wqs[:, buf, :], c.d["wqkv"][l, _ts(d), :],
                                   wait=wqs_mark[buf], sem=dsem(c, f"wqs{buf}"))
                    for wi, wt in enumerate(tiles):
                        for part in range(3):
                            bk_ = pb[wt][part]
                            last = (d == DCH - 1)
                            round_last = (wi == 1 and part == 2)
                            inc = (c.pe.bump(),) if (last or round_last) else ()
                            R.op("tensor",
                                 lambda t, o=bk_.t, l_=HT[:, wt % 2, d, :],
                                        r=wqs[:, buf, _ts(part, 512)],
                                        st=(d == 0), sp=last:
                                     t.matmul(o, l_, r, start=st, stop=sp),
                                 wait=wld + wh +
                                      (w0s[(wt, part)] if d == 0 else ()),
                                 inc=inc)
                            if last:
                                bk_.ready = (c.pe.at(),)
                    wqs_mark[buf] = (c.pe.at(),)
                for wt in tiles:
                    ht_mark[wt % 2] = (c.pe.at(),)
                if DIAG_FENCE_QKV:
                    fence(c)
                for wt in tiles:
                    T = w * 4 + wt
                    bq, bkk, bv = pb[wt]
                    slot = T % 8
                    if w == 0 and wt == 1:
                        _tap_psum(c, "bvps", bv)
                    vse[slot] = ev_op(
                        c, "vector",
                        lambda v, bv=bv, slot=slot: v.tensor_copy(
                            vsb[:, slot, :, 0:64],
                            bv.t[:].rearrange("p (h n) -> p h n", h=HL)),
                        waits=tuple(wcur) + c.vslot_mark.get(slot, ()),
                        banks=(bv,))
                    ev_op(c, "scalar",
                          lambda sc, bq=bq: sc.activation(SQF[:, 0:512], bq.t[:], AF.Square),
                          waits=tuple(bq.ready) + c.sqf_guard + c.bscr_guard)
                    ev_op(c, "scalar",
                          lambda sc, bkk=bkk: sc.activation(SQF[:, 512:1024], bkk.t[:], AF.Square),
                          waits=tuple(bkk.ready))
                    wsq = (c.ac.at(),)
                    ev_op(c, "vector",
                          lambda v: v.reduce_sum(
                              QRS[:, :].rearrange("p (a b) -> p a b", a=16),
                              SQF[:, :].rearrange("p (a b) -> p a b", a=16),
                              axis=AX.X),
                          waits=wsq)
                    c.sqf_guard = (c.dv.at(),)
                    c.bscr_guard = c.bscr_guard + (c.dv.at(),)
                    ev_op(c, "vector",
                          lambda v: v.tensor_scalar_max(QRS[:, :], QRS[:, :], 1e-24),
                          hard=True)
                    wv_ = (c.dv.at(),)
                    ev_op(c, "scalar",
                          lambda sc: sc.activation(QRS[:, :], QRS[:, :], AF.Ln),
                          waits=wv_)
                    ev_op(c, "scalar",
                          lambda sc: sc.activation(QRS[:, :], QRS[:, :], AF.Exp,
                                                   scale=np.float32(-0.5)), hard=True)
                    wrs = (c.ac.at(),)
                    for hp in range(4):
                        for hh in range(2):
                            h = 2 * hp + hh
                            for (qk, bsrc, tA, tV, rcol) in (
                                    (0, bq, 0, 1, h), (1, bkk, 2, 3, 8 + h)):
                                ps = bsrc.t
                                base = h * DH
                                rstd = QRS[:, rcol:rcol + 1]
                                dst = c.STG[:, hp, qk, :]
                                dsl = dst[:, hh * DH:(hh + 1) * DH]
                                t1 = c.LNT[:, 0:64]
                                t2 = c.LNT[:, 64:128]
                                tab_ = c.TAB
                                ev_op(c, "vector",
                                      lambda v, ps=ps, base=base, rstd=rstd, tA=tA,
                                             T=T, t1=t1, tab_=tab_:
                                      v.scalar_tensor_tensor(
                                          t1, ps[:, base:base + DH], rstd,
                                          tab_[:, tA, T, :],
                                          op0=ALU.mult, op1=ALU.mult),
                                      waits=wrs + c.w_tab + c.stg_guard)
                                ev_op(c, "vector",
                                      lambda v, ps=ps, base=base, rstd=rstd, tV=tV,
                                             T=T, t2=t2, tab_=tab_:
                                      v.scalar_tensor_tensor(
                                          t2[:, 0:32], ps[:, base + 32:base + 64], rstd,
                                          tab_[:, tV, T, 0:32],
                                          op0=ALU.mult, op1=ALU.mult))
                                ev_op(c, "vector",
                                      lambda v, ps=ps, base=base, rstd=rstd, tV=tV,
                                             T=T, t2=t2, tab_=tab_:
                                      v.scalar_tensor_tensor(
                                          t2[:, 32:64], ps[:, base:base + 32], rstd,
                                          tab_[:, tV, T, 32:64],
                                          op0=ALU.mult, op1=ALU.mult))
                                ev_op(c, "vector",
                                      lambda v, dsl=dsl, t1=t1, t2=t2:
                                          v.tensor_add(dsl, t1, t2),
                                      hard=True)
                    wstg = (c.dv.at(),)
                    bq.release((c.dv, c.dv.v))
                    bkk.release((c.dv, c.dv.v))
                    wtb = tb_wait(c)
                    for hp in range(4):
                        for qk in range(2):
                            tb = c.tbk[2 * hp + qk]
                            stg_ = c.STG
                            R.op("tensor",
                                 lambda t, tb=tb, hp=hp, qk=qk, stg_=stg_:
                                     t.transpose(tb.t[:], stg_[:, hp, qk, :], c.ID[:]),
                                 wait=wstg + wtb + c.w_init,
                                 inc=(c.pe.bump(),) if (hp == 3 and qk == 1) else ())
                    wtr = (c.pe.at(),)
                    c.stg_guard = (c.pe.at(),)
                    tbq = c.tbv[:, :, :].rearrange("p (a b) n -> p a b n", b=2)
                    ev_op(c, "vector",
                          lambda v, wt=wt: v.tensor_copy(qw[:, :, wt * 128:(wt + 1) * 128],
                                                         tbq[:, :, 0, :]),
                          waits=wtr + qwm.get(0, ()))
                    ev_op(c, "vector",
                          lambda v, wt=wt, w=w: v.tensor_copy(kw[:, w % 2, :, wt * 128:(wt + 1) * 128],
                                                              tbq[:, :, 1, :]),
                          waits=kwm.get(w % 2, ()))
                    tb_mark(c, (c.dv, c.dv.v))
            wqk = (c.dv.at(), c.ac.at())
            if w == 0:
                _tap(c, "qw0", qw[:, :, :], wqk)
                _tap(c, "kw0", kw[:, :, :, :], wqk)
                _tap(c, "vsb0", vsb[:, :, :, :], wqk)
            cur_mark = [(), ()]
            for hp in range(4):
                for hh in range(2):
                    h = 2 * hp + hh
                    cb = h % 2
                    wcp1 = ev_op(c, "gpsimd",
                                 lambda g, cb=cb, hp=hp, hh=hh: g.tensor_copy(
                                     qc2[0:64, cb, :], qw[_ts(hh, 64), hp, :]),
                                 waits=wqk + cur_mark[cb] + wcur)
                    ev_op(c, "gpsimd",
                          lambda g, cb=cb: g.memset(qc2[64:65, cb, :], 0.0))
                    if w > 0:
                        ev_op(c, "gpsimd",
                              lambda g, cb=cb, hp=hp, hh=hh, w=w: g.tensor_copy(
                                  kc2[0:64, cb, 0:512],
                                  kw[_ts(hh, 64), (w - 1) % 2, hp, :]))
                    wcp = ev_op(c, "gpsimd",
                                lambda g, cb=cb, hp=hp, hh=hh, w=w: g.tensor_copy(
                                    kc2[0:64, cb, 512:1024], kw[_ts(hh, 64), w % 2, hp, :]))
                    qwm[0] = (c.po.at(),)
                    kwm[w % 2] = (c.po.at(),)
                    kwm[(w - 1) % 2] = (c.po.at(),)
                    MX = c.MXS
                    nstW = 2 if w > 0 else 1
                    for qc in range(4):
                        if w > 0:
                            strips = [(qc * 128, 512), (qc * 128 + 512, 128)]
                        else:
                            strips = [(512, (qc + 1) * 128)]
                        sb_ = []
                        for (k0, ln_) in strips:
                            bb = bank(c)
                            mm_fill(c, bb, [(bb.t[:, 0:ln_],
                                             qc2[:, cb, _ts(qc)],
                                             kc2[:, cb, k0:k0 + ln_], wcp)])
                            sb_.append((bb, ln_))
                        if w > 0:
                            b0 = sb_[0][0]
                            ev_op(c, "vector",
                                  lambda v, bb=b0: v.tensor_add(
                                      bb.t[:, 0:128], bb.t[:, 0:128], c.MLO[:]),
                                  waits=c.w_init + tuple(b0.ready))
                            b0.ready = b0.ready + (c.dv.at(),)
                            bcz, lcz = sb_[1]
                            ev_op(c, "vector",
                                  lambda v, bcz=bcz: v.tensor_add(
                                      bcz.t[:, 0:128], bcz.t[:, 0:128], c.MUP[:]),
                                  waits=c.w_init + tuple(bcz.ready))
                            bcz.ready = bcz.ready + (c.dv.at(),)
                        else:
                            bcz, lcz = sb_[0]
                            ev_op(c, "vector",
                                  lambda v, bcz=bcz, lcz=lcz: v.tensor_add(
                                      bcz.t[:, lcz - 128:lcz], bcz.t[:, lcz - 128:lcz],
                                      c.MUP[:]),
                                  waits=c.w_init + tuple(bcz.ready))
                            bcz.ready = bcz.ready + (c.dv.at(),)
                        for i, (bb, ln_) in enumerate(sb_):
                            ev_op(c, "vector",
                                  lambda v, bb=bb, ln_=ln_, i=i, qc=qc: v.reduce_max(
                                      MX[:, qc * nstW + i:qc * nstW + i + 1],
                                      bb.t[:, 0:ln_], axis=AX.X),
                                  banks=(bb,))
                    if w > 0:
                        mxv = MX[:, 0:8].rearrange("p (q s) -> p q s", s=2)
                        ev_op(c, "vector",
                              lambda v, mxv=mxv: v.tensor_max(
                                  MX[:, 16:20], mxv[:, :, 0], mxv[:, :, 1]),
                              hard=True)
                        src16 = MX[:, 16:20]
                    else:
                        src16 = MX[:, 0:4]
                    ev_op(c, "vector",
                          lambda v, src16=src16: v.tensor_scalar_mul(
                              c.NMB[:, 0:4], src16, -1.0),
                          hard=True)
                    wnm = (c.dv.at(),)
                    tbm = bank(c)
                    w0m = tbm.begin_fill()
                    for qc in range(4):
                        R.op("tensor",
                             lambda t, tbm=tbm, qc=qc: t.matmul(
                                 tbm.t[0:1, _ts(qc)], c.NMB[:, qc:qc + 1],
                                 c.ID[:], start=True, stop=True),
                             wait=wnm + (w0m if qc == 0 else ()) + c.w_init,
                             inc=(c.pe.bump(),) if qc == 3 else ())
                    tbm.ready = (c.pe.at(),)
                    ev_op(c, "vector",
                          lambda v, tbm=tbm, cb=cb: v.tensor_copy(
                              qc2[64:65, cb, 0:512], tbm.t[0:1, 0:512]),
                          banks=(tbm,))
                    wmax = (c.dv.at(),)
                    if w == 0 and h == 0:
                        _tap(c, "qc0", qc2[:, cb, :], wmax)
                        _tap(c, "kc0", kc2[:, cb, :], wmax)
                    if w == 1 and h == 0:
                        _tap(c, "qc1", qc2[:, cb, :], wmax)
                        _tap(c, "kc1", kc2[:, cb, :], wmax)
                    ob = bank(c)
                    first = True
                    tlist = list(range(8)) if w > 0 else list(range(4, 8))
                    tlist = [t_ for t_ in tlist if t_ not in DIAG_SKIP_T]
                    tlist.sort(key=lambda t_: (t_ != 4,))
                    for t_ in tlist:
                        qlo = max(0, (t_ - 4) * 128)
                        qhi = min(512, t_ * 128 + 128)
                        bst = bank(c)
                        mm_fill(c, bst, [(bst.t[:, 0:qhi - qlo],
                                          kc2[:, cb, _ts(t_)],
                                          qc2[:, cb, qlo:qhi], wmax)])
                        if t_ >= 4 and t_ - 4 <= 3 and qlo == (t_ - 4) * 128:
                            ev_op(c, "vector",
                                  lambda v, bst=bst: v.tensor_add(
                                      bst.t[:, 0:128], bst.t[:, 0:128], c.MLO[:]),
                                  waits=c.w_init + tuple(bst.ready))
                            bst.ready = bst.ready + (c.dv.at(),)
                        if w > 0 and t_ <= 3:
                            off = t_ * 128 - qlo
                            ev_op(c, "vector",
                                  lambda v, bst=bst, off=off: v.tensor_add(
                                      bst.t[:, off:off + 128], bst.t[:, off:off + 128],
                                      c.MUP[:]),
                                  waits=c.w_init + tuple(bst.ready))
                            bst.ready = bst.ready + (c.dv.at(),)
                        ptslot = c.pt_rr % 4
                        c.pt_rr += 1
                        pts = c.PT[:, ptslot, 0:qhi - qlo]
                        wex = ev_op(c, "scalar",
                                    lambda sc, pts=pts, bst=bst, n=qhi - qlo:
                                        sc.activation(pts, bst.t[:, 0:n], AF.Exp),
                                    waits=c.pt_guard[ptslot], banks=(bst,))
                        slot = ((w - 1) * 4 + t_) % 8
                        w0 = ob.begin_fill() if first else ()
                        R.op("tensor",
                             lambda t, ob=ob, qlo=qlo, qhi=qhi, slot=slot, pts=pts,
                                    st=first, h=h:
                                 t.matmul(ob.t[0:65, qlo:qhi], vsb[:, slot, h, :],
                                          pts, start=st, stop=True),
                             wait=wex + tuple(vse.get(slot, ())) + tuple(w0),
                             inc=(c.pe.bump(),))
                        ob.ready = (c.pe.at(),)
                        c.pt_guard[ptslot] = (c.pe.at(),)
                        c.vslot_mark[slot] = (c.pe.at(),)
                        if w == 0 and h == 0:
                            _tap(c, f"pt{t_}", pts, wex)
                        if w == 1 and h == 0:
                            _tap(c, f"pu{t_}", pts, wex)
                        first = False
                    if w == 0 and h == 0:
                        _tap_psum(c, "obps", ob)
                    if w == 1 and h == 0:
                        _tap_psum(c, "obp1", ob)
                    rr_row = mrow(c, "rr")
                    ev_op(c, "vector",
                          lambda v, ob=ob: v.reciprocal(rr_row, ob.t[64:65, 0:512]),
                          waits=tuple(ob.ready))
                    ev_op(c, "vector",
                          lambda v: v.tensor_copy(c.RZ[0:1, 0, :], rr_row),
                          waits=c.rz_guard, hard=True)
                    ev_op(c, "vector",
                          lambda v: v.tensor_sub(mrow(c, "sc2"), rr_row, c.RZ[0:1, 0, :]),
                          hard=True)
                    ev_op(c, "vector",
                          lambda v: v.tensor_copy(c.RZ[32:33, 0, :], mrow(c, "sc2")),
                          hard=True)
                    wrr = (c.dv.at(),)
                    br = bank(c)
                    mm_fill(c, br, [(br.t[0:64, :], c.RE64[:], c.RZ[:, 0, :],
                                     tuple(wrr) + c.w_init)])
                    c.rz_guard = (c.pe.at(),)
                    ev_op(c, "scalar",
                          lambda sc, ob=ob: sc.copy(OSC[0:64, :], ob.t[0:64, 0:512]),
                          waits=wrr, banks=(ob,))
                    wos_ = (c.ac.at(),)
                    dsto = o_l[_ts(hh, 64), hp, :]
                    ev_op(c, "vector",
                          lambda v, dsto=dsto, br=br: v.tensor_mul(
                              dsto, OSC[0:64, :], br.t[0:64, :]),
                          waits=wos_ + c.ol_guard, banks=(br,))
                    cur_mark[cb] = (c.pe.at(),) + (c.dv.at(),)
            wo_w = (c.dv.at(),)
            if w == 0:
                _tap(c, "ol0", o_l[:, :, :], wo_w)
            if w == 1:
                _tap(c, "ol1", o_l[:, :, :], wo_w)
            for dp in range(4):
                buf = 0
                wld = pool_dma(c, wos[:, buf, :, :],
                               c.d["wol"][l, :, dp * 256:(dp + 1) * 256]
                               .rearrange("(c p) n -> p c n", p=128),
                               wait=c.wos_mark[buf], sem=dsem(c, "wos"))
                for dd in range(2):
                    dout = 2 * dp + dd
                    bb = bank(c)
                    parts = []
                    for ic in range(4):
                        parts.append((wos[:, buf, ic, dd * 128:(dd + 1) * 128],
                                      o_l[:, ic, :], wld + wo_w))
                    mm_fill(c, bb, parts)
                    c.wos_mark[buf] = (c.pe.at(),)
                    ev_op(c, "vector",
                          lambda v, bb=bb, dout=dout, w=w: v.tensor_add(
                              YR[:, dout, 1 + w * 512:1 + (w + 1) * 512],
                              YR[:, dout, 1 + w * 512:1 + (w + 1) * 512],
                              bb.t[:]),
                          banks=(bb,))
            c.ol_guard = (c.pe.at(),)
        fence(c)
    if not SKIP_INNER_FFN:
        _inner_ffn(c, s, apply_idx, l, YR)


def _inner_ffn(c, s, apply_idx, l, YR):
    nc = c.nc
    R = c.R
    tag = f"f{s}_{apply_idx}_{l}"
    with ExitStack() as fs:
        e = fs.enter_context
        c.GLA = e(nc.sbuf_tensor(f"GLA_{tag}", [128, DCH, 128], BF16))
        c.GLB = e(nc.sbuf_tensor(f"GLB_{tag}", [128, DCH, 128], BF16))
        HN = e(nc.sbuf_tensor(f"HN_{tag}", [128, DCH, 2, 512], BF16))
        U = e(nc.sbuf_tensor(f"U_{tag}", [128, 11, 2, 512], BF16))
        GG = e(nc.sbuf_tensor(f"GG_{tag}", [128, 2, 512], F32))
        w1s = e(nc.sbuf_tensor(f"w1s_{tag}", [128, 2, 2, DCH, 128], BF16))
        w2s = e(nc.sbuf_tensor(f"w2s_{tag}", [128, 2, 11, 128], BF16))
        SVP = e(nc.sbuf_tensor(f"SVP_{tag}", [128, DCH, 2], F32))
        _build_gl(c, l)

        def srcf(d, j):
            off = 1 if d < 4 else 0
            return YR[:, d, off + j * 512: off + j * 512 + 512]

        BC = S // 2
        wpre = ev_op(c, "gpsimd",
                     lambda g: g.tensor_copy(SVP[:, :, 0:1], YR[:, :, BC:BC + 1]),
                     waits=(c.dv.at(),))
        w1_mark = [(c.pe.at(),), (c.pe.at(),)]
        w2_mark = [(c.pe.at(),), (c.pe.at(),)]
        c.gg_guard = [(), ()]
        c.u_guard = ()
        hn_guard = ()
        for half in range(2):
            wy = (c.dv.at(),)
            if half == 1:
                ev_op(c, "gpsimd",
                      lambda g: g.tensor_copy(SVP[:, :, 1:2], YR[:, :, BC:BC + 1]),
                      waits=wy)
                ev_op(c, "gpsimd",
                      lambda g: g.tensor_copy(YR[:, :, BC:BC + 1], SVP[:, :, 0:1]),
                      hard=True)
                wy = wy + (c.po.at(),)
            c.tap_rz = True
            _ln(c, get_src=srcf,
                get_dst=lambda d, j: HN[:, d, j % 2, :],
                src_wait=wy + hn_guard,
                glA_fn=lambda d: c.GLA[:, d, :],
                glB_fn=lambda d: c.GLB[:, d, :],
                blocks=(2 * half, 2 * half + 2),
                mm_wait=c.w_gl)
            whn = (c.dv.at(),)
            if half == 0:
                _tap(c, "hn", HN[:, :, :, :], whn)
            else:
                _tap(c, f"hn2_{tag}", HN[:, :, :, :], whn)
                ev_op(c, "gpsimd",
                      lambda g: g.tensor_copy(YR[:, :, BC:BC + 1], SVP[:, :, 1:2]),
                      waits=whn + (c.ac.at(),))
            for grp in range(2):
                for i in range(11):
                    hid = grp * 11 + i
                    buf = i % 2
                    wld = pool_dma(
                        c, w1s[:, buf, 0, :, :],
                        c.d["w1"][l, :, _ts(hid)].rearrange("(cc p) n -> p cc n", p=128),
                        wait=w1_mark[buf], sem=dsem(c, f"w1s{buf}"))
                    wld2 = pool_dma(
                        c, w1s[:, buf, 1, :, :],
                        c.d["w1"][l, :, _ts(NHC + hid)].rearrange("(cc p) n -> p cc n", p=128),
                        sem=dsem(c, f"w1s{buf}"))
                    bks = {}
                    for ag in range(2):
                        for ttl in range(2):
                            tt = 2 * half + ttl
                            bb = bank(c)
                            parts = []
                            for d in range(DCH):
                                parts.append((w1s[:, buf, ag, d, :],
                                              HN[:, d, ttl, :],
                                              (wld + wld2 + whn) if d == 0 else ()))
                            mm_fill(c, bb, parts)
                            bks[(ag, ttl)] = bb
                    w1_mark[buf] = (c.pe.at(),)
                    for ttl in range(2):
                        ba, bg = bks[(0, ttl)], bks[(1, ttl)]
                        gg = GG[:, ttl, :]
                        ev_op(c, "scalar",
                              lambda sc, bg=bg, gg=gg, hid=hid: sc.activation(
                                  gg, bg.t[:], AF.Gelu,
                                  bias=c.B1C[:, l, NHC + hid:NHC + hid + 1]),
                              waits=c.w_b1c + c.gg_guard[ttl], banks=(bg,))
                        wgg = (c.ac.at(),)
                        ev_op(c, "vector",
                              lambda v, ba=ba, gg=gg, hid=hid, i=i, ttl=ttl:
                              v.scalar_tensor_tensor(
                                  U[:, i, ttl, :], ba.t[:],
                                  c.B1C[:, l, hid:hid + 1], gg,
                                  op0=ALU.add, op1=ALU.mult),
                              waits=wgg + c.w_b1c + c.u_guard, banks=(ba,))
                        c.gg_guard[ttl] = (c.dv.at(),)
                wu = (c.dv.at(),)
                for dc in range(DCH):
                    buf = dc % 2
                    wld = pool_dma(
                        c, w2s[:, buf, :, :],
                        c.d["w2"][l, grp * 1408:(grp + 1) * 1408, _ts(dc)]
                        .rearrange("(hh p) n -> p hh n", p=128),
                        wait=w2_mark[buf], sem=dsem(c, f"w2s{buf}"))
                    for ttl in range(2):
                        tt = 2 * half + ttl
                        bb = bank(c)
                        parts = []
                        for i in range(11):
                            parts.append((w2s[:, buf, i, :], U[:, i, ttl, :],
                                          (wld + wu) if i == 0 else ()))
                        mm_fill(c, bb, parts)
                        w2_mark[buf] = (c.pe.at(),)
                        wbnd = wpre if half == 0 else ()
                        if grp == 0:
                            ev_op(c, "vector",
                                  lambda v, bb=bb, dc=dc, tt=tt: v.tensor_add(
                                      YR[:, dc, 1 + tt * 512:1 + (tt + 1) * 512],
                                      YR[:, dc, 1 + tt * 512:1 + (tt + 1) * 512],
                                      bb.t[:]),
                                  waits=wbnd, banks=(bb,))
                        else:
                            ev_op(c, "vector",
                                  lambda v, bb=bb, dc=dc, tt=tt: v.scalar_tensor_tensor(
                                      YR[:, dc, 1 + tt * 512:1 + (tt + 1) * 512],
                                      bb.t[:],
                                      c.SM[:, l * 24 + 16 + dc:l * 24 + 16 + dc + 1],
                                      YR[:, dc, 1 + tt * 512:1 + (tt + 1) * 512],
                                      op0=ALU.add, op1=ALU.add),
                                  waits=c.w_sm + wbnd, banks=(bb,))
                if half == 0 and grp == 0:
                    _tap(c, "u", U[:, :, :, :], (c.dv.at(),))
                c.u_guard = (c.pe.at(),)
            hn_guard = (c.pe.at(),)
        fence(c)


def _global_attn(c, s):
    nc = c.nc
    R = c.R
    tag = f"g{s}"
    with ExitStack() as ph:
        e = ph.enter_context
        kfm = e(nc.sbuf_tensor(f"kfm_{tag}", [128, DCH, S], BF16))
        vsg = e(nc.sbuf_tensor(f"vsg_{tag}", [128, 16, HG, 65], BF16))
        qfm = e(nc.sbuf_tensor(f"qfm_{tag}", [128, DCH, 512], BF16))
        ybf = e(nc.sbuf_tensor(f"ybf_{tag}", [128, DCH, 512], BF16))
        kcur = e(nc.sbuf_tensor(f"kcur_{tag}", [128, 2, S], BF16))
        qcur = e(nc.sbuf_tensor(f"qcur_{tag}", [128, 2, 512], BF16))
        og = e(nc.sbuf_tensor(f"og_{tag}", [128, DCH, 512], BF16))
        wkq = e(nc.sbuf_tensor(f"wkq_{tag}", [128, 2, DCH, 128], BF16))
        wvs = e(nc.sbuf_tensor(f"wvs_{tag}", [128, 2, 1024], BF16))
        wos = e(nc.sbuf_tensor(f"wos_{tag}", [128, 2, DCH, 128], BF16))
        OSC = e(nc.sbuf_tensor(f"gOSC_{tag}", [128, 512], BF16))
        c.LNT = e(nc.sbuf_tensor(f"gLNT_{tag}", [128, 512], F32))
        c.BSCR = e(nc.sbuf_tensor(f"gBSCR_{tag}", [128, 2048], BF16))
        c.stat_tag = f"g{tag}"
        c.PT = e(nc.sbuf_tensor(f"gPT_{tag}", [128, 4, 512], BF16))
        c.pt_guard = [(), (), (), ()]
        c.bscr_guard = ()
        c.pt_rr = 0
        MX = c.MXS

        for fn in (lambda g: g.memset(kcur[:], 0.0),
                   lambda g: g.memset(kcur[64:65, :, :], 1.0),
                   lambda g: g.memset(qcur[:], 0.0),
                   lambda g: g.memset(vsg[:], 0.0),
                   lambda g: g.memset(vsg[:, :, :, 64:65], 1.0)):
            wcur = ev_op(c, "gpsimd", fn, waits=c.w_init, hard=True)
        _tap_async(c, f"avA_{tag}", vsg[:, 15, :, :], wcur)
        def canary(lbl):
            _tap(c, f"cn_{lbl}_{tag}", vsg[:, 15, 12:16, 63:65], (c.po.at(), c.dv.at(), c.pe.at()))
        canary("T0")
        c.vslot_mark = {}
        c.vsg_w = {}
        wkq_mark = [(c.pe.at(),), (c.pe.at(),)]
        wvs_mark = [(c.pe.at(),), (c.pe.at(),)]
        wos_mark = [(c.pe.at(),), (c.pe.at(),)]
        kfm_mark = ()
        qfm_mark = ()
        ybf_mark = ()
        ogm = ()
        cur_mark = [(), ()]

        for qb in range(NTB):
            tb = qb
            _ln(c, get_src=lambda d, j: c.XR[:, d, _ts(j, 512)],
                get_dst=lambda d, j: ybf[:, d, :],
                src_wait=c.w_x + ybf_mark, blocks=(tb, tb + 1))
            wyb = (c.dv.at(),)
            canary(f"ln{qb}")
            for wname, dstbuf, dmark, scl in (("wk", kfm, kfm_mark, None),
                                              ("wq", qfm, qfm_mark, 0.125)):
                for cc in range(DCH):
                    buf = cc % 2
                    wld = pool_dma(
                        c, wkq[:, buf, :, :],
                        c.d[wname][:, _ts(cc)].rearrange("(dd p) n -> p dd n", p=128),
                        wait=wkq_mark[buf], sem=dsem(c, f"gkq{buf}"))
                    bb = bank(c)
                    parts = []
                    for d in range(DCH):
                        parts.append((wkq[:, buf, d, :], ybf[:, d, :],
                                      (wld + wyb) if d == 0 else ()))
                    mm_fill(c, bb, parts)
                    wkq_mark[buf] = (c.pe.at(),)
                    if scl is None:
                        ev_op(c, "vector",
                              lambda v, bb=bb, cc=cc, tb=tb, dstbuf=dstbuf:
                                  v.tensor_copy(dstbuf[:, cc, _ts(tb, 512)], bb.t[:]),
                              banks=(bb,))
                    else:
                        ev_op(c, "vector",
                              lambda v, bb=bb, cc=cc, dstbuf=dstbuf:
                                  v.tensor_scalar_mul(dstbuf[:, cc, :], bb.t[:],
                                                      0.125),
                              banks=(bb,))
            wkq_done = (c.dv.at(),)
            canary(f"kq{qb}")
            if qb == 3:
                _tap(c, f"gqf3_{tag}", qfm[:, :, :], wkq_done)
                _tap_async(c, f"agqf3_{tag}", qfm[:, :, :], wkq_done)
                _tap_async(c, f"agkf3_{tag}", kfm[:, :, :], wkq_done)
            for pr4 in range(2):
                tts = (2 * pr4, 2 * pr4 + 1)
                pbv = {}
                w0v = {}
                for tt4 in tts:
                    pbv[tt4] = bank(c, 2)
                    for half in range(2):
                        w0v[(tt4, half)] = pbv[tt4][half].begin_fill()
                for d in range(DCH):
                    buf = d % 2
                    wld = pool_dma(c, wvs[:, buf, :], c.d["wv"][_ts(d), :],
                                   wait=wvs_mark[buf], sem=dsem(c, f"gwv{buf}"))
                    for tt4 in tts:
                        for half in range(2):
                            bk_ = pbv[tt4][half]
                            last = (d == DCH - 1)
                            rlast = (tt4 == tts[1] and half == 1)
                            R.op("tensor",
                                 lambda t, o=bk_.t, tt4=tt4, half=half, d=d, buf=buf:
                                     t.matmul(o, ybf[:, d, _ts(tt4)],
                                              wvs[:, buf, _ts(half, 512)],
                                              start=(d == 0), stop=(d == DCH - 1)),
                                 wait=wld + wyb + (w0v[(tt4, half)] if d == 0 else ()),
                                 inc=(c.pe.bump(),) if (last or rlast) else ())
                            if last:
                                bk_.ready = (c.pe.at(),)
                    wvs_mark[buf] = (c.pe.at(),)
                for tt4 in tts:
                    slot = tb * 4 + tt4
                    for half in range(2):
                        bk_ = pbv[tt4][half]
                        ev_op(c, "vector",
                              lambda v, bk_=bk_, slot=slot, half=half: v.tensor_copy(
                                  vsg[:, slot, half * 8:(half + 1) * 8, 0:64],
                                  bk_.t[:].rearrange("p (h n) -> p h n", h=8)),
                              waits=tuple(wcur) + c.vslot_mark.get((slot, half), ()),
                              banks=(bk_,))
                        c.vsg_w[(slot, half)] = (c.dv.at(),)
            ybf_mark = (c.pe.at(),)
            wv_done = (c.dv.at(),)
            canary(f"v{qb}")
            if qb == 3:
                _tap_async(c, f"avB_{tag}", vsg[:, 15, :, :], wv_done)
            nkt = 4 * qb + 4
            for h in range(HG):
                cb = h % 2
                wc1 = ev_op(c, "gpsimd",
                            lambda g, h=h, cb=cb: g.tensor_copy(
                                qcur[0:64, cb, :], qfm[_ts(h % 2, 64), h // 2, :]),
                            waits=wkq_done + cur_mark[cb])
                ev_op(c, "gpsimd",
                      lambda g, cb=cb: g.memset(qcur[64:65, cb, :], 0.0))
                wcp = ev_op(c, "gpsimd",
                            lambda g, h=h, qb=qb, cb=cb: g.tensor_copy(
                                kcur[0:64, cb, 0:(qb + 1) * 512],
                                kfm[_ts(h % 2, 64), h // 2, 0:(qb + 1) * 512]),
                            waits=wkq_done + cur_mark[cb])
                nst = qb + 1
                for qc in range(4):
                    kspan = qb * 512 + qc * 128 + 128
                    sb_ = []
                    for si in range(nst):
                        k0 = si * 512
                        ln_ = min(512, kspan - k0)
                        bb = bank(c)
                        mm_fill(c, bb, [(bb.t[:, 0:ln_], qcur[:, cb, _ts(qc)],
                                         kcur[:, cb, k0:k0 + ln_], tuple(wcp))])
                        sb_.append((bb, ln_))
                    bcz, lcz = sb_[-1]
                    ev_op(c, "vector",
                          lambda v, bcz=bcz, lcz=lcz: v.tensor_add(
                              bcz.t[:, lcz - 128:lcz], bcz.t[:, lcz - 128:lcz],
                              c.MUP[:]),
                          waits=c.w_init + tuple(bcz.ready))
                    bcz.ready = bcz.ready + (c.dv.at(),)
                    for i, (bb, ln_) in enumerate(sb_):
                        ev_op(c, "vector",
                              lambda v, bb=bb, ln_=ln_, i=i, qc=qc: v.reduce_max(
                                  MX[:, qc * nst + i:qc * nst + i + 1],
                                  bb.t[:, 0:ln_], axis=AX.X),
                              banks=(bb,))
                if nst == 1:
                    src16 = MX[:, 0:4]
                else:
                    mxv = MX[:, 0:4 * nst].rearrange("p (q s) -> p q s", s=nst)
                    ev_op(c, "vector",
                          lambda v, mxv=mxv: v.tensor_max(
                              MX[:, 16:20], mxv[:, :, 0], mxv[:, :, 1]),
                          hard=True)
                    for i in range(2, nst):
                        ev_op(c, "vector",
                              lambda v, mxv=mxv, i=i: v.tensor_max(
                                  MX[:, 16:20], MX[:, 16:20], mxv[:, :, i]),
                              hard=True)
                    src16 = MX[:, 16:20]
                ev_op(c, "vector",
                      lambda v, src16=src16: v.tensor_scalar_mul(
                          c.NMB[:, 0:4], src16, -1.0),
                      hard=True)
                wnm = (c.dv.at(),)
                tbm = bank(c)
                w0m = tbm.begin_fill()
                for qc in range(4):
                    R.op("tensor",
                         lambda t, tbm=tbm, qc=qc: t.matmul(
                             tbm.t[0:1, _ts(qc)], c.NMB[:, qc:qc + 1],
                             c.ID[:], start=True, stop=True),
                         wait=wnm + (w0m if qc == 0 else ()) + c.w_init,
                         inc=(c.pe.bump(),) if qc == 3 else ())
                tbm.ready = (c.pe.at(),)
                ev_op(c, "vector",
                      lambda v, tbm=tbm, cb=cb: v.tensor_copy(
                          qcur[64:65, cb, 0:512], tbm.t[0:1, 0:512]),
                      banks=(tbm,))
                wmax = (c.dv.at(),)
                if qb == 3 and h in (12, 14):
                    _tap_async(c, f"aqc_{h}_{tag}", qcur[:, cb, :], wmax)
                    _tap_async(c, f"amx_{h}_{tag}", c.MXS[:, :], wmax)
                if qb == 1 and h == 0:
                    _tap(c, "gqc1", qcur[:, cb, :], wmax)
                    _tap(c, "gkc1", kcur[:, cb, :], wmax)
                if qb == 3 and h == 12:
                    _tap(c, f"gqc3_{tag}", qcur[:, cb, :], wmax)
                ob = bank(c)
                first = True
                for t_ in range(nkt):
                    qlo = max(0, t_ * 128 - qb * 512)
                    qhi = 512
                    bst = bank(c)
                    mm_fill(c, bst, [(bst.t[:, 0:qhi - qlo], kcur[:, cb, _ts(t_)],
                                      qcur[:, cb, qlo:qhi], wmax)])
                    if t_ >= 4 * qb:
                        ev_op(c, "vector",
                              lambda v, bst=bst: v.tensor_add(
                                  bst.t[:, 0:128], bst.t[:, 0:128], c.MLO[:]),
                              waits=c.w_init + tuple(bst.ready))
                        bst.ready = bst.ready + (c.dv.at(),)
                    ptslot = c.pt_rr % 4
                    c.pt_rr += 1
                    pts = c.PT[:, ptslot, 0:qhi - qlo]
                    wex = ev_op(c, "scalar",
                                lambda sc, pts=pts, bst=bst, n=qhi - qlo:
                                    sc.activation(pts, bst.t[:, 0:n], AF.Exp),
                                waits=c.pt_guard[ptslot], banks=(bst,))
                    w0 = ob.begin_fill() if first else ()
                    R.op("tensor",
                         lambda t, ob=ob, qlo=qlo, pts=pts, slot=t_, st=first, h=h:
                             t.matmul(ob.t[0:65, qlo:512], vsg[:, slot, h, :],
                                      pts, start=st, stop=True),
                         wait=wex + tuple(c.vsg_w.get((t_, h // 8), ())) + tuple(w0),
                         inc=(c.pe.bump(),))
                    ob.ready = (c.pe.at(),)
                    c.pt_guard[ptslot] = (c.pe.at(),)
                    c.vslot_mark[(t_, h // 8)] = (c.pe.at(),)
                    first = False
                if qb == 1 and h == 0:
                    _tap_psum(c, "gob1", ob)
                if qb == 3 and h == 12:
                    _tap_psum(c, f"gob3_{tag}", ob)
                if qb == 3 and h in (12, 14) and f"aob_{h}_{tag}" in DEBUG_TAPS:
                    stg_ = c.BSCR[:, (h - 12) // 2 * 512:(h - 12) // 2 * 512 + 512]
                    wst_ = ev_op(c, "vector",
                                 lambda v, stg_=stg_, ob=ob: v.tensor_copy(stg_, ob.t[:]),
                                 waits=tuple(ob.ready))
                    _tap_async(c, f"aob_{h}_{tag}", stg_, wst_)
                rr_row = mrow(c, "rr")
                ev_op(c, "vector",
                      lambda v, ob=ob: v.reciprocal(rr_row, ob.t[64:65, 0:512]),
                      waits=tuple(ob.ready))
                ev_op(c, "vector",
                      lambda v: v.tensor_copy(c.RZ[0:1, 0, :], rr_row),
                      waits=c.rz_guard, hard=True)
                ev_op(c, "vector",
                      lambda v: v.tensor_sub(mrow(c, "sc2"), rr_row, c.RZ[0:1, 0, :]),
                      hard=True)
                ev_op(c, "vector",
                      lambda v: v.tensor_copy(c.RZ[32:33, 0, :], mrow(c, "sc2")),
                      hard=True)
                wrr = (c.dv.at(),)
                br = bank(c)
                mm_fill(c, br, [(br.t[0:64, :], c.RE64[:], c.RZ[:, 0, :],
                                 tuple(wrr) + c.w_init)])
                c.rz_guard = (c.pe.at(),)
                ev_op(c, "scalar",
                      lambda sc, ob=ob: sc.copy(OSC[0:64, :], ob.t[0:64, 0:512]),
                      waits=wrr, banks=(ob,))
                wos_ = (c.ac.at(),)
                dsto = og[_ts(h % 2, 64), h // 2, :]
                ev_op(c, "vector",
                      lambda v, dsto=dsto, br=br: v.tensor_mul(
                          dsto, OSC[0:64, :], br.t[0:64, :]),
                      waits=wos_ + ogm, banks=(br,))
                cur_mark[cb] = (c.pe.at(), c.dv.at())
                if h % 4 == 3:
                    canary(f"h{qb}_{h}")
            if qb == 3:
                _tap(c, f"gog3_{tag}", og[:, :, :], (c.dv.at(),))
                _tap_async(c, f"agog3_{tag}", og[:, :, :], (c.dv.at(),))
            wog_w = (c.dv.at(),)
            for dc in range(DCH):
                buf = dc % 2
                wld = pool_dma(c, wos[:, buf, :, :],
                               c.d["wog"][:, _ts(dc)].rearrange("(ic p) n -> p ic n", p=128),
                               wait=wos_mark[buf], sem=dsem(c, f"gwo{buf}"))
                bb = bank(c)
                parts = []
                for ic in range(DCH):
                    parts.append((wos[:, buf, ic, :], og[:, ic, :],
                                  (wld + wog_w) if ic == 0 else ()))
                mm_fill(c, bb, parts)
                wos_mark[buf] = (c.pe.at(),)
                ev_op(c, "vector",
                      lambda v, bb=bb, dc=dc, qb=qb: v.tensor_add(
                          c.XR[:, dc, _ts(qb, 512)],
                          c.XR[:, dc, _ts(qb, 512)], bb.t[:]),
                      waits=c.w_x, banks=(bb,))
            ogm = (c.pe.at(),)
            canary(f"wo{qb}")
        _tap_async(c, f"avD_{tag}", vsg[:, 15, :, :], (c.dv.at(),))
        fence(c)


_CACHE = {}


def _get_nc(stages):
    key = (tuple(stages), INNER_LAYERS, SKIP_INNER_FFN, SKIP_INNER_ATTN)
    if key not in _CACHE:
        _CACHE[key] = build_nc(stages)
    return _CACHE[key]


def launch(inputs, stages=ALL_STAGES):
    g, xs = host_prep(inputs)
    nc = _get_nc(stages)
    in_maps = []
    for cid in range(NCORE):
        m = dict(g)
        m["xT"] = xs[cid]
        in_maps.append(m)
    res = run_bass_kernel_spmd(nc, in_maps, core_ids=list(range(NCORE)))
    outs = []
    for cid in range(NCORE):
        o = np.asarray(res.results[cid]["outT"], np.float32)
        outs.append(o.reshape(D, NSEQ, S).transpose(1, 2, 0))
    if DEBUG_TAPS:
        global LAST_TAPS
        LAST_TAPS = res.results
    return np.concatenate(outs, axis=0)


def kernel(**inputs):
    return launch(inputs, ALL_STAGES)
